# version 11
# baseline (speedup 1.0000x reference)
import numpy as np
import ml_dtypes
import concourse.bass as bass
import concourse.mybir as mybir
from contextlib import ExitStack

F32 = mybir.dt.float32
BF16 = mybir.dt.bfloat16
I32 = mybir.dt.int32
AF = mybir.ActivationFunctionType
ALU = mybir.AluOpType
bfdt = ml_dtypes.bfloat16

T, TOWN, D, KVR, QR, NH, HD, RD, HID = 4096, 2048, 1024, 256, 512, 16, 64, 32, 3072
NTG = T // 512
SCALE = float(1.0 / np.sqrt(96.0))
EXP_BIAS = -8.0
FFS = float(1.0 / 127.0)
CHUNKS = {0: [(0, 4), (512, 8), (3072, 28), (3584, 32)],
          1: [(1024, 12), (1536, 16), (2048, 20), (2560, 24)]}
ENGINES = ("pe", "act", "dve", "pl", "sp")


class Step:
    __slots__ = ("eng", "emit", "deps", "idx", "seq", "inc_amt", "needed", "tag",
                 "gen", "self_wait")

    def __init__(self, eng, emit, deps, idx, tag="", gen=0):
        self.eng, self.emit, self.deps, self.idx, self.tag = eng, emit, deps, idx, tag
        self.seq = None
        self.inc_amt = 0
        self.needed = False
        self.gen = gen
        self.self_wait = None


class Reg:
    __slots__ = ("w", "readers", "name")

    def __init__(self, name=""):
        self.w, self.readers, self.name = None, [], name


class Sched:
    def __init__(self, nc, es):
        self.nc = nc
        self.steps = []
        self.gen = 0
        self.sems = {}
        self.counts = {}
        for e in ENGINES:
            self.sems[e] = es.enter_context(nc.semaphore(f"sem_{e}"))
            self.sems[e + "_dma"] = es.enter_context(nc.semaphore(f"sem_{e}dma"))
            self.counts[e] = 0
            self.counts[e + "_dma"] = 0

    def step(self, eng, emit, reads=(), writes=(), deps=(), dma=False, tag=""):
        alldeps = list(deps)
        for r in reads:
            if r.w is not None:
                alldeps.append(r.w)
        for wreg in writes:
            if wreg.w is not None:
                alldeps.append(wreg.w)
            alldeps.extend(wreg.readers)
        key = eng + ("_dma" if dma else "")
        pruned, seen = [], set()
        for d in alldeps:
            if d is None or id(d) in seen:
                continue
            seen.add(id(d))
            if d.gen < self.gen:
                continue
            if d.eng == key and key in ("pe", "sp_dma", "pl_dma", "act_dma"):
                continue
            pruned.append(d)
        s = Step(key, emit, pruned, len(self.steps), tag, gen=self.gen)
        self.steps.append(s)
        for d in pruned:
            d.needed = True
        for r in reads:
            r.readers.append(s)
        for wreg in writes:
            wreg.w = s
            wreg.readers = []
        return s

    def group_end(self, reg):
        reg.w = self.steps[-1]
        reg.readers = []
        return reg.w

    def snapshot(self):
        return dict(self.counts)

    def set_counts(self, c):
        self.counts = dict(c)

    def flush(self, in_block=True):
        nc = self.nc
        per_eng = {e: [] for e in ENGINES}
        for s in self.steps:
            e = s.eng[:-4] if s.eng.endswith("_dma") else s.eng
            per_eng[e].append(s)
        for e in ENGINES:
            run = []

            def close_run(run):
                if run:
                    for s in run:
                        self.counts[s.eng] += 16
                    for s in run:
                        s.seq = self.counts[run[0].eng]
            nd = 0
            for s in per_eng[e]:
                if s.eng.endswith("_dma"):
                    if s.deps:
                        close_run(run)
                        s.self_wait = self.counts[s.eng]
                        run = [s]
                    else:
                        run.append(s)
                else:
                    close_run(run)
                    run = []
            close_run(run)
        for s in self.steps:
            if not s.eng.endswith("_dma") and s.needed:
                s.inc_amt = 1
                self.counts[s.eng] += 1
                s.seq = self.counts[s.eng]
        for s in self.steps:
            if s.eng.endswith("_dma"):
                s.inc_amt = 16
        sched = self

        def make(engname):
            def body(eng):
                cur = {}
                for s in per_eng[engname]:
                    isdma = s.eng.endswith("_dma")
                    if s.self_wait:
                        eng.wait_ge(sched.sems[s.eng], s.self_wait)
                    for d in s.deps:
                        assert d.seq is not None, f"dep of {s.tag} on {d.tag} lacks seq"
                        if not isdma and cur.get(d.eng, 0) >= d.seq:
                            continue
                        cur[d.eng] = max(cur.get(d.eng, 0), d.seq)
                        eng.wait_ge(sched.sems[d.eng], d.seq)
                    inst = s.emit(eng)
                    if s.inc_amt:
                        inst.then_inc(sched.sems[s.eng], s.inc_amt)
                dkey = engname + "_dma"
                if any(s.eng == dkey for s in per_eng[engname]):
                    eng.wait_ge(sched.sems[dkey], sched.counts[dkey])
            return body

        with nc.Block() as blk:
            for engname, attr in (("sp", "sync"), ("pl", "gpsimd"), ("pe", "tensor"),
                                  ("act", "scalar"), ("dve", "vector")):
                if per_eng[engname]:
                    getattr(blk, attr)(make(engname))
        self.steps = []
        self.gen += 1


def MM(out, lhsT, rhs, start=True, stop=True, tp=None):
    if tp is None:
        return lambda e: e.matmul(out, lhsT, rhs, start=start, stop=stop)
    return lambda e: e.matmul(out, lhsT, rhs, start=start, stop=stop, tile_position=tp)


def build_nc(taps=(), stop_after=99, ntg=NTG, parts=('ln','kv','qc','kr','rope')):
    taps = set(taps)
    nc = bass.Bass("TRN2", target_bir_lowering=False)

    xT = nc.dram_tensor("xT", [D, T], F32, kind="ExternalInput").ap()
    xTo = nc.dram_tensor("xTo", [D, TOWN], F32, kind="ExternalInput").ap()
    w_kv = nc.dram_tensor("w_kv", [D, KVR], F32, kind="ExternalInput").ap()
    w_q = nc.dram_tensor("w_q", [D, QR], F32, kind="ExternalInput").ap()
    w_kdec = nc.dram_tensor("w_kdec", [KVR, D], F32, kind="ExternalInput").ap()
    w_vdec = nc.dram_tensor("w_vdec", [KVR, D], F32, kind="ExternalInput").ap()
    w_qdec = nc.dram_tensor("w_qdec", [QR, D], F32, kind="ExternalInput").ap()
    w_krope = nc.dram_tensor("w_krope", [D, QR], F32, kind="ExternalInput").ap()
    w_qrope = nc.dram_tensor("w_qrope", [QR, QR], F32, kind="ExternalInput").ap()
    w_o = nc.dram_tensor("w_o", [D, D], F32, kind="ExternalInput").ap()
    g_kv = nc.dram_tensor("g_kv", [KVR], F32, kind="ExternalInput").ap()
    wgT = nc.dram_tensor("wgT", [D, HID], I32, kind="ExternalInput").ap()
    wuT = nc.dram_tensor("wuT", [D, HID], I32, kind="ExternalInput").ap()
    wdT = nc.dram_tensor("wdT", [HID, D], I32, kind="ExternalInput").ap()
    cosk = nc.dram_tensor("cosk", [128, T], BF16, kind="ExternalInput").ap()
    sink = nc.dram_tensor("sink", [128, T], BF16, kind="ExternalInput").ap()
    cosqo = nc.dram_tensor("cosqo", [128, TOWN], BF16, kind="ExternalInput").ap()
    sinqo = nc.dram_tensor("sinqo", [128, TOWN], BF16, kind="ExternalInput").ap()
    maskc = nc.dram_tensor("maskc", [128, 4 * 512], BF16, kind="ExternalInput").ap()
    r4t = nc.dram_tensor("r4t", [128, 128], BF16, kind="ExternalInput").ap()
    psel = nc.dram_tensor("psel", [128, 3 * 32], BF16, kind="ExternalInput").ap()
    outT = nc.dram_tensor("outT", [D, TOWN], F32, kind="ExternalOutput").ap()

    tap_t = {}

    def tap(name, shape, dtype=F32):
        if name in taps:
            tap_t[name] = nc.dram_tensor("tap_" + name, shape, dtype,
                                         kind="ExternalOutput").ap()
        return tap_t.get(name)

    es = ExitStack()
    S = es.enter_context
    sch = Sched(nc, es)

    mask_sb = S(nc.sbuf_tensor("mask_sb", [128, 4 * 512], BF16))
    r4t_sb = S(nc.sbuf_tensor("r4t_sb", [128, 128], BF16))
    psel_sb = S(nc.sbuf_tensor("psel_sb", [128, 3 * 32], BF16))
    onesv = S(nc.sbuf_tensor("onesv", [128, 1], BF16))
    ones1 = S(nc.sbuf_tensor("ones1", [1, 128], F32))
    gkv_sb = S(nc.sbuf_tensor("gkv_sb", [128, 2], F32))
    CVALS = [1e-5, 1e-6, EXP_BIAS, FFS, -0.5, -1.0]
    cst_sb = S(nc.sbuf_tensor("cst_sb", [128, len(CVALS)], F32))
    wst = {}
    r_const, r_gkv = Reg("const"), Reg("gkv")

    r_attnall = [Reg(f"at{h}") for h in range(NH)]
    r_wo = Reg("wo")

    esL = ExitStack()
    SL = esL.enter_context
    kv_sb = SL(nc.sbuf_tensor("kv_sb", [128, 2 * T], BF16))
    kr_sb = SL(nc.sbuf_tensor("kr_sb", [128, 4 * T], BF16))
    qc_dram = nc.dram_tensor("qc_scr", [4 * 128, T], BF16, kind="Internal").ap()
    r_kv, r_qc, r_kr, r_vec = Reg("kv"), Reg("qc"), Reg("kr"), Reg("vec")

    sch.step("sp", lambda e: e.dma_start(out=mask_sb[:, :], in_=maskc[:, :]),
             writes=[r_const], dma=True, tag="ld_mask")
    sch.step("sp", lambda e: e.dma_start(out=r4t_sb[:, :], in_=r4t[:, :]),
             writes=[r_const], dma=True, tag="ld_r4t")
    sch.step("sp", lambda e: e.dma_start(out=psel_sb[:, :], in_=psel[:, :]),
             writes=[r_const], dma=True, tag="ld_psel")
    sch.step("sp", lambda e: e.dma_start(
        out=bass.AP(gkv_sb, 0, [[2, 128], [1, 2]]),
        in_=bass.AP(g_kv.tensor, 0, [[1, 128], [128, 2]]),
        allow_slow_non_contiguous=True),
        writes=[r_gkv], dma=True, tag="ld_gkv")
    sch.step("pl", lambda e: e.memset(onesv[:, :], 1.0), writes=[r_const], tag="ones_v")
    sch.step("pl", lambda e: e.memset(ones1[:, :], 1.0), writes=[r_const], tag="ones_1")
    for i, v in enumerate(CVALS):
        sch.step("pl", lambda e, i=i, v=v: e.memset(cst_sb[:, i:i + 1], float(v)),
                 writes=[r_const], tag=f"cst{i}")
        nc.const_aps.register(F32, float(v), cst_sb[:, i:i + 1])

    es0 = ExitStack()
    S0 = es0.enter_context
    vec_sb = S0(nc.sbuf_tensor("vec_sb", [1, 6 * 512], F32))
    qc_sb = S0(nc.sbuf_tensor("qc_sb", [128, 4 * T], BF16))
    wkv_sb = S0(nc.sbuf_tensor("wkv_sb", [128, 8 * KVR], BF16))
    wq_sb = S0(nc.sbuf_tensor("wq_sb", [128, 8 * QR], BF16))
    wkr_sb = S0(nc.sbuf_tensor("wkr_sb", [128, 8 * QR], BF16))
    xg_sb = [S0(nc.sbuf_tensor(f"xg{i}", [128, 8 * 512], BF16)) for i in range(2)]
    xf_sb = [S0(nc.sbuf_tensor(f"xf{i}", [128, 4 * 512], F32)) for i in range(2)]
    xsq_sb = S0(nc.sbuf_tensor("xsq_sb", [128, 2 * 512], BF16))
    xc_sb = S0(nc.sbuf_tensor("xc_sb", [128, 8 * 512], BF16))
    rb_sb = [S0(nc.sbuf_tensor(f"rb{i}", [128, 512], BF16)) for i in range(2)]
    kvsq_sb = S0(nc.sbuf_tensor("kvsq_sb", [128, 2 * 512], BF16))
    ktab_sb = [S0(nc.sbuf_tensor(f"ktab{i}", [128, 1024], BF16)) for i in range(2)]
    rot_scr = S0(nc.sbuf_tensor("rot_scr", [128, 512], BF16))
    rot_scr2 = S0(nc.sbuf_tensor("rot_scr2", [128, 512], BF16))
    ps_bc = S0(nc.psum_tensor("ps_bc", [128, 1024], F32))
    ps_kv = S0(nc.psum_tensor("ps_kv", [128, 1024], F32))
    ps_mm2 = S0(nc.psum_tensor("ps_mm2", [128, 1024], F32))
    ps_misc = S0(nc.psum_tensor("ps_misc", [128, 512], F32))
    ps_rot = S0(nc.psum_tensor("ps_rot", [128, 512], F32))
    ps_ln = ps_misc
    ps_kr = ps_mm2
    ps_qc = ps_mm2
    ps_ss = ps_misc

    r_wkv, r_wq, r_wkr = Reg("wkv"), Reg("wq"), Reg("wkr")
    r_xg = [Reg("xg0"), Reg("xg1")]
    r_xf = [Reg("xf0"), Reg("xf1")]
    r_xsq = [Reg("xsq0"), Reg("xsq1")]
    r_xc = Reg("xc")
    r_xcc = [Reg(f"xcc{i}") for i in range(8)]
    r_rb = [Reg("rb0"), Reg("rb1")]
    r_kvsq = Reg("kvsq")
    r_ktab = [Reg("kt0"), Reg("kt1")]
    r_rsc, r_rsc2 = Reg("rsc"), Reg("rsc2")
    r_psln, r_psbc0, r_psbc1 = Reg("psmisc"), Reg("psbc0"), Reg("psbc1")
    r_pskv, r_psrot = Reg("pskv"), Reg("psrot")
    r_psss = r_psln
    r_mm2 = [Reg("mm2_0"), Reg("mm2_1")]
    r_pskr = r_mm2
    r_psqc = r_mm2

    def wload(eng, dst, src, cols, kchunks, reg, tag):
        wstage, r_wstage = wst["t"], wst["r"]
        per = max(1, 2048 // cols)
        for k0 in range(0, kchunks, per):
            kn = min(per, kchunks - k0)
            sch.step("sp", lambda e, k0=k0, kn=kn: e.dma_start(
                out=bass.AP(wstage, 0, [[2048, 128], [cols, kn], [1, cols]]),
                in_=bass.AP(src.tensor, k0 * 128 * cols,
                            [[cols, 128], [128 * cols, kn], [1, cols]])),
                writes=[r_wstage], dma=True, tag=f"{tag}_{k0}")
            sch.step("dve", lambda e, k0=k0, kn=kn: e.tensor_copy(
                dst[:, k0 * cols:(k0 + kn) * cols], wstage[:, 0:kn * cols]),
                reads=[r_wstage], writes=[reg], tag=f"{tag}c_{k0}")

    wst["t"] = S0(nc.sbuf_tensor("wstage0", [128, 2048], F32))
    wst["r"] = Reg("wstage0")
    wload("pl", wkv_sb, w_kv, KVR, 8, r_wkv, "ld_wkv")
    wload("pl", wq_sb, w_q, QR, 8, r_wq, "ld_wq")
    wload("pl", wkr_sb, w_krope, QR, 8, r_wkr, "ld_wkr")

    VM, VSD, VR, VMSQ, VA, VRHO = range(6)

    def V(i):
        return vec_sb[0:1, i * 512:(i + 1) * 512]

    for tg in range(ntg):
        b = tg % 2
        c0 = tg * 512
        xg, rxg = xg_sb[b], r_xg[b]
        for hh2 in range(2):
            xfb = (2 * tg + hh2) % 2
            sch.step("sp", lambda e, c0=c0, hh2=hh2, xfb=xfb: e.dma_start(
                out=bass.AP(xf_sb[xfb], 0, [[4 * 512, 128], [512, 4], [1, 512]]),
                in_=bass.AP(xT.tensor, c0 + hh2 * 4 * 128 * T,
                            [[T, 128], [128 * T, 4], [1, 512]])),
                writes=[r_xf[xfb]], dma=True, tag=f"ld_x{tg}_{hh2}")
            sch.step("act", lambda e, xg=xg, hh2=hh2, xfb=xfb: e.activation(
                xg[:, hh2 * 2048:(hh2 + 1) * 2048], xf_sb[xfb][:, :], AF.Copy),
                reads=[r_xf[xfb]], writes=[rxg], tag=f"cx{tg}_{hh2}")
        sch.step("sp", lambda e, b=b, c0=c0: e.dma_start(
            out=ktab_sb[b][:, 0:512], in_=cosk[:, c0:c0 + 512]),
            writes=[r_ktab[b]], dma=True, tag=f"ld_ck{tg}")
        sch.step("sp", lambda e, b=b, c0=c0: e.dma_start(
            out=ktab_sb[b][:, 512:1024], in_=sink[:, c0:c0 + 512]),
            writes=[r_ktab[b]], dma=True, tag=f"ld_sk{tg}")
        if 'ln' not in parts:
            continue
        for c in range(8):
            sch.step("pe", MM(ps_ln[0:1, 0:512], onesv[:, 0:1],
                              xg[:, c * 512:(c + 1) * 512], start=(c == 0), stop=(c == 7)),
                     reads=[rxg, r_const], writes=([r_psln] if c == 0 else []),
                     tag=f"mmm{tg}_{c}")
        sch.group_end(r_psln)
        sch.step("dve", lambda e: e.tensor_scalar_mul(V(VM), ps_ln[0:1, 0:512], 1.0 / D),
                 reads=[r_psln], writes=[r_vec], tag=f"m{tg}")
        sch.step("pe", MM(ps_bc[:, 0:512], ones1[0:1, :], V(VM)),
                 reads=[r_vec, r_const], writes=[r_psbc0], tag=f"mbc{tg}")
        for c in range(8):
            sq = c % 2
            sch.step("act", lambda e, xg=xg, c=c, sq=sq: e.activation(
                xsq_sb[:, sq * 512:sq * 512 + 512], xg[:, c * 512:(c + 1) * 512], AF.Square),
                reads=[rxg], writes=[r_xsq[sq]], tag=f"sq{tg}_{c}")
            sch.step("pe", MM(ps_ln[0:1, 0:512], onesv[:, 0:1],
                              xsq_sb[:, sq * 512:sq * 512 + 512],
                              start=(c == 0), stop=(c == 7)),
                     reads=[r_xsq[sq], r_const], writes=([r_psln] if c == 0 else []),
                     tag=f"mms{tg}_{c}")
        sch.group_end(r_psln)
        sch.step("dve", lambda e: e.tensor_scalar_mul(V(VSD), ps_ln[0:1, 0:512], 1.0 / D),
                 reads=[r_psln], writes=[r_vec], tag=f"v1_{tg}")
        sch.step("dve", lambda e: e.tensor_mul(V(VA), V(VM), V(VM)),
                 reads=[r_vec], writes=[r_vec], tag=f"m2_{tg}")
        sch.step("dve", lambda e: e.tensor_sub(V(VSD), V(VSD), V(VA)),
                 reads=[r_vec], writes=[r_vec], tag=f"var{tg}")
        sch.step("act", lambda e: e.activation(V(VSD), V(VSD), AF.Ln, bias=1e-5),
                 reads=[r_vec], writes=[r_vec], tag=f"sd{tg}")
        sch.step("act", lambda e: e.activation(V(VR), V(VSD), AF.Exp, scale=-0.5),
                 reads=[r_vec], writes=[r_vec], tag=f"r{tg}")
        for c in range(8):
            sch.step("dve", lambda e, xg=xg, c=c: e.tensor_sub(
                xc_sb[:, c * 512:(c + 1) * 512], xg[:, c * 512:(c + 1) * 512],
                ps_bc[:, 0:512]),
                reads=[rxg, r_psbc0], writes=[r_xcc[c]], tag=f"cent{tg}_{c}")
        for mt in range(2 if 'kv' in parts else 0):
            for k in range(8):
                sch.step("pe", MM(ps_kv[:, mt * 512:mt * 512 + 512],
                                  wkv_sb[:, k * KVR + mt * 128:k * KVR + mt * 128 + 128],
                                  xc_sb[:, k * 512:(k + 1) * 512],
                                  start=(k == 0), stop=(k == 7)),
                         reads=[r_xcc[k], r_wkv],
                         writes=([r_pskv] if (mt == 0 and k == 0) else []),
                         tag=f"kv{tg}_{mt}_{k}")
        sch.group_end(r_pskv)
        for mt in range(2 if 'kv' in parts else 0):
            sch.step("dve", lambda e, mt=mt, c0=c0: e.tensor_copy(
                kv_sb[:, mt * T + c0:mt * T + c0 + 512], ps_kv[:, mt * 512:mt * 512 + 512]),
                reads=[r_pskv], writes=[r_kv], tag=f"kve{tg}_{mt}")
            sch.step("act", lambda e, mt=mt, c0=c0: e.activation(
                kvsq_sb[:, mt * 512:mt * 512 + 512],
                kv_sb[:, mt * T + c0:mt * T + c0 + 512], AF.Square),
                reads=[r_kv], writes=[r_kvsq], tag=f"kvq{tg}_{mt}")
        sch.step("pe", MM(ps_bc[:, 512:1024], ones1[0:1, :], V(VR)),
                 reads=[r_vec, r_const], writes=[r_psbc1], tag=f"rbc{tg}")
        sch.step("dve", lambda e, b=b: e.tensor_copy(rb_sb[b][:, :], ps_bc[:, 512:1024]),
                 reads=[r_psbc1], writes=[r_rb[b]], tag=f"rbe{tg}")
        for k in range(2 if 'kv' in parts else 0):
            sch.step("pe", MM(ps_ss[0:1, 0:512], onesv[:, 0:1],
                              kvsq_sb[:, k * 512:(k + 1) * 512], start=(k == 0), stop=(k == 1)),
                     reads=[r_kvsq, r_const], writes=([r_psss] if k == 0 else []),
                     tag=f"ss{tg}_{k}")
        sch.group_end(r_psss)
        if 'kv' not in parts:
            continue
        sch.step("dve", lambda e: e.tensor_scalar_mul(V(VMSQ), ps_ss[0:1, 0:512], 1.0 / KVR),
                 reads=[r_psss], writes=[r_vec], tag=f"msq{tg}")
        sch.step("dve", lambda e: e.tensor_mul(V(VA), V(VR), V(VR)),
                 reads=[r_vec], writes=[r_vec], tag=f"r2{tg}")
        sch.step("dve", lambda e: e.tensor_mul(V(VA), V(VA), V(VMSQ)),
                 reads=[r_vec], writes=[r_vec], tag=f"r2m{tg}")
        sch.step("act", lambda e: e.activation(V(VA), V(VA), AF.Ln, bias=1e-6),
                 reads=[r_vec], writes=[r_vec], tag=f"sqr{tg}")
        sch.step("act", lambda e: e.activation(V(VRHO), V(VA), AF.Exp, scale=-0.5),
                 reads=[r_vec], writes=[r_vec], tag=f"rec{tg}")
        sch.step("dve", lambda e: e.tensor_mul(V(VRHO), V(VRHO), V(VR)),
                 reads=[r_vec], writes=[r_vec], tag=f"rho{tg}")
        for mt in range(4 if 'qc' in parts else 0):
            sl = mt % 2
            rp = r_psqc[sl]
            for k in range(8):
                sch.step("pe", MM(ps_qc[:, sl * 512:sl * 512 + 512],
                                  wq_sb[:, k * QR + mt * 128:k * QR + mt * 128 + 128],
                                  xc_sb[:, k * 512:(k + 1) * 512],
                                  start=(k == 0), stop=(k == 7)),
                         reads=[r_xcc[k], r_wq], writes=([rp] if k == 0 else []),
                         tag=f"qc{tg}_{mt}_{k}")
            sch.group_end(rp)
            sch.step("dve", lambda e, mt=mt, sl=sl, b=b, c0=c0: e.tensor_mul(
                qc_sb[:, mt * T + c0:mt * T + c0 + 512],
                ps_qc[:, sl * 512:sl * 512 + 512], rb_sb[b][:, :]),
                reads=[rp, r_rb[b]], writes=[r_qc], tag=f"qce{tg}_{mt}")
        sch.step("pe", MM(ps_bc[:, 512:1024], ones1[0:1, :], V(VRHO)),
                 reads=[r_vec, r_const], writes=[r_psbc1], tag=f"rhobc{tg}")
        for mt in range(2):
            sch.step("dve", lambda e, mt=mt, c0=c0: e.tensor_mul(
                kv_sb[:, mt * T + c0:mt * T + c0 + 512],
                kv_sb[:, mt * T + c0:mt * T + c0 + 512], ps_bc[:, 512:1024]),
                reads=[r_kv, r_psbc1], writes=[r_kv], tag=f"kvn{tg}_{mt}")
        for g in range(4 if 'kr' in parts else 0):
            sl = g % 2
            rp = r_pskr[sl]
            for k in range(8):
                sch.step("pe", MM(ps_kr[:, sl * 512:sl * 512 + 512],
                                  wkr_sb[:, k * QR + g * 128:k * QR + g * 128 + 128],
                                  xc_sb[:, k * 512:(k + 1) * 512],
                                  start=(k == 0), stop=(k == 7)),
                         reads=[r_xcc[k], r_wkr], writes=([rp] if k == 0 else []),
                         tag=f"kr{tg}_{g}_{k}")
            sch.group_end(rp)
            sch.step("dve", lambda e, g=g, sl=sl, b=b, c0=c0: e.tensor_mul(
                kr_sb[:, g * T + c0:g * T + c0 + 512],
                ps_kr[:, sl * 512:sl * 512 + 512], rb_sb[b][:, :]),
                reads=[rp, r_rb[b]], writes=[r_kr], tag=f"kre{tg}_{g}")
            if 'rope' not in parts:
                continue
            sch.step("pe", MM(ps_rot[:, 0:512], r4t_sb[:, :],
                              kr_sb[:, g * T + c0:g * T + c0 + 512]),
                     reads=[r_kr, r_const], writes=[r_psrot], tag=f"rot{tg}_{g}")
            sch.step("dve", lambda e, g=g, b=b, c0=c0: e.tensor_mul(
                rot_scr[:, :], kr_sb[:, g * T + c0:g * T + c0 + 512], ktab_sb[b][:, 0:512]),
                reads=[r_kr, r_ktab[b]], writes=[r_rsc], tag=f"t1_{tg}_{g}")
            sch.step("dve", lambda e, b=b: e.tensor_mul(
                rot_scr2[:, :], ps_rot[:, 0:512], ktab_sb[b][:, 512:1024]),
                reads=[r_psrot, r_ktab[b]], writes=[r_rsc2], tag=f"u_{tg}_{g}")
            sch.step("dve", lambda e, g=g, c0=c0: e.tensor_add(
                kr_sb[:, g * T + c0:g * T + c0 + 512], rot_scr[:, :], rot_scr2[:, :]),
                reads=[r_rsc, r_rsc2], writes=[r_kr], tag=f"kh_{tg}_{g}")
    sch.flush()

    for mt in range(4):
        sch.step("sp", lambda e, mt=mt: e.dma_start(
            out=qc_dram[mt * 128:(mt + 1) * 128, :], in_=qc_sb[:, mt * T:(mt + 1) * T]),
            reads=[r_qc], dma=True, tag=f"qcspill{mt}")
    r_qcd = Reg("qcd")
    r_qcd.w = sch.steps[-1]

    t_kvn = tap("kv_norm", [2 * 128, T], BF16)
    t_qc = tap("q_c", [4 * 128, T], BF16)
    t_khat = tap("khat", [4 * 128, T], BF16)
    for name, tt, src_sb, reg, n in (("kv_norm", t_kvn, kv_sb, r_kv, 2),
                                     ("q_c", t_qc, qc_sb, r_qc, 4),
                                     ("khat", t_khat, kr_sb, r_kr, 4)):
        if tt is not None:
            for mt in range(n):
                sch.step("sp", lambda e, mt=mt, tt=tt, src_sb=src_sb: e.dma_start(
                    out=tt[mt * 128:(mt + 1) * 128, :], in_=src_sb[:, mt * T:(mt + 1) * T]),
                    reads=[reg], dma=True, tag=f"tap_{name}{mt}")
    sch.flush()
    es0.close()
    es_mid = ExitStack()
    attn_all = es_mid.enter_context(nc.sbuf_tensor("attn_all", [128, 8 * TOWN], BF16))
    wo_sb = es_mid.enter_context(nc.sbuf_tensor("wo_sb", [128, 8 * D], BF16))
    if stop_after < 3:
        es_mid.close()
        esL.close()
        es.close()
        return nc, tap_t

    es3 = ExitStack()
    S3 = es3.enter_context
    wkd_sb = S3(nc.sbuf_tensor("wkd_sb", [128, 2 * D], BF16))
    wvd_sb = S3(nc.sbuf_tensor("wvd_sb", [128, 2 * D], BF16))
    wqd_sb = S3(nc.sbuf_tensor("wqd_sb", [128, 4 * D], BF16))
    wqr_sb = S3(nc.sbuf_tensor("wqr_sb", [128, 4 * QR], BF16))
    r_wkd, r_wvd, r_wqd, r_wqr = Reg("wkd"), Reg("wvd"), Reg("wqd"), Reg("wqr")

    es3w = ExitStack()
    wst["t"] = es3w.enter_context(nc.sbuf_tensor("wstage3", [128, 2048], F32))
    wst["r"] = Reg("wstage3")
    wload("pl", wkd_sb, w_kdec, D, 2, r_wkd, "ld_wkd")
    wload("pl", wvd_sb, w_vdec, D, 2, r_wvd, "ld_wvd")
    wload("pl", wqd_sb, w_qdec, D, 4, r_wqd, "ld_wqd")
    wload("pl", wqr_sb, w_qrope, QR, 4, r_wqr, "ld_wqr")
    wload("pl", wo_sb, w_o, D, 8, r_wo, "ld_wo")
    for w_sbuf, reg in ((wkd_sb, r_wkd), (wvd_sb, r_wvd)):
        for k in range(2):
            sch.step("dve", lambda e, w_sbuf=w_sbuf, k=k: e.tensor_scalar_mul(
                w_sbuf[:, k * D:(k + 1) * D], w_sbuf[:, k * D:(k + 1) * D],
                gkv_sb[:, k:k + 1]),
                reads=[reg, r_gkv], writes=[reg], tag=f"gfold{k}")
    for k in range(4):
        sch.step("dve", lambda e, k=k: e.tensor_scalar_mul(
            wqd_sb[:, k * D:(k + 1) * D], wqd_sb[:, k * D:(k + 1) * D], SCALE),
            reads=[r_wqd], writes=[r_wqd], tag=f"sfold{k}")
    sch.flush()
    es3w.close()
    qhat_sb = S3(nc.sbuf_tensor("qhat_sb", [128, 4 * TOWN], BF16))
    qco_sb = S3(nc.sbuf_tensor("qco_sb", [128, 4 * TOWN], BF16))
    kbuf_sb = [S3(nc.sbuf_tensor(f"kbuf{i}", [96, T], BF16)) for i in range(2)]
    qbuf_sb = [S3(nc.sbuf_tensor(f"qbuf{i}", [96, TOWN], BF16)) for i in range(2)]
    v_sb = S3(nc.sbuf_tensor("v_sb", [128, 32 * 130], BF16))
    p_sb = [S3(nc.sbuf_tensor(f"p{i}", [128, 1024], BF16)) for i in range(2)]
    nrm_sb = S3(nc.sbuf_tensor("nrm_sb", [64, 512], F32))
    aschat = S3(nc.sbuf_tensor("aschat", [64, 512], BF16))
    qtab_sb = [S3(nc.sbuf_tensor(f"qtab{i}", [128, 1024], BF16)) for i in range(2)]
    rsc3 = S3(nc.sbuf_tensor("rsc3", [128, 512], BF16))
    rsc4 = S3(nc.sbuf_tensor("rsc4", [128, 512], BF16))
    recv_sb = S3(nc.sbuf_tensor("recv_sb", [1, 512], F32))
    ps_sc = [S3(nc.psum_tensor(f"ps_sc{i}", [128, 1024], F32)) for i in range(2)]
    ps_pv = S3(nc.psum_tensor("ps_pv", [65, 1024], F32))
    ps_decT = S3(nc.psum_tensor("ps_decT", [128, 1024], F32))
    r_qhat, r_qtab = Reg("qhat"), Reg("qtab")
    r_kbuf = [Reg("kb0"), Reg("kb1")]
    r_qbuf = [Reg("qb0"), Reg("qb1")]
    r_v = Reg("v")
    r_p = [Reg("p0"), Reg("p1")]
    r_nrm, r_aschat, r_recv = Reg("nrm"), Reg("aschat"), Reg("recv")
    r_rsc3, r_rsc4 = Reg("rsc3"), Reg("rsc4")
    r_pssc = [Reg("pssc0"), Reg("pssc1")]
    r_pspv = [Reg("pspv0"), Reg("pspv1")]
    r_psd = [Reg("psd0"), Reg("psd1")]
    r_qtabs = [Reg("qt0"), Reg("qt1")]

    snap0 = sch.snapshot()
    end_counts = []
    pid = nc.partition_id()
    engs = [nc.tensor, nc.vector, nc.scalar, nc.gpsimd, nc.sync]
    idx = nc.snap(pid % 2)
    for var in nc.Switch(engines=engs, index=[idx] * 5, n=2):
        sch.set_counts(snap0)
        chunks = CHUNKS[var]
        own0 = [q0 for q0, _ in chunks]
        r_qco = Reg("qco")

        def dslot(sl):
            return ps_decT[:, sl * 512:sl * 512 + 512]
        for mt in range(4):
            for cc in range(4):
                sch.step("sp", lambda e, mt=mt, cc=cc, q0=own0[cc]: e.dma_start(
                    out=qco_sb[:, mt * TOWN + cc * 512:mt * TOWN + cc * 512 + 512],
                    in_=qc_dram[mt * 128:(mt + 1) * 128, q0:q0 + 512]),
                    reads=[r_qcd], writes=[r_qco], dma=True, tag=f"qcl{var}_{mt}_{cc}")

        for cc in range(4):
            tb = cc % 2
            q0 = own0[cc]
            sch.step("sp", lambda e, tb=tb, q0=q0, cc=cc: e.dma_start(
                out=qtab_sb[tb][:, 0:512], in_=cosqo[:, cc * 512:cc * 512 + 512]),
                writes=[r_qtabs[tb]], dma=True, tag=f"qtc{var}_{cc}")
            sch.step("sp", lambda e, tb=tb, cc=cc: e.dma_start(
                out=qtab_sb[tb][:, 512:1024], in_=sinqo[:, cc * 512:cc * 512 + 512]),
                writes=[r_qtabs[tb]], dma=True, tag=f"qts{var}_{cc}")
            for g in range(4):
                sl = g % 2
                rp = r_psd[sl]
                for k in range(4):
                    sch.step("pe", MM(dslot(sl),
                                      wqr_sb[:, k * QR + g * 128:k * QR + g * 128 + 128],
                                      qco_sb[:, k * TOWN + cc * 512:k * TOWN + cc * 512 + 512],
                                      start=(k == 0), stop=(k == 3)),
                             reads=[r_qco, r_wqr], writes=([rp] if k == 0 else []),
                             tag=f"qr{var}_{g}_{cc}")
                sch.group_end(rp)
                sch.step("dve", lambda e, g=g, cc=cc, sl=sl: e.tensor_copy(
                    qhat_sb[:, g * TOWN + cc * 512:g * TOWN + cc * 512 + 512], dslot(sl)),
                    reads=[rp], writes=[r_qhat], tag=f"qre{var}_{g}_{cc}")
                sch.step("pe", MM(dslot(sl), r4t_sb[:, :],
                                  qhat_sb[:, g * TOWN + cc * 512:g * TOWN + cc * 512 + 512]),
                         reads=[r_qhat, r_const], writes=[rp], tag=f"qrot{var}_{g}_{cc}")
                sch.step("dve", lambda e, g=g, cc=cc, tb=tb: e.tensor_mul(
                    rsc3[:, :],
                    qhat_sb[:, g * TOWN + cc * 512:g * TOWN + cc * 512 + 512],
                    qtab_sb[tb][:, 0:512]),
                    reads=[r_qhat, r_qtabs[tb]], writes=[r_rsc3], tag=f"qt1{var}_{g}_{cc}")
                sch.step("dve", lambda e, sl=sl, tb=tb: e.tensor_mul(
                    rsc4[:, :], dslot(sl), qtab_sb[tb][:, 512:1024]),
                    reads=[rp, r_qtabs[tb]], writes=[r_rsc4], tag=f"qu{var}_{g}_{cc}")
                sch.step("dve", lambda e, g=g, cc=cc: e.tensor_add(
                    qhat_sb[:, g * TOWN + cc * 512:g * TOWN + cc * 512 + 512],
                    rsc3[:, :], rsc4[:, :]),
                    reads=[r_rsc3, r_rsc4], writes=[r_qhat], tag=f"qh{var}_{g}_{cc}")

        def decode_head_gen(h, slot):
            kb_, qb_ = kbuf_sb[slot], qbuf_sb[slot]
            rkb, rqb = r_kbuf[slot], r_qbuf[slot]
            for nc8 in range(8):
                sl = nc8 % 2
                for k in range(2):
                    sch.step("pe", MM(dslot(sl)[0:64, :],
                                      wkd_sb[:, k * D + h * 64:k * D + h * 64 + 64],
                                      kv_sb[:, k * T + nc8 * 512:k * T + nc8 * 512 + 512],
                                      start=(k == 0), stop=(k == 1)),
                             reads=[r_kv, r_wkd],
                             writes=([r_psd[sl]] if k == 0 else []), tag=f"kd{var}_{h}_{nc8}")
                sch.group_end(r_psd[sl])
                sch.step("dve", lambda e, nc8=nc8, kb_=kb_, sl=sl: e.tensor_copy(
                    kb_[0:64, nc8 * 512:nc8 * 512 + 512], dslot(sl)[0:64, :]),
                    reads=[r_psd[sl]], writes=[rkb], tag=f"kde{var}_{h}_{nc8}")
                yield
            G, gsl = h // 4, h % 4
            if gsl == 2:
                sch.step("dve", lambda e, G=G: e.tensor_copy(
                    kb_[64:96, :], kr_sb[64:96, G * T:(G + 1) * T]),
                    reads=[r_kr], writes=[rkb], tag=f"kcp{var}_{h}")
                yield
            else:
                ci = {0: 0, 1: 1, 3: 2}[gsl]
                for nc8 in range(8):
                    sl = nc8 % 2
                    sch.step("pe", MM(dslot(sl)[64:96, :],
                                      psel_sb[:, ci * 32:(ci + 1) * 32],
                                      kr_sb[:, G * T + nc8 * 512:G * T + nc8 * 512 + 512],
                                      tp=(0, 64)),
                             reads=[r_kr, r_const], writes=[r_psd[sl]], tag=f"kpm{var}_{h}_{nc8}")
                    sch.step("dve", lambda e, nc8=nc8, kb_=kb_, sl=sl: e.tensor_copy(
                        kb_[64:96, nc8 * 512:nc8 * 512 + 512], dslot(sl)[64:96, :]),
                        reads=[r_psd[sl]], writes=[rkb], tag=f"kpe{var}_{h}_{nc8}")
                    yield
            for cc in range(4):
                sl = cc % 2
                for k in range(4):
                    sch.step("pe", MM(dslot(sl)[0:64, :],
                                      wqd_sb[:, k * D + h * 64:k * D + h * 64 + 64],
                                      qco_sb[:, k * TOWN + cc * 512:k * TOWN + cc * 512 + 512],
                                      start=(k == 0), stop=(k == 3)),
                             reads=[r_qco, r_wqd],
                             writes=([r_psd[sl]] if k == 0 else []), tag=f"qd{var}_{h}_{cc}")
                sch.group_end(r_psd[sl])
                sch.step("dve", lambda e, cc=cc, qb_=qb_, sl=sl: e.tensor_copy(
                    qb_[0:64, cc * 512:cc * 512 + 512], dslot(sl)[0:64, :]),
                    reads=[r_psd[sl]], writes=[rqb], tag=f"qde{var}_{h}_{cc}")
                yield
            if gsl == 2:
                sch.step("dve", lambda e, G=G, qb_=qb_: e.tensor_copy(
                    qb_[64:96, :], qhat_sb[64:96, G * TOWN:(G + 1) * TOWN]),
                    reads=[r_qhat], writes=[rqb], tag=f"qcp{var}_{h}")
                yield
            else:
                ci = {0: 0, 1: 1, 3: 2}[gsl]
                for cc in range(4):
                    sl = cc % 2
                    sch.step("pe", MM(dslot(sl)[64:96, :],
                                      psel_sb[:, ci * 32:(ci + 1) * 32],
                                      qhat_sb[:, G * TOWN + cc * 512:G * TOWN + cc * 512 + 512],
                                      tp=(0, 64)),
                             reads=[r_qhat, r_const], writes=[r_psd[sl]], tag=f"qpm{var}_{h}_{cc}")
                    sch.step("dve", lambda e, cc=cc, qb_=qb_, sl=sl: e.tensor_copy(
                        qb_[64:96, cc * 512:cc * 512 + 512], dslot(sl)[64:96, :]),
                        reads=[r_psd[sl]], writes=[rqb], tag=f"qpe{var}_{h}_{cc}")
                    yield

        def decode_v_gen(hg):
            for tt in range(32):
                sl = tt % 2
                for k in range(2):
                    sch.step("pe", MM(dslot(sl)[:, 0:128],
                                      kv_sb[:, k * T + tt * 128:k * T + tt * 128 + 128],
                                      wvd_sb[:, k * D + hg * 128:k * D + hg * 128 + 128],
                                      start=(k == 0), stop=(k == 1)),
                             reads=[r_kv, r_wvd],
                             writes=([r_psd[sl]] if k == 0 else []), tag=f"v{var}_{hg}_{tt}")
                sch.group_end(r_psd[sl])
                sch.step("dve", lambda e, tt=tt, sl=sl: e.tensor_copy(
                    bass.AP(v_sb, tt * 130, [[32 * 130, 128], [65, 2], [1, 64]]),
                    bass.AP(ps_decT, sl * 512, [[1024, 128], [64, 2], [1, 64]])),
                    reads=[r_psd[sl]], writes=[r_v], tag=f"ve{var}_{hg}_{tt}")
                if tt % 2 == 1:
                    yield
            sch.step("pl", lambda e: e.memset(
                bass.AP(v_sb, 64, [[32 * 130, 128], [130, 32], [65, 2], [1, 1]]), 1.0),
                writes=[r_v], tag=f"vones{var}_{hg}")

        def pump(g, k=1):
            if g is None:
                return None
            try:
                for _ in range(k):
                    next(g)
                return g
            except StopIteration:
                return None

        def drain(g):
            if g is not None:
                for _ in g:
                    pass

        for _ in decode_head_gen(0, 0):
            pass
        filler = None
        for h in range(NH):
            slot = h % 2
            kb_, qb_ = kbuf_sb[slot], qbuf_sb[slot]
            rkb, rqb = r_kbuf[slot], r_qbuf[slot]
            drain(filler)
            if h % 2 == 0:
                drain(decode_v_gen(h // 2))
            filler = decode_head_gen(h + 1, (h + 1) % 2) if h + 1 < NH else None
            hl = h % 2
            pending_norm = []
            for cc in range(4):
                while len(pending_norm) > 1:
                    pending_norm.pop(0)()
                q0c, nkt = chunks[cc]
                nb4 = (nkt + 1) // 2
                pends = []
                for bb in range(nb4):
                    psl = bb % 2
                    psc, rsc_ = ps_sc[psl], r_pssc[psl]
                    pb, rpb = p_sb[psl], r_p[psl]
                    kts = [kt for kt in range(2 * bb, 2 * bb + 2) if kt < nkt]
                    for j, kt in enumerate(kts):
                        sch.step("pe", MM(psc[:, j * 512:j * 512 + 512],
                                          kb_[:, kt * 128:kt * 128 + 128],
                                          qb_[:, cc * 512:cc * 512 + 512]),
                                 reads=[rkb, rqb],
                                 writes=([rsc_] if j == 0 else []),
                                 tag=f"sc{var}_{h}_{cc}_{kt}")
                    sch.group_end(rsc_)
                    wid = 512 * len(kts)
                    sch.step("act", lambda e, psc=psc, pb=pb, wid=wid: e.activation(
                        pb[:, 0:wid], psc[:, 0:wid], AF.Exp, bias=EXP_BIAS),
                        reads=[rsc_], writes=[rpb], tag=f"ex{var}_{h}_{cc}_{bb}")
                    filler = pump(filler)
                    if bb == nb4 - 1 and pending_norm:
                        pending_norm.pop(0)()
                    for j, kt in enumerate(kts):
                        mj = kt - (nkt - 4)
                        if 0 <= mj < 4:
                            sch.step("pl", lambda e, pb=pb, j=j, mj=mj: e.tensor_mul(
                                pb[:, j * 512:j * 512 + 512], pb[:, j * 512:j * 512 + 512],
                                mask_sb[:, mj * 512:mj * 512 + 512]),
                                reads=[rpb, r_const], writes=[rpb],
                                tag=f"mk{var}_{h}_{cc}_{kt}")
                    pvsl = cc % 2

                    def emit_pv(pend, pvsl=pvsl, nkt=nkt):
                        ppsl, pkts = pend
                        for j, kt in enumerate(pkts):
                            sch.step("pe", MM(ps_pv[:, pvsl * 512:pvsl * 512 + 512],
                                              bass.AP(v_sb, kt * 130 + hl * 65,
                                                      [[32 * 130, 128], [1, 65]]),
                                              p_sb[ppsl][:, j * 512:j * 512 + 512],
                                              start=(kt == 0), stop=(kt == nkt - 1)),
                                     reads=[r_p[ppsl], r_v],
                                     writes=([r_pspv[pvsl]] if kt == 0 else []),
                                     tag=f"pv{var}_{h}_{cc}_{kt}")
                    pends.append((psl, kts))
                    if len(pends) > 2:
                        emit_pv(pends.pop(0))
                pvsl = cc % 2
                while pends:
                    emit_pv(pends.pop(0))
                sch.group_end(r_pspv[pvsl])
                def emit_norm(h=h, cc=cc, pvsl=pvsl):
                    sch.step("act", lambda e, pvsl=pvsl: e.activation(
                        recv_sb[0:1, :], ps_pv[64:65, pvsl * 512:pvsl * 512 + 512], AF.Ln),
                        reads=[r_pspv[pvsl]], writes=[r_recv], tag=f"rl{var}_{h}_{cc}")
                    sch.step("act", lambda e: e.activation(
                        recv_sb[0:1, :], recv_sb[0:1, :], AF.Exp, scale=-1.0),
                        reads=[r_recv], writes=[r_recv], tag=f"rc{var}_{h}_{cc}")
                    sch.step("pe", MM(ps_decT[0:64, 512:1024], ones1[0:1, 0:64],
                                      recv_sb[0:1, :]),
                             reads=[r_recv, r_const], writes=[r_psd[1]],
                             tag=f"nb{var}_{h}_{cc}")
                    sch.step("dve", lambda e: e.tensor_copy(nrm_sb[:, :],
                                                            ps_decT[0:64, 512:1024]),
                        reads=[r_psd[1]], writes=[r_nrm], tag=f"nbe{var}_{h}_{cc}")
                    if h % 2 == 0:
                        sch.step("dve", lambda e, h=h, cc=cc, pvsl=pvsl: e.tensor_mul(
                            attn_all[0:64,
                                     (h // 2) * TOWN + cc * 512:(h // 2) * TOWN + cc * 512 + 512],
                            ps_pv[0:64, pvsl * 512:pvsl * 512 + 512], nrm_sb[:, :]),
                            reads=[r_pspv[pvsl], r_nrm], writes=[r_attnall[h]],
                            tag=f"an{var}_{h}_{cc}")
                    else:
                        sch.step("dve", lambda e, pvsl=pvsl: e.tensor_mul(
                            aschat[:, :], ps_pv[0:64, pvsl * 512:pvsl * 512 + 512],
                            nrm_sb[:, :]),
                            reads=[r_pspv[pvsl], r_nrm], writes=[r_aschat],
                            tag=f"an{var}_{h}_{cc}")
                        sch.step("sp", lambda e, h=h, cc=cc: e.dma_start(
                            out=attn_all[64:128,
                                         (h // 2) * TOWN + cc * 512:(h // 2) * TOWN + cc * 512 + 512],
                            in_=aschat[:, :]),
                            reads=[r_aschat], writes=[r_attnall[h]], dma=True,
                            tag=f"as{var}_{h}_{cc}")
                pending_norm.append(emit_norm)
            while pending_norm:
                pending_norm.pop(0)()
        sch.flush()
        end_counts.append(sch.snapshot())
    assert end_counts[0] == end_counts[1], (
        "switch bodies must produce identical semaphore counts",
        {k: (end_counts[0][k], end_counts[1][k]) for k in end_counts[0]
         if end_counts[0][k] != end_counts[1][k]})
    sch.set_counts(end_counts[0])

    t_attn = tap("attn_pre", [128, 8 * TOWN], BF16)
    if t_attn is not None:
        sch.step("sp", lambda e: e.dma_start(out=t_attn[:, :], in_=attn_all[:, :]),
                 deps=[r for h in range(NH) for r in ([r_attnall[h].w] if r_attnall[h].w else [])],
                 dma=True, tag="tap_attn")
    sch.flush()
    es3.close()
    if stop_after < 4:
        es_mid.close()
        esL.close()
        es.close()
        return nc, tap_t

    es4 = ExitStack()
    S4 = es4.enter_context
    xo_sb = [S4(nc.sbuf_tensor(f"xo{i}", [128, 8 * 512], F32)) for i in range(2)]
    po_sb = S4(nc.sbuf_tensor("po_sb", [128, 512], F32))
    ps_wo = S4(nc.psum_tensor("ps_wo", [128, 1024], F32))
    ps_wo2 = S4(nc.psum_tensor("ps_wo2", [128, 1024], F32))
    r_xo = [Reg("xo0"), Reg("xo1")]
    r_po = Reg("po")
    r_pswo = [Reg("pswo0"), Reg("pswo1")]
    r_pswo2 = [Reg("pswo20"), Reg("pswo21")]

    for cc in range(4):
        b = cc % 2
        sch.step("sp", lambda e, cc=cc, b=b: e.dma_start(
            out=bass.AP(xo_sb[b], 0, [[8 * 512, 128], [512, 8], [1, 512]]),
            in_=bass.AP(xTo.tensor, cc * 512, [[TOWN, 128], [128 * TOWN, 8], [1, 512]])),
            writes=[r_xo[b]], dma=True, tag=f"ld_xo{cc}")
        for dchunk in range(8):
            sl = dchunk % 2
            rp, rp2 = r_pswo[sl], r_pswo2[sl]
            for j, h in enumerate(range(0, NH, 2)):
                lhs = wo_sb[0:64,
                            (h // 2) * D + dchunk * 128:(h // 2) * D + dchunk * 128 + 128]
                rhs = attn_all[0:64,
                               (h // 2) * TOWN + cc * 512:(h // 2) * TOWN + cc * 512 + 512]
                sch.step("pe", MM(ps_wo[:, sl * 512:sl * 512 + 512], lhs, rhs,
                                  start=(j == 0), stop=(j == 7)),
                         reads=[r_wo, r_attnall[h]],
                         writes=([rp] if j == 0 else []), tag=f"wo{cc}_{dchunk}_{h}")
            sch.group_end(rp)
            for j, h in enumerate(range(1, NH, 2)):
                lhs = wo_sb[64:128,
                            (h // 2) * D + dchunk * 128:(h // 2) * D + dchunk * 128 + 128]
                rhs = attn_all[64:128,
                               (h // 2) * TOWN + cc * 512:(h // 2) * TOWN + cc * 512 + 512]
                sch.step("pe", MM(ps_wo2[:, sl * 512:sl * 512 + 512], lhs, rhs,
                                  start=(j == 0), stop=(j == 7)),
                         reads=[r_wo, r_attnall[h]],
                         writes=([rp2] if j == 0 else []), tag=f"wo2{cc}_{dchunk}_{h}")
            sch.group_end(rp2)
            sch.step("dve", lambda e, dchunk=dchunk, sl=sl, b=b: e.tensor_add(
                po_sb[:, :], ps_wo[:, sl * 512:sl * 512 + 512],
                xo_sb[b][:, dchunk * 512:dchunk * 512 + 512]),
                reads=[rp, r_xo[b]], writes=[r_po], tag=f"poa{cc}_{dchunk}")
            sch.step("dve", lambda e, sl=sl: e.scalar_tensor_tensor(
                po_sb[:, :], ps_wo2[:, sl * 512:sl * 512 + 512], 1.0, po_sb[:, :],
                ALU.mult, ALU.add),
                reads=[rp2, r_po], writes=[r_po], tag=f"pob{cc}_{dchunk}")
            sch.step("sp", lambda e, dchunk=dchunk, cc=cc: e.dma_start(
                out=outT[dchunk * 128:(dchunk + 1) * 128, cc * 512:cc * 512 + 512],
                in_=po_sb[:, :]),
                reads=[r_po], dma=True, tag=f"st_p{cc}_{dchunk}")
    sch.flush()
    es4.close()
    es_mid.close()
    esL.close()
    if stop_after < 5:
        es.close()
        return nc, tap_t


    es5 = ExitStack()
    S5 = es5.enter_context
    xn_sb = S5(nc.sbuf_tensor("xn_sb", [128, 8 * TOWN], BF16))
    h_sb = S5(nc.sbuf_tensor("h_sb", [128, 24 * 1024], BF16))
    silu_sb = S5(nc.sbuf_tensor("silu_sb", [128, 1024], BF16))
    wg_sb = [S5(nc.sbuf_tensor(f"wg{i}", [128, 8 * 384], BF16)) for i in range(2)]
    wu_sb = [S5(nc.sbuf_tensor(f"wu{i}", [128, 8 * 384], BF16)) for i in range(2)]
    wd_sb = [S5(nc.sbuf_tensor(f"wd{i}", [128, 24 * 128], BF16)) for i in range(2)]
    xo5_sb = [S5(nc.sbuf_tensor(f"xo5{i}", [128, 8 * 512], BF16)) for i in range(2)]
    xf5_sb = [S5(nc.sbuf_tensor(f"xf5{i}", [128, 4 * 512], F32)) for i in range(2)]
    xsq5 = S5(nc.sbuf_tensor("xsq5", [128, 2 * 512], BF16))
    pr_sb = [S5(nc.sbuf_tensor(f"pr{i}", [128, 512], F32)) for i in range(2)]
    ob_sb = S5(nc.sbuf_tensor("ob_sb", [128, 512], F32))
    vec5 = S5(nc.sbuf_tensor("vec5", [1, 3 * 512], F32))
    ps5_bc = S5(nc.psum_tensor("ps5_bc", [128, 1024], F32))
    es5a = ExitStack()
    ps5_ln = es5a.enter_context(nc.psum_tensor("ps5_ln", [1, 1024], F32))
    r_xn, r_h, r_silu = Reg("xn"), Reg("h"), Reg("silu")
    r_wg = [Reg("wg0"), Reg("wg1")]
    r_wu = [Reg("wu0"), Reg("wu1")]
    r_wd = [Reg("wd0"), Reg("wd1")]
    r_xo5 = [Reg("xo50"), Reg("xo51")]
    r_xf5 = [Reg("xf50"), Reg("xf51")]
    r_xsq5 = [Reg("xsq50"), Reg("xsq51")]
    r_pr = [Reg("pr0"), Reg("pr1")]
    r_ob, r_vec5 = Reg("ob"), Reg("vec5")
    r_ps5ln, r_ps5bc0, r_ps5bc1 = Reg("ps5ln"), Reg("ps5bc0"), Reg("ps5bc1")
    r_ps5g, r_ps5u = Reg("ps5g"), Reg("ps5u")
    r_ps5d = [Reg("ps5d0"), Reg("ps5d1")]

    def V5(i):
        return vec5[0:1, i * 512:(i + 1) * 512]

    for cc in range(4):
        b = cc % 2
        for hh2 in range(2):
            xfb = (2 * cc + hh2) % 2
            sch.step("sp", lambda e, cc=cc, hh2=hh2, xfb=xfb: e.dma_start(
                out=bass.AP(xf5_sb[xfb], 0, [[4 * 512, 128], [512, 4], [1, 512]]),
                in_=bass.AP(xTo.tensor, cc * 512 + hh2 * 4 * 128 * TOWN,
                            [[TOWN, 128], [128 * TOWN, 4], [1, 512]])),
                writes=[r_xf5[xfb]], dma=True, tag=f"l5x{cc}_{hh2}")
            sch.step("act", lambda e, b=b, hh2=hh2, xfb=xfb: e.activation(
                xo5_sb[b][:, hh2 * 2048:(hh2 + 1) * 2048], xf5_sb[xfb][:, :], AF.Copy),
                reads=[r_xf5[xfb]], writes=[r_xo5[b]], tag=f"cx5{cc}_{hh2}")
        for c in range(8):
            sch.step("pe", MM(ps5_ln[0:1, 0:512], onesv[:, 0:1],
                              xo5_sb[b][:, c * 512:(c + 1) * 512],
                              start=(c == 0), stop=(c == 7)),
                     reads=[r_xo5[b], r_const], writes=([r_ps5ln] if c == 0 else []),
                     tag=f"l5m{cc}_{c}")
        sch.group_end(r_ps5ln)
        sch.step("dve", lambda e: e.tensor_scalar_mul(V5(0), ps5_ln[0:1, 0:512], 1.0 / D),
                 reads=[r_ps5ln], writes=[r_vec5], tag=f"l5mm{cc}")
        sch.step("pe", MM(ps5_bc[:, 0:512], ones1[0:1, :], V5(0)),
                 reads=[r_vec5, r_const], writes=[r_ps5bc0], tag=f"l5bc{cc}")
        for c in range(8):
            sq = c % 2
            sch.step("act", lambda e, b=b, c=c, sq=sq: e.activation(
                xsq5[:, sq * 512:sq * 512 + 512], xo5_sb[b][:, c * 512:(c + 1) * 512],
                AF.Square),
                reads=[r_xo5[b]], writes=[r_xsq5[sq]], tag=f"l5s{cc}_{c}")
            sch.step("pe", MM(ps5_ln[0:1, 512:1024], onesv[:, 0:1],
                              xsq5[:, sq * 512:sq * 512 + 512],
                              start=(c == 0), stop=(c == 7)),
                     reads=[r_xsq5[sq], r_const], writes=([r_ps5ln] if c == 0 else []),
                     tag=f"l5q{cc}_{c}")
        sch.group_end(r_ps5ln)
        sch.step("dve", lambda e: e.tensor_scalar_mul(V5(1), ps5_ln[0:1, 512:1024], 1.0 / D),
                 reads=[r_ps5ln], writes=[r_vec5], tag=f"l5v{cc}")
        sch.step("dve", lambda e: e.tensor_mul(V5(2), V5(0), V5(0)),
                 reads=[r_vec5], writes=[r_vec5], tag=f"l5m2{cc}")
        sch.step("dve", lambda e: e.tensor_sub(V5(1), V5(1), V5(2)),
                 reads=[r_vec5], writes=[r_vec5], tag=f"l5var{cc}")
        sch.step("act", lambda e: e.activation(V5(1), V5(1), AF.Ln, bias=1e-5),
                 reads=[r_vec5], writes=[r_vec5], tag=f"l5sd{cc}")
        sch.step("act", lambda e: e.activation(V5(2), V5(1), AF.Exp, scale=-0.5),
                 reads=[r_vec5], writes=[r_vec5], tag=f"l5r{cc}")
        sch.step("pe", MM(ps5_bc[:, 512:1024], ones1[0:1, :], V5(2)),
                 reads=[r_vec5, r_const], writes=[r_ps5bc1], tag=f"l5rb{cc}")
        sch.step("dve", lambda e, b=b, cc=cc: e.tensor_sub(
            bass.AP(xn_sb, cc * 512, [[8 * TOWN, 128], [TOWN, 8], [1, 512]]),
            bass.AP(xo5_sb[b], 0, [[8 * 512, 128], [512, 8], [1, 512]]),
            bass.AP(ps5_bc, 0, [[1024, 128], [0, 8], [1, 512]])),
            reads=[r_xo5[b], r_ps5bc0], writes=[r_xn], tag=f"l5c{cc}")
        sch.step("dve", lambda e, cc=cc: e.tensor_mul(
            bass.AP(xn_sb, cc * 512, [[8 * TOWN, 128], [TOWN, 8], [1, 512]]),
            bass.AP(xn_sb, cc * 512, [[8 * TOWN, 128], [TOWN, 8], [1, 512]]),
            bass.AP(ps5_bc, 512, [[1024, 128], [0, 8], [1, 512]])),
            reads=[r_xn, r_ps5bc1], writes=[r_xn], tag=f"l5n{cc}")
    sch.flush()

    es5a.close()
    ps5_g = S5(nc.psum_tensor("ps5_g", [128, 1024], F32))
    ps5_u = S5(nc.psum_tensor("ps5_u", [128, 1024], F32))
    ps5_d = S5(nc.psum_tensor("ps5_d", [128, 1024], F32))
    r_outT = Reg("outT")
    last_store = None
    for tw in range(2):
        for hb in range(8):
            wbuf = hb % 2
            sch.step("pl", lambda e, hb=hb, wbuf=wbuf: e.dma_start(
                out=bass.AP(wg_sb[wbuf], 0, [[8 * 384, 128], [384, 8], [1, 384]]),
                in_=bass.AP(wgT.tensor, hb * 384, [[HID, 128], [128 * HID, 8], [1, 384]])),
                writes=[r_wg[wbuf]], dma=True, tag=f"f_wg{tw}_{hb}")
            sch.step("pl", lambda e, hb=hb, wbuf=wbuf: e.dma_start(
                out=bass.AP(wu_sb[wbuf], 0, [[8 * 384, 128], [384, 8], [1, 384]]),
                in_=bass.AP(wuT.tensor, hb * 384, [[HID, 128], [128 * HID, 8], [1, 384]])),
                writes=[r_wu[wbuf]], dma=True, tag=f"f_wu{tw}_{hb}")
            for j in range(3):
                hh = 3 * hb + j
                for nn in range(2):
                    for k in range(8):
                        sch.step("pe", MM(ps5_g[:, nn * 512:nn * 512 + 512],
                                          wg_sb[wbuf][:, k * 384 + j * 128:k * 384 + j * 128 + 128],
                                          xn_sb[:, k * TOWN + tw * 1024 + nn * 512:
                                                k * TOWN + tw * 1024 + nn * 512 + 512],
                                          start=(k == 0), stop=(k == 7)),
                                 reads=[r_wg[wbuf], r_xn],
                                 writes=([r_ps5g] if (nn == 0 and k == 0) else []),
                                 tag=f"fg{tw}_{hh}_{nn}_{k}")
                sch.group_end(r_ps5g)
                sch.step("act", lambda e: e.activation(
                    silu_sb[:, :], ps5_g[:, :], AF.Silu, scale=FFS),
                    reads=[r_ps5g], writes=[r_silu], tag=f"fs{tw}_{hh}")
                for nn in range(2):
                    for k in range(8):
                        sch.step("pe", MM(ps5_u[:, nn * 512:nn * 512 + 512],
                                          wu_sb[wbuf][:, k * 384 + j * 128:k * 384 + j * 128 + 128],
                                          xn_sb[:, k * TOWN + tw * 1024 + nn * 512:
                                                k * TOWN + tw * 1024 + nn * 512 + 512],
                                          start=(k == 0), stop=(k == 7)),
                                 reads=[r_wu[wbuf], r_xn],
                                 writes=([r_ps5u] if (nn == 0 and k == 0) else []),
                                 tag=f"fu{tw}_{hh}_{nn}_{k}")
                sch.group_end(r_ps5u)
                sch.step("dve", lambda e, hh=hh: e.scalar_tensor_tensor(
                    h_sb[:, hh * 1024:(hh + 1) * 1024], ps5_u[:, :], FFS, silu_sb[:, :],
                    ALU.mult, ALU.mult),
                    reads=[r_ps5u, r_silu], writes=[r_h], tag=f"fh{tw}_{hh}")
        def wd_load(dchunk):
            dbuf = dchunk % 2
            sch.step("pl", lambda e, dchunk=dchunk, dbuf=dbuf: e.dma_start(
                out=bass.AP(wd_sb[dbuf], 0, [[24 * 128, 128], [128, 24], [1, 128]]),
                in_=bass.AP(wdT.tensor, dchunk * 128, [[D, 128], [128 * D, 24], [1, 128]])),
                writes=[r_wd[dbuf]], dma=True, tag=f"f_wd{tw}_{dchunk}")
        wd_load(0)
        for dchunk in range(8):
            dbuf = dchunk % 2
            if dchunk + 1 < 8:
                wd_load(dchunk + 1)
            for nn in range(2):
                sl = nn
                for k in range(24):
                    sch.step("pe", MM(ps5_d[:, sl * 512:sl * 512 + 512],
                                      wd_sb[dbuf][:, k * 128:(k + 1) * 128],
                                      h_sb[:, k * 1024 + nn * 512:k * 1024 + nn * 512 + 512],
                                      start=(k == 0), stop=(k == 23)),
                             reads=[r_wd[dbuf], r_h],
                             writes=([r_ps5d[sl]] if k == 0 else []),
                             tag=f"fd{tw}_{dchunk}_{nn}_{k}")
                sch.group_end(r_ps5d[sl])
                pbuf = nn
                sch.step("sp", lambda e, dchunk=dchunk, tw=tw, nn=nn, pbuf=pbuf: e.dma_start(
                    out=pr_sb[pbuf][:, :],
                    in_=outT[dchunk * 128:(dchunk + 1) * 128,
                             tw * 1024 + nn * 512:tw * 1024 + nn * 512 + 512]),
                    reads=[r_outT], writes=[r_pr[pbuf]], dma=True, tag=f"fp{tw}_{dchunk}_{nn}")
                sch.step("dve", lambda e, sl=sl, pbuf=pbuf: e.scalar_tensor_tensor(
                    ob_sb[:, :], ps5_d[:, sl * 512:sl * 512 + 512], FFS, pr_sb[pbuf][:, :],
                    ALU.mult, ALU.add),
                    reads=[r_ps5d[sl], r_pr[pbuf]], writes=[r_ob], tag=f"fo{tw}_{dchunk}_{nn}")
                last_store = sch.step("sp", lambda e, dchunk=dchunk, tw=tw, nn=nn: e.dma_start(
                    out=outT[dchunk * 128:(dchunk + 1) * 128,
                             tw * 1024 + nn * 512:tw * 1024 + nn * 512 + 512],
                    in_=ob_sb[:, :]),
                    reads=[r_ob], writes=[r_outT], dma=True, tag=f"fst{tw}_{dchunk}_{nn}")
    sch.step("sp", lambda e: e.nop(), deps=[last_store], tag="final_wait")
    sch.flush()
    es5.close()
    es.close()
    return nc, tap_t


def make_core_inputs(ins, const, b, core_type):
    x = ins["L0_attn_x"][b]
    own = own_positions(core_type)
    cosq = np.ascontiguousarray(
        (np.tile(const["cos_t"], (4, 1))[:, own] * SCALE).astype(bfdt))
    sinq = np.ascontiguousarray(
        (np.tile(const["sin_t"], (4, 1))[:, own] * SCALE).astype(bfdt))
    return {
        "xT": np.ascontiguousarray(x.T),
        "xTo": np.ascontiguousarray(x.T[:, own]),
        "w_kv": ins["L0_attn_W_kv"], "w_q": ins["L0_attn_W_q"],
        "w_kdec": ins["L0_attn_W_kdec"], "w_vdec": ins["L0_attn_W_vdec"],
        "w_qdec": ins["L0_attn_W_qdec"], "w_krope": ins["L0_attn_W_krope"],
        "w_qrope": ins["L0_attn_W_qrope"], "w_o": ins["L0_attn_W_o"],
        "g_kv": ins["L0_attn_g_kv"],
        "wgT": np.ascontiguousarray(ins["L0_ffn_w_gate"].T),
        "wuT": np.ascontiguousarray(ins["L0_ffn_w_up"].T),
        "wdT": np.ascontiguousarray(ins["L0_ffn_w_down"].T),
        "cosk": const["cosk"], "sink": const["sink"],
        "cosqo": cosq, "sinqo": sinq,
        "maskc": const["maskc"], "r4t": const["r4t"], "psel": const["psel"],
    }


_NC_CACHE = {}


def kernel(**inputs):
    from concourse.bass_utils import run_bass_kernel_spmd
    if "nc" not in _NC_CACHE:
        _NC_CACHE["nc"] = build_nc()[0]
    nc = _NC_CACHE["nc"]
    const = host_constants()
    in_maps = [make_core_inputs(inputs, const, c // 2, c % 2) for c in range(8)]
    res = run_bass_kernel_spmd(nc, in_maps, core_ids=list(range(8)))
    out = np.empty((4, T, D), np.float32)
    for c in range(8):
        own = own_positions(c % 2)
        out[c // 2, own, :] = np.asarray(res.results[c]["outT"]).T
    return out


def host_constants():
    inv_freq = 1.0 / 10000.0 ** (np.arange(0, RD, 2, dtype=np.float32) / RD)
    t = np.arange(T, dtype=np.float32)
    freqs = np.outer(t, inv_freq)
    emb = np.concatenate([freqs, freqs], -1)
    cos_t = np.cos(emb).astype(np.float32).T
    sin_t = np.sin(emb).astype(np.float32).T
    cosk = np.ascontiguousarray(np.tile(cos_t, (4, 1)).astype(bfdt))
    sink = np.ascontiguousarray(np.tile(sin_t, (4, 1)).astype(bfdt))
    kk = np.arange(128)[:, None]
    qq = np.arange(512)[None, :]
    masks = np.ascontiguousarray(np.concatenate(
        [(qq - kk >= 128 * j).astype(np.float32) for j in range(4)],
        axis=1).astype(bfdt))
    R = np.zeros((32, 32), np.float32)
    for i in range(16):
        R[i, i + 16] = -1.0
        R[i + 16, i] = 1.0
    R4 = np.zeros((128, 128), np.float32)
    for g in range(4):
        R4[g * 32:(g + 1) * 32, g * 32:(g + 1) * 32] = R
    r4t = np.ascontiguousarray(R4.T.astype(bfdt))
    ps = np.zeros((128, 96), np.float32)
    for ci, g in enumerate((0, 1, 3)):
        for i in range(32):
            ps[g * 32 + i, ci * 32 + i] = 1.0
    psel = np.ascontiguousarray(ps.astype(bfdt))
    return dict(cosk=cosk, sink=sink, maskc=masks, r4t=r4t, psel=psel,
                cos_t=cos_t, sin_t=sin_t)


def own_positions(core_type):
    if core_type == 0:
        return np.concatenate([np.arange(0, 1024), np.arange(3072, 4096)])
    return np.arange(1024, 3072)


# version 12
# speedup vs baseline: 1.2934x; 1.2934x over previous
import numpy as np
import ml_dtypes
import concourse.bass as bass
import concourse.mybir as mybir
from contextlib import ExitStack

F32 = mybir.dt.float32
BF16 = mybir.dt.bfloat16
I32 = mybir.dt.int32
AF = mybir.ActivationFunctionType
ALU = mybir.AluOpType
bfdt = ml_dtypes.bfloat16

T, TOWN, D, KVR, QR, NH, HD, RD, HID = 4096, 2048, 1024, 256, 512, 16, 64, 32, 3072
NTG = T // 512
SCALE = float(1.0 / np.sqrt(96.0))
EXP_BIAS = -8.0
FFS = float(1.0 / 127.0)
CHUNKS = {0: [(0, 4), (512, 8), (3072, 28), (3584, 32)],
          1: [(1024, 12), (1536, 16), (2048, 20), (2560, 24)]}
ENGINES = ("pe", "act", "dve", "pl", "sp")


class Step:
    __slots__ = ("eng", "emit", "deps", "idx", "seq", "inc_amt", "needed", "tag",
                 "gen", "self_wait")

    def __init__(self, eng, emit, deps, idx, tag="", gen=0):
        self.eng, self.emit, self.deps, self.idx, self.tag = eng, emit, deps, idx, tag
        self.seq = None
        self.inc_amt = 0
        self.needed = False
        self.gen = gen
        self.self_wait = None


class Reg:
    __slots__ = ("w", "readers", "name")

    def __init__(self, name=""):
        self.w, self.readers, self.name = None, [], name


class Sched:
    def __init__(self, nc, es):
        self.nc = nc
        self.steps = []
        self.gen = 0
        self.sems = {}
        self.counts = {}
        for e in ENGINES:
            self.sems[e] = es.enter_context(nc.semaphore(f"sem_{e}"))
            self.sems[e + "_dma"] = es.enter_context(nc.semaphore(f"sem_{e}dma"))
            self.counts[e] = 0
            self.counts[e + "_dma"] = 0

    def step(self, eng, emit, reads=(), writes=(), deps=(), dma=False, tag=""):
        alldeps = list(deps)
        for r in reads:
            if r.w is not None:
                alldeps.append(r.w)
        for wreg in writes:
            if wreg.w is not None:
                alldeps.append(wreg.w)
            alldeps.extend(wreg.readers)
        key = eng + ("_dma" if dma else "")
        pruned, seen = [], set()
        for d in alldeps:
            if d is None or id(d) in seen:
                continue
            seen.add(id(d))
            if d.gen < self.gen:
                continue
            if d.eng == key and key in ("pe", "sp_dma", "pl_dma", "act_dma"):
                continue
            pruned.append(d)
        s = Step(key, emit, pruned, len(self.steps), tag, gen=self.gen)
        self.steps.append(s)
        for d in pruned:
            d.needed = True
        for r in reads:
            r.readers.append(s)
        for wreg in writes:
            wreg.w = s
            wreg.readers = []
        return s

    def group_end(self, reg):
        reg.w = self.steps[-1]
        reg.readers = []
        return reg.w

    def snapshot(self):
        return dict(self.counts)

    def set_counts(self, c):
        self.counts = dict(c)

    def flush(self, in_block=True):
        nc = self.nc
        per_eng = {e: [] for e in ENGINES}
        for s in self.steps:
            e = s.eng[:-4] if s.eng.endswith("_dma") else s.eng
            per_eng[e].append(s)
        for e in ENGINES:
            run = []

            def close_run(run):
                if run:
                    for s in run:
                        self.counts[s.eng] += 16
                    for s in run:
                        s.seq = self.counts[run[0].eng]
            nd = 0
            for s in per_eng[e]:
                if s.eng.endswith("_dma"):
                    if s.deps:
                        close_run(run)
                        s.self_wait = self.counts[s.eng]
                        run = [s]
                    else:
                        run.append(s)
                else:
                    close_run(run)
                    run = []
            close_run(run)
        for s in self.steps:
            if not s.eng.endswith("_dma") and s.needed:
                s.inc_amt = 1
                self.counts[s.eng] += 1
                s.seq = self.counts[s.eng]
        for s in self.steps:
            if s.eng.endswith("_dma"):
                s.inc_amt = 16
        sched = self

        def make(engname):
            def body(eng):
                cur = {}
                for s in per_eng[engname]:
                    isdma = s.eng.endswith("_dma")
                    if s.self_wait:
                        eng.wait_ge(sched.sems[s.eng], s.self_wait)
                    for d in s.deps:
                        assert d.seq is not None, f"dep of {s.tag} on {d.tag} lacks seq"
                        if not isdma and cur.get(d.eng, 0) >= d.seq:
                            continue
                        cur[d.eng] = max(cur.get(d.eng, 0), d.seq)
                        eng.wait_ge(sched.sems[d.eng], d.seq)
                    inst = s.emit(eng)
                    if s.inc_amt:
                        inst.then_inc(sched.sems[s.eng], s.inc_amt)
                dkey = engname + "_dma"
                if any(s.eng == dkey for s in per_eng[engname]):
                    eng.wait_ge(sched.sems[dkey], sched.counts[dkey])
            return body

        with nc.Block() as blk:
            for engname, attr in (("sp", "sync"), ("pl", "gpsimd"), ("pe", "tensor"),
                                  ("act", "scalar"), ("dve", "vector")):
                if per_eng[engname]:
                    getattr(blk, attr)(make(engname))
        self.steps = []
        self.gen += 1


def MM(out, lhsT, rhs, start=True, stop=True, tp=None):
    if tp is None:
        return lambda e: e.matmul(out, lhsT, rhs, start=start, stop=stop)
    return lambda e: e.matmul(out, lhsT, rhs, start=start, stop=stop, tile_position=tp)


def build_nc(taps=(), stop_after=99, ntg=NTG, parts=('ln','kv','qc','kr','rope')):
    taps = set(taps)
    nc = bass.Bass("TRN2", target_bir_lowering=False)

    xT = nc.dram_tensor("xT", [D, T], F32, kind="ExternalInput").ap()
    xTo = nc.dram_tensor("xTo", [D, TOWN], F32, kind="ExternalInput").ap()
    w_kv = nc.dram_tensor("w_kv", [D, KVR], F32, kind="ExternalInput").ap()
    w_q = nc.dram_tensor("w_q", [D, QR], F32, kind="ExternalInput").ap()
    w_kdec = nc.dram_tensor("w_kdec", [KVR, D], F32, kind="ExternalInput").ap()
    w_vdec = nc.dram_tensor("w_vdec", [KVR, D], F32, kind="ExternalInput").ap()
    w_qdec = nc.dram_tensor("w_qdec", [QR, D], F32, kind="ExternalInput").ap()
    w_krope = nc.dram_tensor("w_krope", [D, QR], F32, kind="ExternalInput").ap()
    w_qrope = nc.dram_tensor("w_qrope", [QR, QR], F32, kind="ExternalInput").ap()
    w_o = nc.dram_tensor("w_o", [D, D], F32, kind="ExternalInput").ap()
    g_kv = nc.dram_tensor("g_kv", [KVR], F32, kind="ExternalInput").ap()
    wgT = nc.dram_tensor("wgT", [D, HID], I32, kind="ExternalInput").ap()
    wuT = nc.dram_tensor("wuT", [D, HID], I32, kind="ExternalInput").ap()
    wdT = nc.dram_tensor("wdT", [HID, D], I32, kind="ExternalInput").ap()
    cosk = nc.dram_tensor("cosk", [128, T], BF16, kind="ExternalInput").ap()
    sink = nc.dram_tensor("sink", [128, T], BF16, kind="ExternalInput").ap()
    cosqo = nc.dram_tensor("cosqo", [128, TOWN], BF16, kind="ExternalInput").ap()
    sinqo = nc.dram_tensor("sinqo", [128, TOWN], BF16, kind="ExternalInput").ap()
    maskc = nc.dram_tensor("maskc", [128, 4 * 512], BF16, kind="ExternalInput").ap()
    r4t = nc.dram_tensor("r4t", [128, 128], BF16, kind="ExternalInput").ap()
    psel = nc.dram_tensor("psel", [128, 3 * 32], BF16, kind="ExternalInput").ap()
    outT = nc.dram_tensor("outT", [D, TOWN], F32, kind="ExternalOutput").ap()

    tap_t = {}

    def tap(name, shape, dtype=F32):
        if name in taps:
            tap_t[name] = nc.dram_tensor("tap_" + name, shape, dtype,
                                         kind="ExternalOutput").ap()
        return tap_t.get(name)

    es = ExitStack()
    S = es.enter_context
    sch = Sched(nc, es)

    mask_sb = S(nc.sbuf_tensor("mask_sb", [128, 4 * 512], BF16))
    r4t_sb = S(nc.sbuf_tensor("r4t_sb", [128, 128], BF16))
    psel_sb = S(nc.sbuf_tensor("psel_sb", [128, 3 * 32], BF16))
    onesv = S(nc.sbuf_tensor("onesv", [128, 1], BF16))
    ones1 = S(nc.sbuf_tensor("ones1", [1, 128], F32))
    gkv_sb = S(nc.sbuf_tensor("gkv_sb", [128, 2], F32))
    CVALS = [1e-5, 1e-6, EXP_BIAS, FFS, -0.5, -1.0]
    cst_sb = S(nc.sbuf_tensor("cst_sb", [128, len(CVALS)], F32))
    wst = {}
    r_const, r_gkv = Reg("const"), Reg("gkv")

    r_attnall = [Reg(f"at{h}") for h in range(NH)]
    r_wo = Reg("wo")

    esL = ExitStack()
    SL = esL.enter_context
    kv_sb = SL(nc.sbuf_tensor("kv_sb", [128, 2 * T], BF16))
    kr_sb = SL(nc.sbuf_tensor("kr_sb", [128, 4 * T], BF16))
    qc_dram = nc.dram_tensor("qc_scr", [4 * 128, T], BF16, kind="Internal").ap()
    r_kv, r_qc, r_kr, r_vec = Reg("kv"), Reg("qc"), Reg("kr"), Reg("vec")

    sch.step("sp", lambda e: e.dma_start(out=mask_sb[:, :], in_=maskc[:, :]),
             writes=[r_const], dma=True, tag="ld_mask")
    sch.step("sp", lambda e: e.dma_start(out=r4t_sb[:, :], in_=r4t[:, :]),
             writes=[r_const], dma=True, tag="ld_r4t")
    sch.step("sp", lambda e: e.dma_start(out=psel_sb[:, :], in_=psel[:, :]),
             writes=[r_const], dma=True, tag="ld_psel")
    sch.step("sp", lambda e: e.dma_start(
        out=bass.AP(gkv_sb, 0, [[2, 128], [1, 2]]),
        in_=bass.AP(g_kv.tensor, 0, [[1, 128], [128, 2]]),
        allow_slow_non_contiguous=True),
        writes=[r_gkv], dma=True, tag="ld_gkv")
    sch.step("pl", lambda e: e.memset(onesv[:, :], 1.0), writes=[r_const], tag="ones_v")
    sch.step("pl", lambda e: e.memset(ones1[:, :], 1.0), writes=[r_const], tag="ones_1")
    for i, v in enumerate(CVALS):
        sch.step("pl", lambda e, i=i, v=v: e.memset(cst_sb[:, i:i + 1], float(v)),
                 writes=[r_const], tag=f"cst{i}")
        nc.const_aps.register(F32, float(v), cst_sb[:, i:i + 1])

    es0 = ExitStack()
    S0 = es0.enter_context
    vec_sb = S0(nc.sbuf_tensor("vec_sb", [1, 6 * 512], F32))
    qc_sb = S0(nc.sbuf_tensor("qc_sb", [128, 4 * T], BF16))
    wkv_sb = S0(nc.sbuf_tensor("wkv_sb", [128, 8 * KVR], BF16))
    wq_sb = S0(nc.sbuf_tensor("wq_sb", [128, 8 * QR], BF16))
    wkr_sb = S0(nc.sbuf_tensor("wkr_sb", [128, 8 * QR], BF16))
    xg_sb = [S0(nc.sbuf_tensor(f"xg{i}", [128, 8 * 512], BF16)) for i in range(2)]
    xf_sb = [S0(nc.sbuf_tensor(f"xf{i}", [128, 4 * 512], F32)) for i in range(2)]
    xsq_sb = S0(nc.sbuf_tensor("xsq_sb", [128, 2 * 512], BF16))
    xc_sb = S0(nc.sbuf_tensor("xc_sb", [128, 8 * 512], BF16))
    rb_sb = [S0(nc.sbuf_tensor(f"rb{i}", [128, 512], BF16)) for i in range(2)]
    kvsq_sb = S0(nc.sbuf_tensor("kvsq_sb", [128, 2 * 512], BF16))
    ktab_sb = [S0(nc.sbuf_tensor(f"ktab{i}", [128, 1024], BF16)) for i in range(2)]
    rot_scr = S0(nc.sbuf_tensor("rot_scr", [128, 512], BF16))
    rot_scr2 = S0(nc.sbuf_tensor("rot_scr2", [128, 512], BF16))
    ps_bc = S0(nc.psum_tensor("ps_bc", [128, 1024], F32))
    ps_kv = S0(nc.psum_tensor("ps_kv", [128, 1024], F32))
    ps_mm2 = S0(nc.psum_tensor("ps_mm2", [128, 1024], F32))
    ps_misc = S0(nc.psum_tensor("ps_misc", [128, 512], F32))
    ps_rot = S0(nc.psum_tensor("ps_rot", [128, 512], F32))
    ps_ln = ps_misc
    ps_kr = ps_mm2
    ps_qc = ps_mm2
    ps_ss = ps_misc

    r_wkv, r_wq, r_wkr = Reg("wkv"), Reg("wq"), Reg("wkr")
    r_xg = [Reg("xg0"), Reg("xg1")]
    r_xf = [Reg("xf0"), Reg("xf1")]
    r_xsq = [Reg("xsq0"), Reg("xsq1")]
    r_xc = Reg("xc")
    r_xcc = [Reg(f"xcc{i}") for i in range(8)]
    r_rb = [Reg("rb0"), Reg("rb1")]
    r_kvsq = Reg("kvsq")
    r_ktab = [Reg("kt0"), Reg("kt1")]
    r_rsc, r_rsc2 = Reg("rsc"), Reg("rsc2")
    r_psln, r_psbc0, r_psbc1 = Reg("psmisc"), Reg("psbc0"), Reg("psbc1")
    r_pskv, r_psrot = Reg("pskv"), Reg("psrot")
    r_psss = r_psln
    r_mm2 = [Reg("mm2_0"), Reg("mm2_1")]
    r_pskr = r_mm2
    r_psqc = r_mm2

    def wload(eng, dst, src, cols, kchunks, reg, tag):
        wstage, r_wstage = wst["t"], wst["r"]
        per = max(1, 2048 // cols)
        for k0 in range(0, kchunks, per):
            kn = min(per, kchunks - k0)
            sch.step("sp", lambda e, k0=k0, kn=kn: e.dma_start(
                out=bass.AP(wstage, 0, [[2048, 128], [cols, kn], [1, cols]]),
                in_=bass.AP(src.tensor, k0 * 128 * cols,
                            [[cols, 128], [128 * cols, kn], [1, cols]])),
                writes=[r_wstage], dma=True, tag=f"{tag}_{k0}")
            sch.step("dve", lambda e, k0=k0, kn=kn: e.tensor_copy(
                dst[:, k0 * cols:(k0 + kn) * cols], wstage[:, 0:kn * cols]),
                reads=[r_wstage], writes=[reg], tag=f"{tag}c_{k0}")

    wst["t"] = S0(nc.sbuf_tensor("wstage0", [128, 2048], F32))
    wst["r"] = Reg("wstage0")
    wload("pl", wkv_sb, w_kv, KVR, 8, r_wkv, "ld_wkv")
    wload("pl", wq_sb, w_q, QR, 8, r_wq, "ld_wq")
    wload("pl", wkr_sb, w_krope, QR, 8, r_wkr, "ld_wkr")

    VM, VSD, VR, VMSQ, VA, VRHO = range(6)

    def V(i):
        return vec_sb[0:1, i * 512:(i + 1) * 512]

    for tg in range(ntg):
        b = tg % 2
        c0 = tg * 512
        xg, rxg = xg_sb[b], r_xg[b]
        for hh2 in range(2):
            xfb = (2 * tg + hh2) % 2
            sch.step("sp", lambda e, c0=c0, hh2=hh2, xfb=xfb: e.dma_start(
                out=bass.AP(xf_sb[xfb], 0, [[4 * 512, 128], [512, 4], [1, 512]]),
                in_=bass.AP(xT.tensor, c0 + hh2 * 4 * 128 * T,
                            [[T, 128], [128 * T, 4], [1, 512]])),
                writes=[r_xf[xfb]], dma=True, tag=f"ld_x{tg}_{hh2}")
            sch.step("act", lambda e, xg=xg, hh2=hh2, xfb=xfb: e.activation(
                xg[:, hh2 * 2048:(hh2 + 1) * 2048], xf_sb[xfb][:, :], AF.Copy),
                reads=[r_xf[xfb]], writes=[rxg], tag=f"cx{tg}_{hh2}")
        sch.step("sp", lambda e, b=b, c0=c0: e.dma_start(
            out=ktab_sb[b][:, 0:512], in_=cosk[:, c0:c0 + 512]),
            writes=[r_ktab[b]], dma=True, tag=f"ld_ck{tg}")
        sch.step("sp", lambda e, b=b, c0=c0: e.dma_start(
            out=ktab_sb[b][:, 512:1024], in_=sink[:, c0:c0 + 512]),
            writes=[r_ktab[b]], dma=True, tag=f"ld_sk{tg}")
        if 'ln' not in parts:
            continue
        for c in range(8):
            sch.step("pe", MM(ps_ln[0:1, 0:512], onesv[:, 0:1],
                              xg[:, c * 512:(c + 1) * 512], start=(c == 0), stop=(c == 7)),
                     reads=[rxg, r_const], writes=([r_psln] if c == 0 else []),
                     tag=f"mmm{tg}_{c}")
        sch.group_end(r_psln)
        sch.step("dve", lambda e: e.tensor_scalar_mul(V(VM), ps_ln[0:1, 0:512], 1.0 / D),
                 reads=[r_psln], writes=[r_vec], tag=f"m{tg}")
        sch.step("pe", MM(ps_bc[:, 0:512], ones1[0:1, :], V(VM)),
                 reads=[r_vec, r_const], writes=[r_psbc0], tag=f"mbc{tg}")
        for c in range(8):
            sq = c % 2
            sch.step("act", lambda e, xg=xg, c=c, sq=sq: e.activation(
                xsq_sb[:, sq * 512:sq * 512 + 512], xg[:, c * 512:(c + 1) * 512], AF.Square),
                reads=[rxg], writes=[r_xsq[sq]], tag=f"sq{tg}_{c}")
            sch.step("pe", MM(ps_ln[0:1, 0:512], onesv[:, 0:1],
                              xsq_sb[:, sq * 512:sq * 512 + 512],
                              start=(c == 0), stop=(c == 7)),
                     reads=[r_xsq[sq], r_const], writes=([r_psln] if c == 0 else []),
                     tag=f"mms{tg}_{c}")
        sch.group_end(r_psln)
        sch.step("dve", lambda e: e.tensor_scalar_mul(V(VSD), ps_ln[0:1, 0:512], 1.0 / D),
                 reads=[r_psln], writes=[r_vec], tag=f"v1_{tg}")
        sch.step("dve", lambda e: e.tensor_mul(V(VA), V(VM), V(VM)),
                 reads=[r_vec], writes=[r_vec], tag=f"m2_{tg}")
        sch.step("dve", lambda e: e.tensor_sub(V(VSD), V(VSD), V(VA)),
                 reads=[r_vec], writes=[r_vec], tag=f"var{tg}")
        sch.step("act", lambda e: e.activation(V(VSD), V(VSD), AF.Ln, bias=1e-5),
                 reads=[r_vec], writes=[r_vec], tag=f"sd{tg}")
        sch.step("act", lambda e: e.activation(V(VR), V(VSD), AF.Exp, scale=-0.5),
                 reads=[r_vec], writes=[r_vec], tag=f"r{tg}")
        for c in range(8):
            sch.step("dve", lambda e, xg=xg, c=c: e.tensor_sub(
                xc_sb[:, c * 512:(c + 1) * 512], xg[:, c * 512:(c + 1) * 512],
                ps_bc[:, 0:512]),
                reads=[rxg, r_psbc0], writes=[r_xcc[c]], tag=f"cent{tg}_{c}")
        for mt in range(2 if 'kv' in parts else 0):
            for k in range(8):
                sch.step("pe", MM(ps_kv[:, mt * 512:mt * 512 + 512],
                                  wkv_sb[:, k * KVR + mt * 128:k * KVR + mt * 128 + 128],
                                  xc_sb[:, k * 512:(k + 1) * 512],
                                  start=(k == 0), stop=(k == 7)),
                         reads=[r_xcc[k], r_wkv],
                         writes=([r_pskv] if (mt == 0 and k == 0) else []),
                         tag=f"kv{tg}_{mt}_{k}")
        sch.group_end(r_pskv)
        for mt in range(2 if 'kv' in parts else 0):
            sch.step("dve", lambda e, mt=mt, c0=c0: e.tensor_copy(
                kv_sb[:, mt * T + c0:mt * T + c0 + 512], ps_kv[:, mt * 512:mt * 512 + 512]),
                reads=[r_pskv], writes=[r_kv], tag=f"kve{tg}_{mt}")
            sch.step("act", lambda e, mt=mt, c0=c0: e.activation(
                kvsq_sb[:, mt * 512:mt * 512 + 512],
                kv_sb[:, mt * T + c0:mt * T + c0 + 512], AF.Square),
                reads=[r_kv], writes=[r_kvsq], tag=f"kvq{tg}_{mt}")
        sch.step("pe", MM(ps_bc[:, 512:1024], ones1[0:1, :], V(VR)),
                 reads=[r_vec, r_const], writes=[r_psbc1], tag=f"rbc{tg}")
        sch.step("dve", lambda e, b=b: e.tensor_copy(rb_sb[b][:, :], ps_bc[:, 512:1024]),
                 reads=[r_psbc1], writes=[r_rb[b]], tag=f"rbe{tg}")
        for k in range(2 if 'kv' in parts else 0):
            sch.step("pe", MM(ps_ss[0:1, 0:512], onesv[:, 0:1],
                              kvsq_sb[:, k * 512:(k + 1) * 512], start=(k == 0), stop=(k == 1)),
                     reads=[r_kvsq, r_const], writes=([r_psss] if k == 0 else []),
                     tag=f"ss{tg}_{k}")
        sch.group_end(r_psss)
        if 'kv' not in parts:
            continue
        sch.step("dve", lambda e: e.tensor_scalar_mul(V(VMSQ), ps_ss[0:1, 0:512], 1.0 / KVR),
                 reads=[r_psss], writes=[r_vec], tag=f"msq{tg}")
        sch.step("dve", lambda e: e.tensor_mul(V(VA), V(VR), V(VR)),
                 reads=[r_vec], writes=[r_vec], tag=f"r2{tg}")
        sch.step("dve", lambda e: e.tensor_mul(V(VA), V(VA), V(VMSQ)),
                 reads=[r_vec], writes=[r_vec], tag=f"r2m{tg}")
        sch.step("act", lambda e: e.activation(V(VA), V(VA), AF.Ln, bias=1e-6),
                 reads=[r_vec], writes=[r_vec], tag=f"sqr{tg}")
        sch.step("act", lambda e: e.activation(V(VRHO), V(VA), AF.Exp, scale=-0.5),
                 reads=[r_vec], writes=[r_vec], tag=f"rec{tg}")
        sch.step("dve", lambda e: e.tensor_mul(V(VRHO), V(VRHO), V(VR)),
                 reads=[r_vec], writes=[r_vec], tag=f"rho{tg}")
        for mt in range(4 if 'qc' in parts else 0):
            sl = mt % 2
            rp = r_psqc[sl]
            for k in range(8):
                sch.step("pe", MM(ps_qc[:, sl * 512:sl * 512 + 512],
                                  wq_sb[:, k * QR + mt * 128:k * QR + mt * 128 + 128],
                                  xc_sb[:, k * 512:(k + 1) * 512],
                                  start=(k == 0), stop=(k == 7)),
                         reads=[r_xcc[k], r_wq], writes=([rp] if k == 0 else []),
                         tag=f"qc{tg}_{mt}_{k}")
            sch.group_end(rp)
            sch.step("dve", lambda e, mt=mt, sl=sl, b=b, c0=c0: e.tensor_mul(
                qc_sb[:, mt * T + c0:mt * T + c0 + 512],
                ps_qc[:, sl * 512:sl * 512 + 512], rb_sb[b][:, :]),
                reads=[rp, r_rb[b]], writes=[r_qc], tag=f"qce{tg}_{mt}")
        sch.step("pe", MM(ps_bc[:, 512:1024], ones1[0:1, :], V(VRHO)),
                 reads=[r_vec, r_const], writes=[r_psbc1], tag=f"rhobc{tg}")
        for mt in range(2):
            sch.step("dve", lambda e, mt=mt, c0=c0: e.tensor_mul(
                kv_sb[:, mt * T + c0:mt * T + c0 + 512],
                kv_sb[:, mt * T + c0:mt * T + c0 + 512], ps_bc[:, 512:1024]),
                reads=[r_kv, r_psbc1], writes=[r_kv], tag=f"kvn{tg}_{mt}")
        for g in range(4 if 'kr' in parts else 0):
            sl = g % 2
            rp = r_pskr[sl]
            for k in range(8):
                sch.step("pe", MM(ps_kr[:, sl * 512:sl * 512 + 512],
                                  wkr_sb[:, k * QR + g * 128:k * QR + g * 128 + 128],
                                  xc_sb[:, k * 512:(k + 1) * 512],
                                  start=(k == 0), stop=(k == 7)),
                         reads=[r_xcc[k], r_wkr], writes=([rp] if k == 0 else []),
                         tag=f"kr{tg}_{g}_{k}")
            sch.group_end(rp)
            sch.step("dve", lambda e, g=g, sl=sl, b=b, c0=c0: e.tensor_mul(
                kr_sb[:, g * T + c0:g * T + c0 + 512],
                ps_kr[:, sl * 512:sl * 512 + 512], rb_sb[b][:, :]),
                reads=[rp, r_rb[b]], writes=[r_kr], tag=f"kre{tg}_{g}")
            if 'rope' not in parts:
                continue
            sch.step("pe", MM(ps_rot[:, 0:512], r4t_sb[:, :],
                              kr_sb[:, g * T + c0:g * T + c0 + 512]),
                     reads=[r_kr, r_const], writes=[r_psrot], tag=f"rot{tg}_{g}")
            sch.step("dve", lambda e, g=g, b=b, c0=c0: e.tensor_mul(
                rot_scr[:, :], kr_sb[:, g * T + c0:g * T + c0 + 512], ktab_sb[b][:, 0:512]),
                reads=[r_kr, r_ktab[b]], writes=[r_rsc], tag=f"t1_{tg}_{g}")
            sch.step("dve", lambda e, b=b: e.tensor_mul(
                rot_scr2[:, :], ps_rot[:, 0:512], ktab_sb[b][:, 512:1024]),
                reads=[r_psrot, r_ktab[b]], writes=[r_rsc2], tag=f"u_{tg}_{g}")
            sch.step("dve", lambda e, g=g, c0=c0: e.tensor_add(
                kr_sb[:, g * T + c0:g * T + c0 + 512], rot_scr[:, :], rot_scr2[:, :]),
                reads=[r_rsc, r_rsc2], writes=[r_kr], tag=f"kh_{tg}_{g}")
    sch.flush()

    for mt in range(4):
        sch.step("sp", lambda e, mt=mt: e.dma_start(
            out=qc_dram[mt * 128:(mt + 1) * 128, :], in_=qc_sb[:, mt * T:(mt + 1) * T]),
            reads=[r_qc], dma=True, tag=f"qcspill{mt}")
    r_qcd = Reg("qcd")
    r_qcd.w = sch.steps[-1]

    t_kvn = tap("kv_norm", [2 * 128, T], BF16)
    t_qc = tap("q_c", [4 * 128, T], BF16)
    t_khat = tap("khat", [4 * 128, T], BF16)
    for name, tt, src_sb, reg, n in (("kv_norm", t_kvn, kv_sb, r_kv, 2),
                                     ("q_c", t_qc, qc_sb, r_qc, 4),
                                     ("khat", t_khat, kr_sb, r_kr, 4)):
        if tt is not None:
            for mt in range(n):
                sch.step("sp", lambda e, mt=mt, tt=tt, src_sb=src_sb: e.dma_start(
                    out=tt[mt * 128:(mt + 1) * 128, :], in_=src_sb[:, mt * T:(mt + 1) * T]),
                    reads=[reg], dma=True, tag=f"tap_{name}{mt}")
    sch.flush()
    es0.close()
    es_mid = ExitStack()
    attn_all = es_mid.enter_context(nc.sbuf_tensor("attn_all", [128, 8 * TOWN], BF16))
    wo_sb = es_mid.enter_context(nc.sbuf_tensor("wo_sb", [128, 8 * D], BF16))
    if stop_after < 3:
        es_mid.close()
        esL.close()
        es.close()
        return nc, tap_t

    es3 = ExitStack()
    S3 = es3.enter_context
    wkd_sb = S3(nc.sbuf_tensor("wkd_sb", [128, 2 * D], BF16))
    wvd_sb = S3(nc.sbuf_tensor("wvd_sb", [128, 2 * D], BF16))
    wqd_sb = S3(nc.sbuf_tensor("wqd_sb", [128, 4 * D], BF16))
    wqr_sb = S3(nc.sbuf_tensor("wqr_sb", [128, 4 * QR], BF16))
    r_wkd, r_wvd, r_wqd, r_wqr = Reg("wkd"), Reg("wvd"), Reg("wqd"), Reg("wqr")

    es3w = ExitStack()
    wst["t"] = es3w.enter_context(nc.sbuf_tensor("wstage3", [128, 2048], F32))
    wst["r"] = Reg("wstage3")
    wload("pl", wkd_sb, w_kdec, D, 2, r_wkd, "ld_wkd")
    wload("pl", wvd_sb, w_vdec, D, 2, r_wvd, "ld_wvd")
    wload("pl", wqd_sb, w_qdec, D, 4, r_wqd, "ld_wqd")
    wload("pl", wqr_sb, w_qrope, QR, 4, r_wqr, "ld_wqr")
    wload("pl", wo_sb, w_o, D, 8, r_wo, "ld_wo")
    for w_sbuf, reg in ((wkd_sb, r_wkd), (wvd_sb, r_wvd)):
        for k in range(2):
            sch.step("dve", lambda e, w_sbuf=w_sbuf, k=k: e.tensor_scalar_mul(
                w_sbuf[:, k * D:(k + 1) * D], w_sbuf[:, k * D:(k + 1) * D],
                gkv_sb[:, k:k + 1]),
                reads=[reg, r_gkv], writes=[reg], tag=f"gfold{k}")
    for k in range(4):
        sch.step("dve", lambda e, k=k: e.tensor_scalar_mul(
            wqd_sb[:, k * D:(k + 1) * D], wqd_sb[:, k * D:(k + 1) * D], SCALE),
            reads=[r_wqd], writes=[r_wqd], tag=f"sfold{k}")
    sch.flush()
    es3w.close()
    qhat_sb = S3(nc.sbuf_tensor("qhat_sb", [128, 4 * TOWN], BF16))
    qco_sb = S3(nc.sbuf_tensor("qco_sb", [128, 4 * TOWN], BF16))
    kbuf_sb = [S3(nc.sbuf_tensor(f"kbuf{i}", [96, T], BF16)) for i in range(2)]
    qbuf_sb = [S3(nc.sbuf_tensor(f"qbuf{i}", [96, TOWN], BF16)) for i in range(2)]
    v_sb = S3(nc.sbuf_tensor("v_sb", [128, 32 * 130], BF16))
    p_sb = [S3(nc.sbuf_tensor(f"p{i}", [128, 1024], BF16)) for i in range(2)]
    nrm_sb = S3(nc.sbuf_tensor("nrm_sb", [64, 512], F32))
    aschat = S3(nc.sbuf_tensor("aschat", [64, 512], BF16))
    qtab_sb = [S3(nc.sbuf_tensor(f"qtab{i}", [128, 1024], BF16)) for i in range(2)]
    rsc3 = S3(nc.sbuf_tensor("rsc3", [128, 512], BF16))
    rsc4 = S3(nc.sbuf_tensor("rsc4", [128, 512], BF16))
    recv_sb = S3(nc.sbuf_tensor("recv_sb", [1, 512], F32))
    ps_sc = [S3(nc.psum_tensor(f"ps_sc{i}", [128, 1024], F32)) for i in range(2)]
    ps_pv = S3(nc.psum_tensor("ps_pv", [65, 1024], F32))
    ps_decT = S3(nc.psum_tensor("ps_decT", [128, 1024], F32))
    r_qhat, r_qtab = Reg("qhat"), Reg("qtab")
    r_kbuf = [Reg("kb0"), Reg("kb1")]
    r_qbuf = [Reg("qb0"), Reg("qb1")]
    r_v = Reg("v")
    r_p = [Reg("p0"), Reg("p1")]
    r_nrm, r_aschat, r_recv = Reg("nrm"), Reg("aschat"), Reg("recv")
    r_rsc3, r_rsc4 = Reg("rsc3"), Reg("rsc4")
    r_pssc = [Reg("pssc0"), Reg("pssc1")]
    r_pspv = [Reg("pspv0"), Reg("pspv1")]
    r_psd = [Reg("psd0"), Reg("psd1")]
    r_qtabs = [Reg("qt0"), Reg("qt1")]

    snap0 = sch.snapshot()
    end_counts = []
    pid = nc.partition_id()
    engs = [nc.tensor, nc.vector, nc.scalar, nc.gpsimd, nc.sync]
    idx = nc.snap(pid % 2)
    for var in nc.Switch(engines=engs, index=[idx] * 5, n=2):
        sch.set_counts(snap0)
        chunks = CHUNKS[var]
        own0 = [q0 for q0, _ in chunks]
        r_qco = Reg("qco")

        def dslot(sl):
            return ps_decT[:, sl * 512:sl * 512 + 512]
        for mt in range(4):
            for cc in range(4):
                sch.step("sp", lambda e, mt=mt, cc=cc, q0=own0[cc]: e.dma_start(
                    out=qco_sb[:, mt * TOWN + cc * 512:mt * TOWN + cc * 512 + 512],
                    in_=qc_dram[mt * 128:(mt + 1) * 128, q0:q0 + 512]),
                    reads=[r_qcd], writes=[r_qco], dma=True, tag=f"qcl{var}_{mt}_{cc}")

        for cc in range(4):
            tb = cc % 2
            q0 = own0[cc]
            sch.step("sp", lambda e, tb=tb, q0=q0, cc=cc: e.dma_start(
                out=qtab_sb[tb][:, 0:512], in_=cosqo[:, cc * 512:cc * 512 + 512]),
                writes=[r_qtabs[tb]], dma=True, tag=f"qtc{var}_{cc}")
            sch.step("sp", lambda e, tb=tb, cc=cc: e.dma_start(
                out=qtab_sb[tb][:, 512:1024], in_=sinqo[:, cc * 512:cc * 512 + 512]),
                writes=[r_qtabs[tb]], dma=True, tag=f"qts{var}_{cc}")
            for g in range(4):
                sl = g % 2
                rp = r_psd[sl]
                for k in range(4):
                    sch.step("pe", MM(dslot(sl),
                                      wqr_sb[:, k * QR + g * 128:k * QR + g * 128 + 128],
                                      qco_sb[:, k * TOWN + cc * 512:k * TOWN + cc * 512 + 512],
                                      start=(k == 0), stop=(k == 3)),
                             reads=[r_qco, r_wqr], writes=([rp] if k == 0 else []),
                             tag=f"qr{var}_{g}_{cc}")
                sch.group_end(rp)
                sch.step("dve", lambda e, g=g, cc=cc, sl=sl: e.tensor_copy(
                    qhat_sb[:, g * TOWN + cc * 512:g * TOWN + cc * 512 + 512], dslot(sl)),
                    reads=[rp], writes=[r_qhat], tag=f"qre{var}_{g}_{cc}")
                sch.step("pe", MM(dslot(sl), r4t_sb[:, :],
                                  qhat_sb[:, g * TOWN + cc * 512:g * TOWN + cc * 512 + 512]),
                         reads=[r_qhat, r_const], writes=[rp], tag=f"qrot{var}_{g}_{cc}")
                sch.step("dve", lambda e, g=g, cc=cc, tb=tb: e.tensor_mul(
                    rsc3[:, :],
                    qhat_sb[:, g * TOWN + cc * 512:g * TOWN + cc * 512 + 512],
                    qtab_sb[tb][:, 0:512]),
                    reads=[r_qhat, r_qtabs[tb]], writes=[r_rsc3], tag=f"qt1{var}_{g}_{cc}")
                sch.step("dve", lambda e, sl=sl, tb=tb: e.tensor_mul(
                    rsc4[:, :], dslot(sl), qtab_sb[tb][:, 512:1024]),
                    reads=[rp, r_qtabs[tb]], writes=[r_rsc4], tag=f"qu{var}_{g}_{cc}")
                sch.step("dve", lambda e, g=g, cc=cc: e.tensor_add(
                    qhat_sb[:, g * TOWN + cc * 512:g * TOWN + cc * 512 + 512],
                    rsc3[:, :], rsc4[:, :]),
                    reads=[r_rsc3, r_rsc4], writes=[r_qhat], tag=f"qh{var}_{g}_{cc}")

        def decode_head_gen(h, slot):
            kb_, qb_ = kbuf_sb[slot], qbuf_sb[slot]
            rkb, rqb = r_kbuf[slot], r_qbuf[slot]
            for nc8 in range(8):
                sl = nc8 % 2
                for k in range(2):
                    sch.step("pe", MM(dslot(sl)[0:64, :],
                                      wkd_sb[:, k * D + h * 64:k * D + h * 64 + 64],
                                      kv_sb[:, k * T + nc8 * 512:k * T + nc8 * 512 + 512],
                                      start=(k == 0), stop=(k == 1)),
                             reads=[r_kv, r_wkd],
                             writes=([r_psd[sl]] if k == 0 else []), tag=f"kd{var}_{h}_{nc8}")
                sch.group_end(r_psd[sl])
                sch.step("dve", lambda e, nc8=nc8, kb_=kb_, sl=sl: e.tensor_copy(
                    kb_[0:64, nc8 * 512:nc8 * 512 + 512], dslot(sl)[0:64, :]),
                    reads=[r_psd[sl]], writes=[rkb], tag=f"kde{var}_{h}_{nc8}")
                yield
            G, gsl = h // 4, h % 4
            if gsl == 2:
                sch.step("dve", lambda e, G=G: e.tensor_copy(
                    kb_[64:96, :], kr_sb[64:96, G * T:(G + 1) * T]),
                    reads=[r_kr], writes=[rkb], tag=f"kcp{var}_{h}")
                yield
            else:
                ci = {0: 0, 1: 1, 3: 2}[gsl]
                for nc8 in range(8):
                    sl = nc8 % 2
                    sch.step("pe", MM(dslot(sl)[64:96, :],
                                      psel_sb[:, ci * 32:(ci + 1) * 32],
                                      kr_sb[:, G * T + nc8 * 512:G * T + nc8 * 512 + 512],
                                      tp=(0, 64)),
                             reads=[r_kr, r_const], writes=[r_psd[sl]], tag=f"kpm{var}_{h}_{nc8}")
                    sch.step("dve", lambda e, nc8=nc8, kb_=kb_, sl=sl: e.tensor_copy(
                        kb_[64:96, nc8 * 512:nc8 * 512 + 512], dslot(sl)[64:96, :]),
                        reads=[r_psd[sl]], writes=[rkb], tag=f"kpe{var}_{h}_{nc8}")
                    yield
            for cc in range(4):
                sl = cc % 2
                for k in range(4):
                    sch.step("pe", MM(dslot(sl)[0:64, :],
                                      wqd_sb[:, k * D + h * 64:k * D + h * 64 + 64],
                                      qco_sb[:, k * TOWN + cc * 512:k * TOWN + cc * 512 + 512],
                                      start=(k == 0), stop=(k == 3)),
                             reads=[r_qco, r_wqd],
                             writes=([r_psd[sl]] if k == 0 else []), tag=f"qd{var}_{h}_{cc}")
                sch.group_end(r_psd[sl])
                sch.step("dve", lambda e, cc=cc, qb_=qb_, sl=sl: e.tensor_copy(
                    qb_[0:64, cc * 512:cc * 512 + 512], dslot(sl)[0:64, :]),
                    reads=[r_psd[sl]], writes=[rqb], tag=f"qde{var}_{h}_{cc}")
                yield
            if gsl == 2:
                sch.step("dve", lambda e, G=G, qb_=qb_: e.tensor_copy(
                    qb_[64:96, :], qhat_sb[64:96, G * TOWN:(G + 1) * TOWN]),
                    reads=[r_qhat], writes=[rqb], tag=f"qcp{var}_{h}")
                yield
            else:
                ci = {0: 0, 1: 1, 3: 2}[gsl]
                for cc in range(4):
                    sl = cc % 2
                    sch.step("pe", MM(dslot(sl)[64:96, :],
                                      psel_sb[:, ci * 32:(ci + 1) * 32],
                                      qhat_sb[:, G * TOWN + cc * 512:G * TOWN + cc * 512 + 512],
                                      tp=(0, 64)),
                             reads=[r_qhat, r_const], writes=[r_psd[sl]], tag=f"qpm{var}_{h}_{cc}")
                    sch.step("dve", lambda e, cc=cc, qb_=qb_, sl=sl: e.tensor_copy(
                        qb_[64:96, cc * 512:cc * 512 + 512], dslot(sl)[64:96, :]),
                        reads=[r_psd[sl]], writes=[rqb], tag=f"qpe{var}_{h}_{cc}")
                    yield

        def decode_v_gen(hg):
            for tt in range(32):
                sl = tt % 2
                for k in range(2):
                    sch.step("pe", MM(dslot(sl)[:, 0:128],
                                      kv_sb[:, k * T + tt * 128:k * T + tt * 128 + 128],
                                      wvd_sb[:, k * D + hg * 128:k * D + hg * 128 + 128],
                                      start=(k == 0), stop=(k == 1)),
                             reads=[r_kv, r_wvd],
                             writes=([r_psd[sl]] if k == 0 else []), tag=f"v{var}_{hg}_{tt}")
                sch.group_end(r_psd[sl])
                sch.step("dve", lambda e, tt=tt, sl=sl: e.tensor_copy(
                    bass.AP(v_sb, tt * 130, [[32 * 130, 128], [65, 2], [1, 64]]),
                    bass.AP(ps_decT, sl * 512, [[1024, 128], [64, 2], [1, 64]])),
                    reads=[r_psd[sl]], writes=[r_v], tag=f"ve{var}_{hg}_{tt}")
                if tt % 2 == 1:
                    yield
            sch.step("pl", lambda e: e.memset(
                bass.AP(v_sb, 64, [[32 * 130, 128], [130, 32], [65, 2], [1, 1]]), 1.0),
                writes=[r_v], tag=f"vones{var}_{hg}")

        def pump(g, k=1):
            if g is None:
                return None
            try:
                for _ in range(k):
                    next(g)
                return g
            except StopIteration:
                return None

        def drain(g):
            if g is not None:
                for _ in g:
                    pass

        for _ in decode_head_gen(0, 0):
            pass
        filler = None
        for h in range(NH):
            slot = h % 2
            kb_, qb_ = kbuf_sb[slot], qbuf_sb[slot]
            rkb, rqb = r_kbuf[slot], r_qbuf[slot]
            drain(filler)
            if h % 2 == 0:
                drain(decode_v_gen(h // 2))
            filler = decode_head_gen(h + 1, (h + 1) % 2) if h + 1 < NH else None
            hl = h % 2
            pending_norm = []
            for cc in range(4):
                while len(pending_norm) > 1:
                    pending_norm.pop(0)()
                q0c, nkt = chunks[cc]
                nb4 = (nkt + 1) // 2
                pends = []
                for bb in range(nb4):
                    if len(pends) >= 2:
                        emit_pv_q(pends.pop(0))
                    psl = bb % 2
                    psc, rsc_ = ps_sc[psl], r_pssc[psl]
                    pb, rpb = p_sb[psl], r_p[psl]
                    kts = [kt for kt in range(2 * bb, 2 * bb + 2) if kt < nkt]
                    for j, kt in enumerate(kts):
                        sch.step("pe", MM(psc[:, j * 512:j * 512 + 512],
                                          kb_[:, kt * 128:kt * 128 + 128],
                                          qb_[:, cc * 512:cc * 512 + 512]),
                                 reads=[rkb, rqb],
                                 writes=([rsc_] if j == 0 else []),
                                 tag=f"sc{var}_{h}_{cc}_{kt}")
                    sch.group_end(rsc_)
                    wid = 512 * len(kts)
                    sch.step("act", lambda e, psc=psc, pb=pb, wid=wid: e.activation(
                        pb[:, 0:wid], psc[:, 0:wid], AF.Exp, bias=EXP_BIAS),
                        reads=[rsc_], writes=[rpb], tag=f"ex{var}_{h}_{cc}_{bb}")
                    filler = pump(filler)
                    if bb == nb4 - 1 and pending_norm:
                        pending_norm.pop(0)()
                    for j, kt in enumerate(kts):
                        mj = kt - (nkt - 4)
                        if 0 <= mj < 4:
                            sch.step("pl", lambda e, pb=pb, j=j, mj=mj: e.tensor_mul(
                                pb[:, j * 512:j * 512 + 512], pb[:, j * 512:j * 512 + 512],
                                mask_sb[:, mj * 512:mj * 512 + 512]),
                                reads=[rpb, r_const], writes=[rpb],
                                tag=f"mk{var}_{h}_{cc}_{kt}")
                    pvsl = cc % 2

                    def emit_pv_q(pend, pvsl=pvsl, nkt=nkt):
                        ppsl, pkts = pend
                        for j, kt in enumerate(pkts):
                            sch.step("pe", MM(ps_pv[:, pvsl * 512:pvsl * 512 + 512],
                                              bass.AP(v_sb, kt * 130 + hl * 65,
                                                      [[32 * 130, 128], [1, 65]]),
                                              p_sb[ppsl][:, j * 512:j * 512 + 512],
                                              start=(kt == 0), stop=(kt == nkt - 1)),
                                     reads=[r_p[ppsl], r_v],
                                     writes=([r_pspv[pvsl]] if kt == 0 else []),
                                     tag=f"pv{var}_{h}_{cc}_{kt}")
                    pends.append((psl, kts))
                pvsl = cc % 2
                while pends:
                    emit_pv_q(pends.pop(0))
                sch.group_end(r_pspv[pvsl])
                def emit_norm(h=h, cc=cc, pvsl=pvsl):
                    sch.step("act", lambda e, pvsl=pvsl: e.activation(
                        recv_sb[0:1, :], ps_pv[64:65, pvsl * 512:pvsl * 512 + 512], AF.Ln),
                        reads=[r_pspv[pvsl]], writes=[r_recv], tag=f"rl{var}_{h}_{cc}")
                    sch.step("act", lambda e: e.activation(
                        recv_sb[0:1, :], recv_sb[0:1, :], AF.Exp, scale=-1.0),
                        reads=[r_recv], writes=[r_recv], tag=f"rc{var}_{h}_{cc}")
                    sch.step("pe", MM(ps_decT[0:64, 512:1024], ones1[0:1, 0:64],
                                      recv_sb[0:1, :]),
                             reads=[r_recv, r_const], writes=[r_psd[1]],
                             tag=f"nb{var}_{h}_{cc}")
                    sch.step("dve", lambda e: e.tensor_copy(nrm_sb[:, :],
                                                            ps_decT[0:64, 512:1024]),
                        reads=[r_psd[1]], writes=[r_nrm], tag=f"nbe{var}_{h}_{cc}")
                    if h % 2 == 0:
                        sch.step("dve", lambda e, h=h, cc=cc, pvsl=pvsl: e.tensor_mul(
                            attn_all[0:64,
                                     (h // 2) * TOWN + cc * 512:(h // 2) * TOWN + cc * 512 + 512],
                            ps_pv[0:64, pvsl * 512:pvsl * 512 + 512], nrm_sb[:, :]),
                            reads=[r_pspv[pvsl], r_nrm], writes=[r_attnall[h]],
                            tag=f"an{var}_{h}_{cc}")
                    else:
                        sch.step("dve", lambda e, pvsl=pvsl: e.tensor_mul(
                            aschat[:, :], ps_pv[0:64, pvsl * 512:pvsl * 512 + 512],
                            nrm_sb[:, :]),
                            reads=[r_pspv[pvsl], r_nrm], writes=[r_aschat],
                            tag=f"an{var}_{h}_{cc}")
                        sch.step("sp", lambda e, h=h, cc=cc: e.dma_start(
                            out=attn_all[64:128,
                                         (h // 2) * TOWN + cc * 512:(h // 2) * TOWN + cc * 512 + 512],
                            in_=aschat[:, :]),
                            reads=[r_aschat], writes=[r_attnall[h]], dma=True,
                            tag=f"as{var}_{h}_{cc}")
                pending_norm.append(emit_norm)
            while pending_norm:
                pending_norm.pop(0)()
        sch.flush()
        end_counts.append(sch.snapshot())
    assert end_counts[0] == end_counts[1], (
        "switch bodies must produce identical semaphore counts",
        {k: (end_counts[0][k], end_counts[1][k]) for k in end_counts[0]
         if end_counts[0][k] != end_counts[1][k]})
    sch.set_counts(end_counts[0])

    t_attn = tap("attn_pre", [128, 8 * TOWN], BF16)
    if t_attn is not None:
        sch.step("sp", lambda e: e.dma_start(out=t_attn[:, :], in_=attn_all[:, :]),
                 deps=[r for h in range(NH) for r in ([r_attnall[h].w] if r_attnall[h].w else [])],
                 dma=True, tag="tap_attn")
    sch.flush()
    es3.close()
    if stop_after < 4:
        es_mid.close()
        esL.close()
        es.close()
        return nc, tap_t

    es4 = ExitStack()
    S4 = es4.enter_context
    xo_sb = [S4(nc.sbuf_tensor(f"xo{i}", [128, 8 * 512], F32)) for i in range(2)]
    po_sb = S4(nc.sbuf_tensor("po_sb", [128, 512], F32))
    ps_wo = S4(nc.psum_tensor("ps_wo", [128, 1024], F32))
    ps_wo2 = S4(nc.psum_tensor("ps_wo2", [128, 1024], F32))
    r_xo = [Reg("xo0"), Reg("xo1")]
    r_po = Reg("po")
    r_pswo = [Reg("pswo0"), Reg("pswo1")]
    r_pswo2 = [Reg("pswo20"), Reg("pswo21")]

    for cc in range(4):
        b = cc % 2
        sch.step("sp", lambda e, cc=cc, b=b: e.dma_start(
            out=bass.AP(xo_sb[b], 0, [[8 * 512, 128], [512, 8], [1, 512]]),
            in_=bass.AP(xTo.tensor, cc * 512, [[TOWN, 128], [128 * TOWN, 8], [1, 512]])),
            writes=[r_xo[b]], dma=True, tag=f"ld_xo{cc}")
        for dchunk in range(8):
            sl = dchunk % 2
            rp, rp2 = r_pswo[sl], r_pswo2[sl]
            for j, h in enumerate(range(0, NH, 2)):
                lhs = wo_sb[0:64,
                            (h // 2) * D + dchunk * 128:(h // 2) * D + dchunk * 128 + 128]
                rhs = attn_all[0:64,
                               (h // 2) * TOWN + cc * 512:(h // 2) * TOWN + cc * 512 + 512]
                sch.step("pe", MM(ps_wo[:, sl * 512:sl * 512 + 512], lhs, rhs,
                                  start=(j == 0), stop=(j == 7)),
                         reads=[r_wo, r_attnall[h]],
                         writes=([rp] if j == 0 else []), tag=f"wo{cc}_{dchunk}_{h}")
            sch.group_end(rp)
            for j, h in enumerate(range(1, NH, 2)):
                lhs = wo_sb[64:128,
                            (h // 2) * D + dchunk * 128:(h // 2) * D + dchunk * 128 + 128]
                rhs = attn_all[64:128,
                               (h // 2) * TOWN + cc * 512:(h // 2) * TOWN + cc * 512 + 512]
                sch.step("pe", MM(ps_wo2[:, sl * 512:sl * 512 + 512], lhs, rhs,
                                  start=(j == 0), stop=(j == 7)),
                         reads=[r_wo, r_attnall[h]],
                         writes=([rp2] if j == 0 else []), tag=f"wo2{cc}_{dchunk}_{h}")
            sch.group_end(rp2)
            sch.step("dve", lambda e, dchunk=dchunk, sl=sl, b=b: e.tensor_add(
                po_sb[:, :], ps_wo[:, sl * 512:sl * 512 + 512],
                xo_sb[b][:, dchunk * 512:dchunk * 512 + 512]),
                reads=[rp, r_xo[b]], writes=[r_po], tag=f"poa{cc}_{dchunk}")
            sch.step("dve", lambda e, sl=sl: e.scalar_tensor_tensor(
                po_sb[:, :], ps_wo2[:, sl * 512:sl * 512 + 512], 1.0, po_sb[:, :],
                ALU.mult, ALU.add),
                reads=[rp2, r_po], writes=[r_po], tag=f"pob{cc}_{dchunk}")
            sch.step("sp", lambda e, dchunk=dchunk, cc=cc: e.dma_start(
                out=outT[dchunk * 128:(dchunk + 1) * 128, cc * 512:cc * 512 + 512],
                in_=po_sb[:, :]),
                reads=[r_po], dma=True, tag=f"st_p{cc}_{dchunk}")
    sch.flush()
    es4.close()
    es_mid.close()
    esL.close()
    if stop_after < 5:
        es.close()
        return nc, tap_t


    es5 = ExitStack()
    S5 = es5.enter_context
    xn_sb = S5(nc.sbuf_tensor("xn_sb", [128, 8 * TOWN], BF16))
    h_sb = S5(nc.sbuf_tensor("h_sb", [128, 24 * 1024], BF16))
    silu_sb = S5(nc.sbuf_tensor("silu_sb", [128, 1024], BF16))
    wg_sb = [S5(nc.sbuf_tensor(f"wg{i}", [128, 8 * 384], BF16)) for i in range(2)]
    wu_sb = [S5(nc.sbuf_tensor(f"wu{i}", [128, 8 * 384], BF16)) for i in range(2)]
    wd_sb = [S5(nc.sbuf_tensor(f"wd{i}", [128, 24 * 128], BF16)) for i in range(2)]
    xo5_sb = [S5(nc.sbuf_tensor(f"xo5{i}", [128, 8 * 512], BF16)) for i in range(2)]
    xf5_sb = [S5(nc.sbuf_tensor(f"xf5{i}", [128, 4 * 512], F32)) for i in range(2)]
    xsq5 = S5(nc.sbuf_tensor("xsq5", [128, 2 * 512], BF16))
    pr_sb = [S5(nc.sbuf_tensor(f"pr{i}", [128, 512], F32)) for i in range(2)]
    ob_sb = S5(nc.sbuf_tensor("ob_sb", [128, 512], F32))
    vec5 = S5(nc.sbuf_tensor("vec5", [1, 3 * 512], F32))
    ps5_bc = S5(nc.psum_tensor("ps5_bc", [128, 1024], F32))
    es5a = ExitStack()
    ps5_ln = es5a.enter_context(nc.psum_tensor("ps5_ln", [1, 1024], F32))
    r_xn, r_h, r_silu = Reg("xn"), Reg("h"), Reg("silu")
    r_wg = [Reg("wg0"), Reg("wg1")]
    r_wu = [Reg("wu0"), Reg("wu1")]
    r_wd = [Reg("wd0"), Reg("wd1")]
    r_xo5 = [Reg("xo50"), Reg("xo51")]
    r_xf5 = [Reg("xf50"), Reg("xf51")]
    r_xsq5 = [Reg("xsq50"), Reg("xsq51")]
    r_pr = [Reg("pr0"), Reg("pr1")]
    r_ob, r_vec5 = Reg("ob"), Reg("vec5")
    r_ps5ln, r_ps5bc0, r_ps5bc1 = Reg("ps5ln"), Reg("ps5bc0"), Reg("ps5bc1")
    r_ps5g, r_ps5u = Reg("ps5g"), Reg("ps5u")
    r_ps5d = [Reg("ps5d0"), Reg("ps5d1")]

    def V5(i):
        return vec5[0:1, i * 512:(i + 1) * 512]

    for cc in range(4):
        b = cc % 2
        for hh2 in range(2):
            xfb = (2 * cc + hh2) % 2
            sch.step("sp", lambda e, cc=cc, hh2=hh2, xfb=xfb: e.dma_start(
                out=bass.AP(xf5_sb[xfb], 0, [[4 * 512, 128], [512, 4], [1, 512]]),
                in_=bass.AP(xTo.tensor, cc * 512 + hh2 * 4 * 128 * TOWN,
                            [[TOWN, 128], [128 * TOWN, 4], [1, 512]])),
                writes=[r_xf5[xfb]], dma=True, tag=f"l5x{cc}_{hh2}")
            sch.step("act", lambda e, b=b, hh2=hh2, xfb=xfb: e.activation(
                xo5_sb[b][:, hh2 * 2048:(hh2 + 1) * 2048], xf5_sb[xfb][:, :], AF.Copy),
                reads=[r_xf5[xfb]], writes=[r_xo5[b]], tag=f"cx5{cc}_{hh2}")
        for c in range(8):
            sch.step("pe", MM(ps5_ln[0:1, 0:512], onesv[:, 0:1],
                              xo5_sb[b][:, c * 512:(c + 1) * 512],
                              start=(c == 0), stop=(c == 7)),
                     reads=[r_xo5[b], r_const], writes=([r_ps5ln] if c == 0 else []),
                     tag=f"l5m{cc}_{c}")
        sch.group_end(r_ps5ln)
        sch.step("dve", lambda e: e.tensor_scalar_mul(V5(0), ps5_ln[0:1, 0:512], 1.0 / D),
                 reads=[r_ps5ln], writes=[r_vec5], tag=f"l5mm{cc}")
        sch.step("pe", MM(ps5_bc[:, 0:512], ones1[0:1, :], V5(0)),
                 reads=[r_vec5, r_const], writes=[r_ps5bc0], tag=f"l5bc{cc}")
        for c in range(8):
            sq = c % 2
            sch.step("act", lambda e, b=b, c=c, sq=sq: e.activation(
                xsq5[:, sq * 512:sq * 512 + 512], xo5_sb[b][:, c * 512:(c + 1) * 512],
                AF.Square),
                reads=[r_xo5[b]], writes=[r_xsq5[sq]], tag=f"l5s{cc}_{c}")
            sch.step("pe", MM(ps5_ln[0:1, 512:1024], onesv[:, 0:1],
                              xsq5[:, sq * 512:sq * 512 + 512],
                              start=(c == 0), stop=(c == 7)),
                     reads=[r_xsq5[sq], r_const], writes=([r_ps5ln] if c == 0 else []),
                     tag=f"l5q{cc}_{c}")
        sch.group_end(r_ps5ln)
        sch.step("dve", lambda e: e.tensor_scalar_mul(V5(1), ps5_ln[0:1, 512:1024], 1.0 / D),
                 reads=[r_ps5ln], writes=[r_vec5], tag=f"l5v{cc}")
        sch.step("dve", lambda e: e.tensor_mul(V5(2), V5(0), V5(0)),
                 reads=[r_vec5], writes=[r_vec5], tag=f"l5m2{cc}")
        sch.step("dve", lambda e: e.tensor_sub(V5(1), V5(1), V5(2)),
                 reads=[r_vec5], writes=[r_vec5], tag=f"l5var{cc}")
        sch.step("act", lambda e: e.activation(V5(1), V5(1), AF.Ln, bias=1e-5),
                 reads=[r_vec5], writes=[r_vec5], tag=f"l5sd{cc}")
        sch.step("act", lambda e: e.activation(V5(2), V5(1), AF.Exp, scale=-0.5),
                 reads=[r_vec5], writes=[r_vec5], tag=f"l5r{cc}")
        sch.step("pe", MM(ps5_bc[:, 512:1024], ones1[0:1, :], V5(2)),
                 reads=[r_vec5, r_const], writes=[r_ps5bc1], tag=f"l5rb{cc}")
        sch.step("dve", lambda e, b=b, cc=cc: e.tensor_sub(
            bass.AP(xn_sb, cc * 512, [[8 * TOWN, 128], [TOWN, 8], [1, 512]]),
            bass.AP(xo5_sb[b], 0, [[8 * 512, 128], [512, 8], [1, 512]]),
            bass.AP(ps5_bc, 0, [[1024, 128], [0, 8], [1, 512]])),
            reads=[r_xo5[b], r_ps5bc0], writes=[r_xn], tag=f"l5c{cc}")
        sch.step("dve", lambda e, cc=cc: e.tensor_mul(
            bass.AP(xn_sb, cc * 512, [[8 * TOWN, 128], [TOWN, 8], [1, 512]]),
            bass.AP(xn_sb, cc * 512, [[8 * TOWN, 128], [TOWN, 8], [1, 512]]),
            bass.AP(ps5_bc, 512, [[1024, 128], [0, 8], [1, 512]])),
            reads=[r_xn, r_ps5bc1], writes=[r_xn], tag=f"l5n{cc}")
    sch.flush()

    es5a.close()
    ps5_g = S5(nc.psum_tensor("ps5_g", [128, 1024], F32))
    ps5_u = S5(nc.psum_tensor("ps5_u", [128, 1024], F32))
    ps5_d = S5(nc.psum_tensor("ps5_d", [128, 1024], F32))
    r_outT = Reg("outT")
    last_store = None
    for tw in range(2):
        for hb in range(8):
            wbuf = hb % 2
            sch.step("pl", lambda e, hb=hb, wbuf=wbuf: e.dma_start(
                out=bass.AP(wg_sb[wbuf], 0, [[8 * 384, 128], [384, 8], [1, 384]]),
                in_=bass.AP(wgT.tensor, hb * 384, [[HID, 128], [128 * HID, 8], [1, 384]])),
                writes=[r_wg[wbuf]], dma=True, tag=f"f_wg{tw}_{hb}")
            sch.step("pl", lambda e, hb=hb, wbuf=wbuf: e.dma_start(
                out=bass.AP(wu_sb[wbuf], 0, [[8 * 384, 128], [384, 8], [1, 384]]),
                in_=bass.AP(wuT.tensor, hb * 384, [[HID, 128], [128 * HID, 8], [1, 384]])),
                writes=[r_wu[wbuf]], dma=True, tag=f"f_wu{tw}_{hb}")
            for j in range(3):
                hh = 3 * hb + j
                for nn in range(2):
                    for k in range(8):
                        sch.step("pe", MM(ps5_g[:, nn * 512:nn * 512 + 512],
                                          wg_sb[wbuf][:, k * 384 + j * 128:k * 384 + j * 128 + 128],
                                          xn_sb[:, k * TOWN + tw * 1024 + nn * 512:
                                                k * TOWN + tw * 1024 + nn * 512 + 512],
                                          start=(k == 0), stop=(k == 7)),
                                 reads=[r_wg[wbuf], r_xn],
                                 writes=([r_ps5g] if (nn == 0 and k == 0) else []),
                                 tag=f"fg{tw}_{hh}_{nn}_{k}")
                sch.group_end(r_ps5g)
                sch.step("act", lambda e: e.activation(
                    silu_sb[:, :], ps5_g[:, :], AF.Silu, scale=FFS),
                    reads=[r_ps5g], writes=[r_silu], tag=f"fs{tw}_{hh}")
                for nn in range(2):
                    for k in range(8):
                        sch.step("pe", MM(ps5_u[:, nn * 512:nn * 512 + 512],
                                          wu_sb[wbuf][:, k * 384 + j * 128:k * 384 + j * 128 + 128],
                                          xn_sb[:, k * TOWN + tw * 1024 + nn * 512:
                                                k * TOWN + tw * 1024 + nn * 512 + 512],
                                          start=(k == 0), stop=(k == 7)),
                                 reads=[r_wu[wbuf], r_xn],
                                 writes=([r_ps5u] if (nn == 0 and k == 0) else []),
                                 tag=f"fu{tw}_{hh}_{nn}_{k}")
                sch.group_end(r_ps5u)
                sch.step("dve", lambda e, hh=hh: e.scalar_tensor_tensor(
                    h_sb[:, hh * 1024:(hh + 1) * 1024], ps5_u[:, :], FFS, silu_sb[:, :],
                    ALU.mult, ALU.mult),
                    reads=[r_ps5u, r_silu], writes=[r_h], tag=f"fh{tw}_{hh}")
        def wd_load(dchunk):
            dbuf = dchunk % 2
            sch.step("pl", lambda e, dchunk=dchunk, dbuf=dbuf: e.dma_start(
                out=bass.AP(wd_sb[dbuf], 0, [[24 * 128, 128], [128, 24], [1, 128]]),
                in_=bass.AP(wdT.tensor, dchunk * 128, [[D, 128], [128 * D, 24], [1, 128]])),
                writes=[r_wd[dbuf]], dma=True, tag=f"f_wd{tw}_{dchunk}")
        wd_load(0)
        for dchunk in range(8):
            dbuf = dchunk % 2
            if dchunk + 1 < 8:
                wd_load(dchunk + 1)
            for nn in range(2):
                sl = nn
                for k in range(24):
                    sch.step("pe", MM(ps5_d[:, sl * 512:sl * 512 + 512],
                                      wd_sb[dbuf][:, k * 128:(k + 1) * 128],
                                      h_sb[:, k * 1024 + nn * 512:k * 1024 + nn * 512 + 512],
                                      start=(k == 0), stop=(k == 23)),
                             reads=[r_wd[dbuf], r_h],
                             writes=([r_ps5d[sl]] if k == 0 else []),
                             tag=f"fd{tw}_{dchunk}_{nn}_{k}")
                sch.group_end(r_ps5d[sl])
                pbuf = nn
                sch.step("sp", lambda e, dchunk=dchunk, tw=tw, nn=nn, pbuf=pbuf: e.dma_start(
                    out=pr_sb[pbuf][:, :],
                    in_=outT[dchunk * 128:(dchunk + 1) * 128,
                             tw * 1024 + nn * 512:tw * 1024 + nn * 512 + 512]),
                    reads=[r_outT], writes=[r_pr[pbuf]], dma=True, tag=f"fp{tw}_{dchunk}_{nn}")
                sch.step("dve", lambda e, sl=sl, pbuf=pbuf: e.scalar_tensor_tensor(
                    ob_sb[:, :], ps5_d[:, sl * 512:sl * 512 + 512], FFS, pr_sb[pbuf][:, :],
                    ALU.mult, ALU.add),
                    reads=[r_ps5d[sl], r_pr[pbuf]], writes=[r_ob], tag=f"fo{tw}_{dchunk}_{nn}")
                last_store = sch.step("sp", lambda e, dchunk=dchunk, tw=tw, nn=nn: e.dma_start(
                    out=outT[dchunk * 128:(dchunk + 1) * 128,
                             tw * 1024 + nn * 512:tw * 1024 + nn * 512 + 512],
                    in_=ob_sb[:, :]),
                    reads=[r_ob], writes=[r_outT], dma=True, tag=f"fst{tw}_{dchunk}_{nn}")
    sch.step("sp", lambda e: e.nop(), deps=[last_store], tag="final_wait")
    sch.flush()
    es5.close()
    es.close()
    return nc, tap_t


def make_core_inputs(ins, const, b, core_type):
    x = ins["L0_attn_x"][b]
    own = own_positions(core_type)
    cosq = np.ascontiguousarray(
        (np.tile(const["cos_t"], (4, 1))[:, own] * SCALE).astype(bfdt))
    sinq = np.ascontiguousarray(
        (np.tile(const["sin_t"], (4, 1))[:, own] * SCALE).astype(bfdt))
    return {
        "xT": np.ascontiguousarray(x.T),
        "xTo": np.ascontiguousarray(x.T[:, own]),
        "w_kv": ins["L0_attn_W_kv"], "w_q": ins["L0_attn_W_q"],
        "w_kdec": ins["L0_attn_W_kdec"], "w_vdec": ins["L0_attn_W_vdec"],
        "w_qdec": ins["L0_attn_W_qdec"], "w_krope": ins["L0_attn_W_krope"],
        "w_qrope": ins["L0_attn_W_qrope"], "w_o": ins["L0_attn_W_o"],
        "g_kv": ins["L0_attn_g_kv"],
        "wgT": np.ascontiguousarray(ins["L0_ffn_w_gate"].T),
        "wuT": np.ascontiguousarray(ins["L0_ffn_w_up"].T),
        "wdT": np.ascontiguousarray(ins["L0_ffn_w_down"].T),
        "cosk": const["cosk"], "sink": const["sink"],
        "cosqo": cosq, "sinqo": sinq,
        "maskc": const["maskc"], "r4t": const["r4t"], "psel": const["psel"],
    }


_NC_CACHE = {}


def kernel(**inputs):
    from concourse.bass_utils import run_bass_kernel_spmd
    if "nc" not in _NC_CACHE:
        _NC_CACHE["nc"] = build_nc()[0]
    nc = _NC_CACHE["nc"]
    const = host_constants()
    in_maps = [make_core_inputs(inputs, const, c // 2, c % 2) for c in range(8)]
    res = run_bass_kernel_spmd(nc, in_maps, core_ids=list(range(8)))
    out = np.empty((4, T, D), np.float32)
    for c in range(8):
        own = own_positions(c % 2)
        out[c // 2, own, :] = np.asarray(res.results[c]["outT"]).T
    return out


def host_constants():
    inv_freq = 1.0 / 10000.0 ** (np.arange(0, RD, 2, dtype=np.float32) / RD)
    t = np.arange(T, dtype=np.float32)
    freqs = np.outer(t, inv_freq)
    emb = np.concatenate([freqs, freqs], -1)
    cos_t = np.cos(emb).astype(np.float32).T
    sin_t = np.sin(emb).astype(np.float32).T
    cosk = np.ascontiguousarray(np.tile(cos_t, (4, 1)).astype(bfdt))
    sink = np.ascontiguousarray(np.tile(sin_t, (4, 1)).astype(bfdt))
    kk = np.arange(128)[:, None]
    qq = np.arange(512)[None, :]
    masks = np.ascontiguousarray(np.concatenate(
        [(qq - kk >= 128 * j).astype(np.float32) for j in range(4)],
        axis=1).astype(bfdt))
    R = np.zeros((32, 32), np.float32)
    for i in range(16):
        R[i, i + 16] = -1.0
        R[i + 16, i] = 1.0
    R4 = np.zeros((128, 128), np.float32)
    for g in range(4):
        R4[g * 32:(g + 1) * 32, g * 32:(g + 1) * 32] = R
    r4t = np.ascontiguousarray(R4.T.astype(bfdt))
    ps = np.zeros((128, 96), np.float32)
    for ci, g in enumerate((0, 1, 3)):
        for i in range(32):
            ps[g * 32 + i, ci * 32 + i] = 1.0
    psel = np.ascontiguousarray(ps.astype(bfdt))
    return dict(cosk=cosk, sink=sink, maskc=masks, r4t=r4t, psel=psel,
                cos_t=cos_t, sin_t=sin_t)


def own_positions(core_type):
    if core_type == 0:
        return np.concatenate([np.arange(0, 1024), np.arange(3072, 4096)])
    return np.arange(1024, 3072)


# version 13
# speedup vs baseline: 1.2997x; 1.0048x over previous
import numpy as np
import ml_dtypes
import concourse.bass as bass
import concourse.mybir as mybir
from contextlib import ExitStack

F32 = mybir.dt.float32
BF16 = mybir.dt.bfloat16
I32 = mybir.dt.int32
AF = mybir.ActivationFunctionType
ALU = mybir.AluOpType
bfdt = ml_dtypes.bfloat16

T, TOWN, D, KVR, QR, NH, HD, RD, HID = 4096, 2048, 1024, 256, 512, 16, 64, 32, 3072
NTG = T // 512
SCALE = float(1.0 / np.sqrt(96.0))
EXP_BIAS = -8.0
FFS = float(1.0 / 127.0)
CHUNKS = {0: [(0, 4), (512, 8), (3072, 28), (3584, 32)],
          1: [(1024, 12), (1536, 16), (2048, 20), (2560, 24)]}
ENGINES = ("pe", "act", "dve", "pl", "sp")


class Step:
    __slots__ = ("eng", "emit", "deps", "idx", "seq", "inc_amt", "needed", "tag",
                 "gen", "self_wait")

    def __init__(self, eng, emit, deps, idx, tag="", gen=0):
        self.eng, self.emit, self.deps, self.idx, self.tag = eng, emit, deps, idx, tag
        self.seq = None
        self.inc_amt = 0
        self.needed = False
        self.gen = gen
        self.self_wait = None


class Reg:
    __slots__ = ("w", "readers", "name")

    def __init__(self, name=""):
        self.w, self.readers, self.name = None, [], name


class Sched:
    def __init__(self, nc, es):
        self.nc = nc
        self.steps = []
        self.gen = 0
        self.sems = {}
        self.counts = {}
        for e in ENGINES:
            self.sems[e] = es.enter_context(nc.semaphore(f"sem_{e}"))
            self.sems[e + "_dma"] = es.enter_context(nc.semaphore(f"sem_{e}dma"))
            self.counts[e] = 0
            self.counts[e + "_dma"] = 0

    def step(self, eng, emit, reads=(), writes=(), deps=(), dma=False, tag=""):
        alldeps = list(deps)
        for r in reads:
            if r.w is not None:
                alldeps.append(r.w)
        for wreg in writes:
            if wreg.w is not None:
                alldeps.append(wreg.w)
            alldeps.extend(wreg.readers)
        key = eng + ("_dma" if dma else "")
        pruned, seen = [], set()
        for d in alldeps:
            if d is None or id(d) in seen:
                continue
            seen.add(id(d))
            if d.gen < self.gen:
                continue
            if d.eng == key and key in ("pe", "sp_dma", "pl_dma", "act_dma"):
                continue
            pruned.append(d)
        s = Step(key, emit, pruned, len(self.steps), tag, gen=self.gen)
        self.steps.append(s)
        for d in pruned:
            d.needed = True
        for r in reads:
            r.readers.append(s)
        for wreg in writes:
            wreg.w = s
            wreg.readers = []
        return s

    def group_end(self, reg):
        reg.w = self.steps[-1]
        reg.readers = []
        return reg.w

    def snapshot(self):
        return dict(self.counts)

    def set_counts(self, c):
        self.counts = dict(c)

    def flush(self, in_block=True):
        nc = self.nc
        per_eng = {e: [] for e in ENGINES}
        for s in self.steps:
            e = s.eng[:-4] if s.eng.endswith("_dma") else s.eng
            per_eng[e].append(s)
        for e in ENGINES:
            run = []

            def close_run(run):
                if run:
                    for s in run:
                        self.counts[s.eng] += 16
                    for s in run:
                        s.seq = self.counts[run[0].eng]
            nd = 0
            for s in per_eng[e]:
                if s.eng.endswith("_dma"):
                    if s.deps:
                        close_run(run)
                        s.self_wait = self.counts[s.eng]
                        run = [s]
                    else:
                        run.append(s)
                else:
                    close_run(run)
                    run = []
            close_run(run)
        for s in self.steps:
            if not s.eng.endswith("_dma") and s.needed:
                s.inc_amt = 1
                self.counts[s.eng] += 1
                s.seq = self.counts[s.eng]
        for s in self.steps:
            if s.eng.endswith("_dma"):
                s.inc_amt = 16
        sched = self

        def make(engname):
            def body(eng):
                cur = {}
                for s in per_eng[engname]:
                    isdma = s.eng.endswith("_dma")
                    if s.self_wait:
                        eng.wait_ge(sched.sems[s.eng], s.self_wait)
                    for d in s.deps:
                        assert d.seq is not None, f"dep of {s.tag} on {d.tag} lacks seq"
                        if not isdma and cur.get(d.eng, 0) >= d.seq:
                            continue
                        cur[d.eng] = max(cur.get(d.eng, 0), d.seq)
                        eng.wait_ge(sched.sems[d.eng], d.seq)
                    inst = s.emit(eng)
                    if s.inc_amt:
                        inst.then_inc(sched.sems[s.eng], s.inc_amt)
                dkey = engname + "_dma"
                if any(s.eng == dkey for s in per_eng[engname]):
                    eng.wait_ge(sched.sems[dkey], sched.counts[dkey])
            return body

        with nc.Block() as blk:
            for engname, attr in (("sp", "sync"), ("pl", "gpsimd"), ("pe", "tensor"),
                                  ("act", "scalar"), ("dve", "vector")):
                if per_eng[engname]:
                    getattr(blk, attr)(make(engname))
        self.steps = []
        self.gen += 1


def MM(out, lhsT, rhs, start=True, stop=True, tp=None):
    if tp is None:
        return lambda e: e.matmul(out, lhsT, rhs, start=start, stop=stop)
    return lambda e: e.matmul(out, lhsT, rhs, start=start, stop=stop, tile_position=tp)


def build_nc(taps=(), stop_after=99, ntg=NTG, parts=('ln','kv','qc','kr','rope')):
    taps = set(taps)
    nc = bass.Bass("TRN2", target_bir_lowering=False)

    xT = nc.dram_tensor("xT", [D, T], F32, kind="ExternalInput").ap()
    xTo = nc.dram_tensor("xTo", [D, TOWN], F32, kind="ExternalInput").ap()
    w_kv = nc.dram_tensor("w_kv", [D, KVR], F32, kind="ExternalInput").ap()
    w_q = nc.dram_tensor("w_q", [D, QR], F32, kind="ExternalInput").ap()
    w_kdec = nc.dram_tensor("w_kdec", [KVR, D], F32, kind="ExternalInput").ap()
    w_vdec = nc.dram_tensor("w_vdec", [KVR, D], F32, kind="ExternalInput").ap()
    w_qdec = nc.dram_tensor("w_qdec", [QR, D], F32, kind="ExternalInput").ap()
    w_krope = nc.dram_tensor("w_krope", [D, QR], F32, kind="ExternalInput").ap()
    w_qrope = nc.dram_tensor("w_qrope", [QR, QR], F32, kind="ExternalInput").ap()
    w_o = nc.dram_tensor("w_o", [D, D], F32, kind="ExternalInput").ap()
    g_kv = nc.dram_tensor("g_kv", [KVR], F32, kind="ExternalInput").ap()
    wgT = nc.dram_tensor("wgT", [D, HID], I32, kind="ExternalInput").ap()
    wuT = nc.dram_tensor("wuT", [D, HID], I32, kind="ExternalInput").ap()
    wdT = nc.dram_tensor("wdT", [HID, D], I32, kind="ExternalInput").ap()
    cosk = nc.dram_tensor("cosk", [128, T], F32, kind="ExternalInput").ap()
    sink = nc.dram_tensor("sink", [128, T], F32, kind="ExternalInput").ap()
    cosqo = nc.dram_tensor("cosqo", [128, TOWN], F32, kind="ExternalInput").ap()
    sinqo = nc.dram_tensor("sinqo", [128, TOWN], F32, kind="ExternalInput").ap()
    maskc = nc.dram_tensor("maskc", [128, 4 * 512], BF16, kind="ExternalInput").ap()
    r4t = nc.dram_tensor("r4t", [128, 128], BF16, kind="ExternalInput").ap()
    psel = nc.dram_tensor("psel", [128, 3 * 32], BF16, kind="ExternalInput").ap()
    outT = nc.dram_tensor("outT", [D, TOWN], F32, kind="ExternalOutput").ap()

    tap_t = {}

    def tap(name, shape, dtype=F32):
        if name in taps:
            tap_t[name] = nc.dram_tensor("tap_" + name, shape, dtype,
                                         kind="ExternalOutput").ap()
        return tap_t.get(name)

    es = ExitStack()
    S = es.enter_context
    sch = Sched(nc, es)

    mask_sb = S(nc.sbuf_tensor("mask_sb", [128, 4 * 512], BF16))
    r4t_sb = S(nc.sbuf_tensor("r4t_sb", [128, 128], BF16))
    psel_sb = S(nc.sbuf_tensor("psel_sb", [128, 3 * 32], BF16))
    onesv = S(nc.sbuf_tensor("onesv", [128, 1], BF16))
    ones1 = S(nc.sbuf_tensor("ones1", [1, 128], F32))
    gkv_sb = S(nc.sbuf_tensor("gkv_sb", [128, 2], F32))
    CVALS = [1e-5, 1e-6, EXP_BIAS, FFS, -0.5, -1.0]
    cst_sb = S(nc.sbuf_tensor("cst_sb", [128, len(CVALS)], F32))
    wst = {}
    r_const, r_gkv = Reg("const"), Reg("gkv")

    r_attnall = [Reg(f"at{h}") for h in range(NH)]
    r_wo = Reg("wo")

    esL = ExitStack()
    SL = esL.enter_context
    kv_sb = SL(nc.sbuf_tensor("kv_sb", [128, 2 * T], BF16))
    kr_sb = SL(nc.sbuf_tensor("kr_sb", [128, 4 * T], BF16))
    qc_dram = nc.dram_tensor("qc_scr", [4 * 128, T], BF16, kind="Internal").ap()
    r_kv, r_qc, r_kr, r_vec = Reg("kv"), Reg("qc"), Reg("kr"), Reg("vec")

    sch.step("sp", lambda e: e.dma_start(out=mask_sb[:, :], in_=maskc[:, :]),
             writes=[r_const], dma=True, tag="ld_mask")
    sch.step("sp", lambda e: e.dma_start(out=r4t_sb[:, :], in_=r4t[:, :]),
             writes=[r_const], dma=True, tag="ld_r4t")
    sch.step("sp", lambda e: e.dma_start(out=psel_sb[:, :], in_=psel[:, :]),
             writes=[r_const], dma=True, tag="ld_psel")
    sch.step("sp", lambda e: e.dma_start(
        out=bass.AP(gkv_sb, 0, [[2, 128], [1, 2]]),
        in_=bass.AP(g_kv.tensor, 0, [[1, 128], [128, 2]]),
        allow_slow_non_contiguous=True),
        writes=[r_gkv], dma=True, tag="ld_gkv")
    sch.step("pl", lambda e: e.memset(onesv[:, :], 1.0), writes=[r_const], tag="ones_v")
    sch.step("pl", lambda e: e.memset(ones1[:, :], 1.0), writes=[r_const], tag="ones_1")
    for i, v in enumerate(CVALS):
        sch.step("pl", lambda e, i=i, v=v: e.memset(cst_sb[:, i:i + 1], float(v)),
                 writes=[r_const], tag=f"cst{i}")
        nc.const_aps.register(F32, float(v), cst_sb[:, i:i + 1])

    es0 = ExitStack()
    S0 = es0.enter_context
    vec_sb = S0(nc.sbuf_tensor("vec_sb", [1, 6 * 512], F32))
    qc_sb = S0(nc.sbuf_tensor("qc_sb", [128, 4 * T], BF16))
    wkv_sb = S0(nc.sbuf_tensor("wkv_sb", [128, 8 * KVR], BF16))
    wq_sb = S0(nc.sbuf_tensor("wq_sb", [128, 8 * QR], BF16))
    wkr_sb = S0(nc.sbuf_tensor("wkr_sb", [128, 8 * QR], BF16))
    xg_sb = [S0(nc.sbuf_tensor(f"xg{i}", [128, 8 * 512], BF16)) for i in range(2)]
    xf_sb = [S0(nc.sbuf_tensor(f"xf{i}", [128, 4 * 512], F32)) for i in range(2)]
    xsq_sb = S0(nc.sbuf_tensor("xsq_sb", [128, 2 * 512], BF16))
    xc_sb = S0(nc.sbuf_tensor("xc_sb", [128, 8 * 512], BF16))
    rb_sb = [S0(nc.sbuf_tensor(f"rb{i}", [128, 512], BF16)) for i in range(2)]
    kvsq_sb = S0(nc.sbuf_tensor("kvsq_sb", [128, 2 * 512], BF16))
    ktab_sb = [S0(nc.sbuf_tensor(f"ktab{i}", [128, 512], F32)) for i in range(2)]
    stab_sb = [S0(nc.sbuf_tensor(f"stab{i}", [128, 512], F32)) for i in range(2)]
    rot_scr = S0(nc.sbuf_tensor("rot_scr", [128, 512], BF16))
    rot_scr2 = S0(nc.sbuf_tensor("rot_scr2", [128, 512], BF16))
    ps_bc = S0(nc.psum_tensor("ps_bc", [128, 1024], F32))
    ps_kv = S0(nc.psum_tensor("ps_kv", [128, 1024], F32))
    ps_mm2 = S0(nc.psum_tensor("ps_mm2", [128, 1024], F32))
    ps_misc = S0(nc.psum_tensor("ps_misc", [128, 512], F32))
    ps_rot = S0(nc.psum_tensor("ps_rot", [128, 512], F32))
    ps_ln = ps_misc
    ps_kr = ps_mm2
    ps_qc = ps_mm2
    ps_ss = ps_misc

    r_wkv, r_wq, r_wkr = Reg("wkv"), Reg("wq"), Reg("wkr")
    r_xg = [Reg("xg0"), Reg("xg1")]
    r_xf = [Reg("xf0"), Reg("xf1")]
    r_xsq = [Reg("xsq0"), Reg("xsq1")]
    r_xc = Reg("xc")
    r_xcc = [Reg(f"xcc{i}") for i in range(8)]
    r_rb = [Reg("rb0"), Reg("rb1")]
    r_kvsq = Reg("kvsq")
    r_ktab = [Reg("kt0"), Reg("kt1")]
    r_rsc, r_rsc2 = Reg("rsc"), Reg("rsc2")
    r_psln, r_psbc0, r_psbc1 = Reg("psmisc"), Reg("psbc0"), Reg("psbc1")
    r_pskv, r_psrot = Reg("pskv"), Reg("psrot")
    r_psss = r_psln
    r_mm2 = [Reg("mm2_0"), Reg("mm2_1")]
    r_pskr = r_mm2
    r_psqc = r_mm2

    def wload(eng, dst, src, cols, kchunks, reg, tag):
        wstage, r_wstage = wst["t"], wst["r"]
        per = max(1, 2048 // cols)
        for k0 in range(0, kchunks, per):
            kn = min(per, kchunks - k0)
            sch.step("sp", lambda e, k0=k0, kn=kn: e.dma_start(
                out=bass.AP(wstage, 0, [[2048, 128], [cols, kn], [1, cols]]),
                in_=bass.AP(src.tensor, k0 * 128 * cols,
                            [[cols, 128], [128 * cols, kn], [1, cols]])),
                writes=[r_wstage], dma=True, tag=f"{tag}_{k0}")
            sch.step("dve", lambda e, k0=k0, kn=kn: e.tensor_copy(
                dst[:, k0 * cols:(k0 + kn) * cols], wstage[:, 0:kn * cols]),
                reads=[r_wstage], writes=[reg], tag=f"{tag}c_{k0}")

    wst["t"] = S0(nc.sbuf_tensor("wstage0", [128, 2048], F32))
    wst["r"] = Reg("wstage0")
    wload("pl", wkv_sb, w_kv, KVR, 8, r_wkv, "ld_wkv")
    wload("pl", wq_sb, w_q, QR, 8, r_wq, "ld_wq")
    wload("pl", wkr_sb, w_krope, QR, 8, r_wkr, "ld_wkr")

    VM, VSD, VR, VMSQ, VA, VRHO = range(6)

    def V(i):
        return vec_sb[0:1, i * 512:(i + 1) * 512]

    for tg in range(ntg):
        b = tg % 2
        c0 = tg * 512
        xg, rxg = xg_sb[b], r_xg[b]
        for hh2 in range(2):
            xfb = (2 * tg + hh2) % 2
            sch.step("sp", lambda e, c0=c0, hh2=hh2, xfb=xfb: e.dma_start(
                out=bass.AP(xf_sb[xfb], 0, [[4 * 512, 128], [512, 4], [1, 512]]),
                in_=bass.AP(xT.tensor, c0 + hh2 * 4 * 128 * T,
                            [[T, 128], [128 * T, 4], [1, 512]])),
                writes=[r_xf[xfb]], dma=True, tag=f"ld_x{tg}_{hh2}")
            sch.step("act", lambda e, xg=xg, hh2=hh2, xfb=xfb: e.activation(
                xg[:, hh2 * 2048:(hh2 + 1) * 2048], xf_sb[xfb][:, :], AF.Copy),
                reads=[r_xf[xfb]], writes=[rxg], tag=f"cx{tg}_{hh2}")
        sch.step("sp", lambda e, b=b, c0=c0: e.dma_start(
            out=ktab_sb[b][:, :], in_=cosk[:, c0:c0 + 512]),
            writes=[r_ktab[b]], dma=True, tag=f"ld_ck{tg}")
        sch.step("sp", lambda e, b=b, c0=c0: e.dma_start(
            out=stab_sb[b][:, :], in_=sink[:, c0:c0 + 512]),
            writes=[r_ktab[b]], dma=True, tag=f"ld_sk{tg}")
        if 'ln' not in parts:
            continue
        for c in range(8):
            sch.step("pe", MM(ps_ln[0:1, 0:512], onesv[:, 0:1],
                              xg[:, c * 512:(c + 1) * 512], start=(c == 0), stop=(c == 7)),
                     reads=[rxg, r_const], writes=([r_psln] if c == 0 else []),
                     tag=f"mmm{tg}_{c}")
        sch.group_end(r_psln)
        sch.step("dve", lambda e: e.tensor_scalar_mul(V(VM), ps_ln[0:1, 0:512], 1.0 / D),
                 reads=[r_psln], writes=[r_vec], tag=f"m{tg}")
        sch.step("pe", MM(ps_bc[:, 0:512], ones1[0:1, :], V(VM)),
                 reads=[r_vec, r_const], writes=[r_psbc0], tag=f"mbc{tg}")
        for c in range(8):
            sq = c % 2
            sch.step("act", lambda e, xg=xg, c=c, sq=sq: e.activation(
                xsq_sb[:, sq * 512:sq * 512 + 512], xg[:, c * 512:(c + 1) * 512], AF.Square),
                reads=[rxg], writes=[r_xsq[sq]], tag=f"sq{tg}_{c}")
            sch.step("pe", MM(ps_ln[0:1, 0:512], onesv[:, 0:1],
                              xsq_sb[:, sq * 512:sq * 512 + 512],
                              start=(c == 0), stop=(c == 7)),
                     reads=[r_xsq[sq], r_const], writes=([r_psln] if c == 0 else []),
                     tag=f"mms{tg}_{c}")
        sch.group_end(r_psln)
        sch.step("dve", lambda e: e.tensor_scalar_mul(V(VSD), ps_ln[0:1, 0:512], 1.0 / D),
                 reads=[r_psln], writes=[r_vec], tag=f"v1_{tg}")
        sch.step("dve", lambda e: e.tensor_mul(V(VA), V(VM), V(VM)),
                 reads=[r_vec], writes=[r_vec], tag=f"m2_{tg}")
        sch.step("dve", lambda e: e.tensor_sub(V(VSD), V(VSD), V(VA)),
                 reads=[r_vec], writes=[r_vec], tag=f"var{tg}")
        sch.step("act", lambda e: e.activation(V(VSD), V(VSD), AF.Ln, bias=1e-5),
                 reads=[r_vec], writes=[r_vec], tag=f"sd{tg}")
        sch.step("act", lambda e: e.activation(V(VR), V(VSD), AF.Exp, scale=-0.5),
                 reads=[r_vec], writes=[r_vec], tag=f"r{tg}")
        for c in range(8):
            sch.step("dve", lambda e, xg=xg, c=c: e.tensor_sub(
                xc_sb[:, c * 512:(c + 1) * 512], xg[:, c * 512:(c + 1) * 512],
                ps_bc[:, 0:512]),
                reads=[rxg, r_psbc0], writes=[r_xcc[c]], tag=f"cent{tg}_{c}")
        for mt in range(2 if 'kv' in parts else 0):
            for k in range(8):
                sch.step("pe", MM(ps_kv[:, mt * 512:mt * 512 + 512],
                                  wkv_sb[:, k * KVR + mt * 128:k * KVR + mt * 128 + 128],
                                  xc_sb[:, k * 512:(k + 1) * 512],
                                  start=(k == 0), stop=(k == 7)),
                         reads=[r_xcc[k], r_wkv],
                         writes=([r_pskv] if (mt == 0 and k == 0) else []),
                         tag=f"kv{tg}_{mt}_{k}")
        sch.group_end(r_pskv)
        for mt in range(2 if 'kv' in parts else 0):
            sch.step("dve", lambda e, mt=mt, c0=c0: e.tensor_copy(
                kv_sb[:, mt * T + c0:mt * T + c0 + 512], ps_kv[:, mt * 512:mt * 512 + 512]),
                reads=[r_pskv], writes=[r_kv], tag=f"kve{tg}_{mt}")
            sch.step("act", lambda e, mt=mt, c0=c0: e.activation(
                kvsq_sb[:, mt * 512:mt * 512 + 512],
                kv_sb[:, mt * T + c0:mt * T + c0 + 512], AF.Square),
                reads=[r_kv], writes=[r_kvsq], tag=f"kvq{tg}_{mt}")
        sch.step("pe", MM(ps_bc[:, 512:1024], ones1[0:1, :], V(VR)),
                 reads=[r_vec, r_const], writes=[r_psbc1], tag=f"rbc{tg}")
        sch.step("dve", lambda e, b=b: e.tensor_copy(rb_sb[b][:, :], ps_bc[:, 512:1024]),
                 reads=[r_psbc1], writes=[r_rb[b]], tag=f"rbe{tg}")
        for k in range(2 if 'kv' in parts else 0):
            sch.step("pe", MM(ps_ss[0:1, 0:512], onesv[:, 0:1],
                              kvsq_sb[:, k * 512:(k + 1) * 512], start=(k == 0), stop=(k == 1)),
                     reads=[r_kvsq, r_const], writes=([r_psss] if k == 0 else []),
                     tag=f"ss{tg}_{k}")
        sch.group_end(r_psss)
        if 'kv' not in parts:
            continue
        sch.step("dve", lambda e: e.tensor_scalar_mul(V(VMSQ), ps_ss[0:1, 0:512], 1.0 / KVR),
                 reads=[r_psss], writes=[r_vec], tag=f"msq{tg}")
        sch.step("dve", lambda e: e.tensor_mul(V(VA), V(VR), V(VR)),
                 reads=[r_vec], writes=[r_vec], tag=f"r2{tg}")
        sch.step("dve", lambda e: e.tensor_mul(V(VA), V(VA), V(VMSQ)),
                 reads=[r_vec], writes=[r_vec], tag=f"r2m{tg}")
        sch.step("act", lambda e: e.activation(V(VA), V(VA), AF.Ln, bias=1e-6),
                 reads=[r_vec], writes=[r_vec], tag=f"sqr{tg}")
        sch.step("act", lambda e: e.activation(V(VRHO), V(VA), AF.Exp, scale=-0.5),
                 reads=[r_vec], writes=[r_vec], tag=f"rec{tg}")
        sch.step("dve", lambda e: e.tensor_mul(V(VRHO), V(VRHO), V(VR)),
                 reads=[r_vec], writes=[r_vec], tag=f"rho{tg}")
        for mt in range(4 if 'qc' in parts else 0):
            sl = mt % 2
            rp = r_psqc[sl]
            for k in range(8):
                sch.step("pe", MM(ps_qc[:, sl * 512:sl * 512 + 512],
                                  wq_sb[:, k * QR + mt * 128:k * QR + mt * 128 + 128],
                                  xc_sb[:, k * 512:(k + 1) * 512],
                                  start=(k == 0), stop=(k == 7)),
                         reads=[r_xcc[k], r_wq], writes=([rp] if k == 0 else []),
                         tag=f"qc{tg}_{mt}_{k}")
            sch.group_end(rp)
            sch.step("dve", lambda e, mt=mt, sl=sl, b=b, c0=c0: e.tensor_mul(
                qc_sb[:, mt * T + c0:mt * T + c0 + 512],
                ps_qc[:, sl * 512:sl * 512 + 512], rb_sb[b][:, :]),
                reads=[rp, r_rb[b]], writes=[r_qc], tag=f"qce{tg}_{mt}")
        sch.step("pe", MM(ps_bc[:, 512:1024], ones1[0:1, :], V(VRHO)),
                 reads=[r_vec, r_const], writes=[r_psbc1], tag=f"rhobc{tg}")
        for mt in range(2):
            sch.step("dve", lambda e, mt=mt, c0=c0: e.tensor_mul(
                kv_sb[:, mt * T + c0:mt * T + c0 + 512],
                kv_sb[:, mt * T + c0:mt * T + c0 + 512], ps_bc[:, 512:1024]),
                reads=[r_kv, r_psbc1], writes=[r_kv], tag=f"kvn{tg}_{mt}")
        for g in range(4 if 'kr' in parts else 0):
            sl = g % 2
            rp = r_pskr[sl]
            for k in range(8):
                sch.step("pe", MM(ps_kr[:, sl * 512:sl * 512 + 512],
                                  wkr_sb[:, k * QR + g * 128:k * QR + g * 128 + 128],
                                  xc_sb[:, k * 512:(k + 1) * 512],
                                  start=(k == 0), stop=(k == 7)),
                         reads=[r_xcc[k], r_wkr], writes=([rp] if k == 0 else []),
                         tag=f"kr{tg}_{g}_{k}")
            sch.group_end(rp)
            sch.step("dve", lambda e, g=g, sl=sl, b=b, c0=c0: e.tensor_mul(
                kr_sb[:, g * T + c0:g * T + c0 + 512],
                ps_kr[:, sl * 512:sl * 512 + 512], rb_sb[b][:, :]),
                reads=[rp, r_rb[b]], writes=[r_kr], tag=f"kre{tg}_{g}")
            if 'rope' not in parts:
                continue
            sch.step("pe", MM(ps_rot[:, 0:512], r4t_sb[:, :],
                              kr_sb[:, g * T + c0:g * T + c0 + 512]),
                     reads=[r_kr, r_const], writes=[r_psrot], tag=f"rot{tg}_{g}")
            sch.step("dve", lambda e, g=g, b=b, c0=c0: e.tensor_mul(
                rot_scr[:, :], kr_sb[:, g * T + c0:g * T + c0 + 512], ktab_sb[b][:, :]),
                reads=[r_kr, r_ktab[b]], writes=[r_rsc], tag=f"t1_{tg}_{g}")
            sch.step("dve", lambda e, b=b: e.tensor_mul(
                rot_scr2[:, :], ps_rot[:, 0:512], stab_sb[b][:, :]),
                reads=[r_psrot, r_ktab[b]], writes=[r_rsc2], tag=f"u_{tg}_{g}")
            sch.step("dve", lambda e, g=g, c0=c0: e.tensor_add(
                kr_sb[:, g * T + c0:g * T + c0 + 512], rot_scr[:, :], rot_scr2[:, :]),
                reads=[r_rsc, r_rsc2], writes=[r_kr], tag=f"kh_{tg}_{g}")
    sch.flush()

    for mt in range(4):
        sch.step("sp", lambda e, mt=mt: e.dma_start(
            out=qc_dram[mt * 128:(mt + 1) * 128, :], in_=qc_sb[:, mt * T:(mt + 1) * T]),
            reads=[r_qc], dma=True, tag=f"qcspill{mt}")
    r_qcd = Reg("qcd")
    r_qcd.w = sch.steps[-1]

    t_kvn = tap("kv_norm", [2 * 128, T], BF16)
    t_qc = tap("q_c", [4 * 128, T], BF16)
    t_khat = tap("khat", [4 * 128, T], BF16)
    for name, tt, src_sb, reg, n in (("kv_norm", t_kvn, kv_sb, r_kv, 2),
                                     ("q_c", t_qc, qc_sb, r_qc, 4),
                                     ("khat", t_khat, kr_sb, r_kr, 4)):
        if tt is not None:
            for mt in range(n):
                sch.step("sp", lambda e, mt=mt, tt=tt, src_sb=src_sb: e.dma_start(
                    out=tt[mt * 128:(mt + 1) * 128, :], in_=src_sb[:, mt * T:(mt + 1) * T]),
                    reads=[reg], dma=True, tag=f"tap_{name}{mt}")
    sch.flush()
    es0.close()
    es_mid = ExitStack()
    attn_all = es_mid.enter_context(nc.sbuf_tensor("attn_all", [128, 8 * TOWN], BF16))
    wo_sb = es_mid.enter_context(nc.sbuf_tensor("wo_sb", [128, 8 * D], BF16))
    if stop_after < 3:
        es_mid.close()
        esL.close()
        es.close()
        return nc, tap_t

    es3 = ExitStack()
    S3 = es3.enter_context
    wkd_sb = S3(nc.sbuf_tensor("wkd_sb", [128, 2 * D], BF16))
    wvd_sb = S3(nc.sbuf_tensor("wvd_sb", [128, 2 * D], BF16))
    wqd_sb = S3(nc.sbuf_tensor("wqd_sb", [128, 4 * D], BF16))
    wqr_sb = S3(nc.sbuf_tensor("wqr_sb", [128, 4 * QR], BF16))
    r_wkd, r_wvd, r_wqd, r_wqr = Reg("wkd"), Reg("wvd"), Reg("wqd"), Reg("wqr")

    es3w = ExitStack()
    wst["t"] = es3w.enter_context(nc.sbuf_tensor("wstage3", [128, 2048], F32))
    wst["r"] = Reg("wstage3")
    wload("pl", wkd_sb, w_kdec, D, 2, r_wkd, "ld_wkd")
    wload("pl", wvd_sb, w_vdec, D, 2, r_wvd, "ld_wvd")
    wload("pl", wqd_sb, w_qdec, D, 4, r_wqd, "ld_wqd")
    wload("pl", wqr_sb, w_qrope, QR, 4, r_wqr, "ld_wqr")
    wload("pl", wo_sb, w_o, D, 8, r_wo, "ld_wo")
    for w_sbuf, reg in ((wkd_sb, r_wkd), (wvd_sb, r_wvd)):
        for k in range(2):
            sch.step("dve", lambda e, w_sbuf=w_sbuf, k=k: e.tensor_scalar_mul(
                w_sbuf[:, k * D:(k + 1) * D], w_sbuf[:, k * D:(k + 1) * D],
                gkv_sb[:, k:k + 1]),
                reads=[reg, r_gkv], writes=[reg], tag=f"gfold{k}")
    for k in range(4):
        sch.step("dve", lambda e, k=k: e.tensor_scalar_mul(
            wqd_sb[:, k * D:(k + 1) * D], wqd_sb[:, k * D:(k + 1) * D], SCALE),
            reads=[r_wqd], writes=[r_wqd], tag=f"sfold{k}")
    sch.flush()
    es3w.close()
    qhat_sb = S3(nc.sbuf_tensor("qhat_sb", [128, 4 * TOWN], BF16))
    qco_sb = S3(nc.sbuf_tensor("qco_sb", [128, 4 * TOWN], BF16))
    kbuf_sb = [S3(nc.sbuf_tensor(f"kbuf{i}", [96, T], BF16)) for i in range(2)]
    qbuf_sb = [S3(nc.sbuf_tensor(f"qbuf{i}", [96, TOWN], BF16)) for i in range(2)]
    v_sb = S3(nc.sbuf_tensor("v_sb", [128, 32 * 130], BF16))
    p_sb = [S3(nc.sbuf_tensor(f"p{i}", [128, 1024], BF16)) for i in range(2)]
    nrm_sb = S3(nc.sbuf_tensor("nrm_sb", [64, 512], F32))
    aschat = S3(nc.sbuf_tensor("aschat", [64, 512], BF16))
    qtab_sb = [S3(nc.sbuf_tensor(f"qtab{i}", [128, 512], F32)) for i in range(2)]
    rsc3 = S3(nc.sbuf_tensor("rsc3", [128, 512], BF16))
    rsc4 = S3(nc.sbuf_tensor("rsc4", [128, 512], BF16))
    recv_sb = S3(nc.sbuf_tensor("recv_sb", [1, 512], F32))
    ps_sc = [S3(nc.psum_tensor(f"ps_sc{i}", [128, 1024], F32)) for i in range(2)]
    ps_pv = S3(nc.psum_tensor("ps_pv", [65, 1024], F32))
    ps_decT = S3(nc.psum_tensor("ps_decT", [128, 1024], F32))
    r_qhat, r_qtab = Reg("qhat"), Reg("qtab")
    r_kbuf = [Reg("kb0"), Reg("kb1")]
    r_qbuf = [Reg("qb0"), Reg("qb1")]
    r_v = Reg("v")
    r_p = [Reg("p0"), Reg("p1")]
    r_nrm, r_aschat, r_recv = Reg("nrm"), Reg("aschat"), Reg("recv")
    r_rsc3, r_rsc4 = Reg("rsc3"), Reg("rsc4")
    r_pssc = [Reg("pssc0"), Reg("pssc1")]
    r_pspv = [Reg("pspv0"), Reg("pspv1")]
    r_psd = [Reg("psd0"), Reg("psd1")]
    r_qtabs = [Reg("qt0"), Reg("qt1")]

    snap0 = sch.snapshot()
    end_counts = []
    pid = nc.partition_id()
    engs = [nc.tensor, nc.vector, nc.scalar, nc.gpsimd, nc.sync]
    idx = nc.snap(pid % 2)
    for var in nc.Switch(engines=engs, index=[idx] * 5, n=2):
        sch.set_counts(snap0)
        chunks = CHUNKS[var]
        own0 = [q0 for q0, _ in chunks]
        r_qco = Reg("qco")

        def dslot(sl):
            return ps_decT[:, sl * 512:sl * 512 + 512]
        for mt in range(4):
            for cc in range(4):
                sch.step("sp", lambda e, mt=mt, cc=cc, q0=own0[cc]: e.dma_start(
                    out=qco_sb[:, mt * TOWN + cc * 512:mt * TOWN + cc * 512 + 512],
                    in_=qc_dram[mt * 128:(mt + 1) * 128, q0:q0 + 512]),
                    reads=[r_qcd], writes=[r_qco], dma=True, tag=f"qcl{var}_{mt}_{cc}")

        for cc in range(4):
            tb = cc % 2
            q0 = own0[cc]
            sch.step("sp", lambda e, cc=cc: e.dma_start(
                out=qtab_sb[0][:, :], in_=cosqo[:, cc * 512:cc * 512 + 512]),
                writes=[r_qtabs[0]], dma=True, tag=f"qtc{var}_{cc}")
            sch.step("sp", lambda e, cc=cc: e.dma_start(
                out=qtab_sb[1][:, :], in_=sinqo[:, cc * 512:cc * 512 + 512]),
                writes=[r_qtabs[1]], dma=True, tag=f"qts{var}_{cc}")
            for g in range(4):
                sl = g % 2
                rp = r_psd[sl]
                for k in range(4):
                    sch.step("pe", MM(dslot(sl),
                                      wqr_sb[:, k * QR + g * 128:k * QR + g * 128 + 128],
                                      qco_sb[:, k * TOWN + cc * 512:k * TOWN + cc * 512 + 512],
                                      start=(k == 0), stop=(k == 3)),
                             reads=[r_qco, r_wqr], writes=([rp] if k == 0 else []),
                             tag=f"qr{var}_{g}_{cc}")
                sch.group_end(rp)
                sch.step("dve", lambda e, g=g, cc=cc, sl=sl: e.tensor_copy(
                    qhat_sb[:, g * TOWN + cc * 512:g * TOWN + cc * 512 + 512], dslot(sl)),
                    reads=[rp], writes=[r_qhat], tag=f"qre{var}_{g}_{cc}")
                sch.step("pe", MM(dslot(sl), r4t_sb[:, :],
                                  qhat_sb[:, g * TOWN + cc * 512:g * TOWN + cc * 512 + 512]),
                         reads=[r_qhat, r_const], writes=[rp], tag=f"qrot{var}_{g}_{cc}")
                sch.step("dve", lambda e, g=g, cc=cc, tb=tb: e.tensor_mul(
                    rsc3[:, :],
                    qhat_sb[:, g * TOWN + cc * 512:g * TOWN + cc * 512 + 512],
                    qtab_sb[0][:, :]),
                    reads=[r_qhat, r_qtabs[0]], writes=[r_rsc3], tag=f"qt1{var}_{g}_{cc}")
                sch.step("dve", lambda e, sl=sl: e.tensor_mul(
                    rsc4[:, :], dslot(sl), qtab_sb[1][:, :]),
                    reads=[rp, r_qtabs[1]], writes=[r_rsc4], tag=f"qu{var}_{g}_{cc}")
                sch.step("dve", lambda e, g=g, cc=cc: e.tensor_add(
                    qhat_sb[:, g * TOWN + cc * 512:g * TOWN + cc * 512 + 512],
                    rsc3[:, :], rsc4[:, :]),
                    reads=[r_rsc3, r_rsc4], writes=[r_qhat], tag=f"qh{var}_{g}_{cc}")

        def decode_head_gen(h, slot):
            kb_, qb_ = kbuf_sb[slot], qbuf_sb[slot]
            rkb, rqb = r_kbuf[slot], r_qbuf[slot]
            for nc8 in range(8):
                sl = nc8 % 2
                for k in range(2):
                    sch.step("pe", MM(dslot(sl)[0:64, :],
                                      wkd_sb[:, k * D + h * 64:k * D + h * 64 + 64],
                                      kv_sb[:, k * T + nc8 * 512:k * T + nc8 * 512 + 512],
                                      start=(k == 0), stop=(k == 1)),
                             reads=[r_kv, r_wkd],
                             writes=([r_psd[sl]] if k == 0 else []), tag=f"kd{var}_{h}_{nc8}")
                sch.group_end(r_psd[sl])
                sch.step("dve", lambda e, nc8=nc8, kb_=kb_, sl=sl: e.tensor_copy(
                    kb_[0:64, nc8 * 512:nc8 * 512 + 512], dslot(sl)[0:64, :]),
                    reads=[r_psd[sl]], writes=[rkb], tag=f"kde{var}_{h}_{nc8}")
                yield
            G, gsl = h // 4, h % 4
            if gsl == 2:
                sch.step("dve", lambda e, G=G: e.tensor_copy(
                    kb_[64:96, :], kr_sb[64:96, G * T:(G + 1) * T]),
                    reads=[r_kr], writes=[rkb], tag=f"kcp{var}_{h}")
                yield
            else:
                ci = {0: 0, 1: 1, 3: 2}[gsl]
                for nc8 in range(8):
                    sl = nc8 % 2
                    sch.step("pe", MM(dslot(sl)[64:96, :],
                                      psel_sb[:, ci * 32:(ci + 1) * 32],
                                      kr_sb[:, G * T + nc8 * 512:G * T + nc8 * 512 + 512],
                                      tp=(0, 64)),
                             reads=[r_kr, r_const], writes=[r_psd[sl]], tag=f"kpm{var}_{h}_{nc8}")
                    sch.step("dve", lambda e, nc8=nc8, kb_=kb_, sl=sl: e.tensor_copy(
                        kb_[64:96, nc8 * 512:nc8 * 512 + 512], dslot(sl)[64:96, :]),
                        reads=[r_psd[sl]], writes=[rkb], tag=f"kpe{var}_{h}_{nc8}")
                    yield
            for cc in range(4):
                sl = cc % 2
                for k in range(4):
                    sch.step("pe", MM(dslot(sl)[0:64, :],
                                      wqd_sb[:, k * D + h * 64:k * D + h * 64 + 64],
                                      qco_sb[:, k * TOWN + cc * 512:k * TOWN + cc * 512 + 512],
                                      start=(k == 0), stop=(k == 3)),
                             reads=[r_qco, r_wqd],
                             writes=([r_psd[sl]] if k == 0 else []), tag=f"qd{var}_{h}_{cc}")
                sch.group_end(r_psd[sl])
                sch.step("dve", lambda e, cc=cc, qb_=qb_, sl=sl: e.tensor_copy(
                    qb_[0:64, cc * 512:cc * 512 + 512], dslot(sl)[0:64, :]),
                    reads=[r_psd[sl]], writes=[rqb], tag=f"qde{var}_{h}_{cc}")
                yield
            if gsl == 2:
                sch.step("dve", lambda e, G=G, qb_=qb_: e.tensor_copy(
                    qb_[64:96, :], qhat_sb[64:96, G * TOWN:(G + 1) * TOWN]),
                    reads=[r_qhat], writes=[rqb], tag=f"qcp{var}_{h}")
                yield
            else:
                ci = {0: 0, 1: 1, 3: 2}[gsl]
                for cc in range(4):
                    sl = cc % 2
                    sch.step("pe", MM(dslot(sl)[64:96, :],
                                      psel_sb[:, ci * 32:(ci + 1) * 32],
                                      qhat_sb[:, G * TOWN + cc * 512:G * TOWN + cc * 512 + 512],
                                      tp=(0, 64)),
                             reads=[r_qhat, r_const], writes=[r_psd[sl]], tag=f"qpm{var}_{h}_{cc}")
                    sch.step("dve", lambda e, cc=cc, qb_=qb_, sl=sl: e.tensor_copy(
                        qb_[64:96, cc * 512:cc * 512 + 512], dslot(sl)[64:96, :]),
                        reads=[r_psd[sl]], writes=[rqb], tag=f"qpe{var}_{h}_{cc}")
                    yield

        def decode_v_gen(hg):
            for tt in range(32):
                sl = tt % 2
                for k in range(2):
                    sch.step("pe", MM(dslot(sl)[:, 0:128],
                                      kv_sb[:, k * T + tt * 128:k * T + tt * 128 + 128],
                                      wvd_sb[:, k * D + hg * 128:k * D + hg * 128 + 128],
                                      start=(k == 0), stop=(k == 1)),
                             reads=[r_kv, r_wvd],
                             writes=([r_psd[sl]] if k == 0 else []), tag=f"v{var}_{hg}_{tt}")
                sch.group_end(r_psd[sl])
                sch.step("dve", lambda e, tt=tt, sl=sl: e.tensor_copy(
                    bass.AP(v_sb, tt * 130, [[32 * 130, 128], [65, 2], [1, 64]]),
                    bass.AP(ps_decT, sl * 512, [[1024, 128], [64, 2], [1, 64]])),
                    reads=[r_psd[sl]], writes=[r_v], tag=f"ve{var}_{hg}_{tt}")
                if tt % 2 == 1:
                    yield
            sch.step("pl", lambda e: e.memset(
                bass.AP(v_sb, 64, [[32 * 130, 128], [130, 32], [65, 2], [1, 1]]), 1.0),
                writes=[r_v], tag=f"vones{var}_{hg}")

        def pump(g, k=1):
            if g is None:
                return None
            try:
                for _ in range(k):
                    next(g)
                return g
            except StopIteration:
                return None

        def drain(g):
            if g is not None:
                for _ in g:
                    pass

        for _ in decode_head_gen(0, 0):
            pass
        filler = None
        for h in range(NH):
            slot = h % 2
            kb_, qb_ = kbuf_sb[slot], qbuf_sb[slot]
            rkb, rqb = r_kbuf[slot], r_qbuf[slot]
            drain(filler)
            if h % 2 == 0:
                drain(decode_v_gen(h // 2))
            filler = decode_head_gen(h + 1, (h + 1) % 2) if h + 1 < NH else None
            hl = h % 2
            pending_norm = []
            for cc in range(4):
                while len(pending_norm) > 1:
                    pending_norm.pop(0)()
                q0c, nkt = chunks[cc]
                nb4 = (nkt + 1) // 2
                pends = []
                for bb in range(nb4):
                    if len(pends) >= 2:
                        emit_pv_q(pends.pop(0))
                    psl = bb % 2
                    psc, rsc_ = ps_sc[psl], r_pssc[psl]
                    pb, rpb = p_sb[psl], r_p[psl]
                    kts = [kt for kt in range(2 * bb, 2 * bb + 2) if kt < nkt]
                    for j, kt in enumerate(kts):
                        sch.step("pe", MM(psc[:, j * 512:j * 512 + 512],
                                          kb_[:, kt * 128:kt * 128 + 128],
                                          qb_[:, cc * 512:cc * 512 + 512]),
                                 reads=[rkb, rqb],
                                 writes=([rsc_] if j == 0 else []),
                                 tag=f"sc{var}_{h}_{cc}_{kt}")
                    sch.group_end(rsc_)
                    wid = 512 * len(kts)
                    sch.step("act", lambda e, psc=psc, pb=pb, wid=wid: e.activation(
                        pb[:, 0:wid], psc[:, 0:wid], AF.Exp, bias=EXP_BIAS),
                        reads=[rsc_], writes=[rpb], tag=f"ex{var}_{h}_{cc}_{bb}")
                    filler = pump(filler)
                    if bb == nb4 - 1 and pending_norm:
                        pending_norm.pop(0)()
                    for j, kt in enumerate(kts):
                        mj = kt - (nkt - 4)
                        if 0 <= mj < 4:
                            sch.step("pl", lambda e, pb=pb, j=j, mj=mj: e.tensor_mul(
                                pb[:, j * 512:j * 512 + 512], pb[:, j * 512:j * 512 + 512],
                                mask_sb[:, mj * 512:mj * 512 + 512]),
                                reads=[rpb, r_const], writes=[rpb],
                                tag=f"mk{var}_{h}_{cc}_{kt}")
                    pvsl = cc % 2

                    def emit_pv_q(pend, pvsl=pvsl, nkt=nkt):
                        ppsl, pkts = pend
                        for j, kt in enumerate(pkts):
                            sch.step("pe", MM(ps_pv[:, pvsl * 512:pvsl * 512 + 512],
                                              bass.AP(v_sb, kt * 130 + hl * 65,
                                                      [[32 * 130, 128], [1, 65]]),
                                              p_sb[ppsl][:, j * 512:j * 512 + 512],
                                              start=(kt == 0), stop=(kt == nkt - 1)),
                                     reads=[r_p[ppsl], r_v],
                                     writes=([r_pspv[pvsl]] if kt == 0 else []),
                                     tag=f"pv{var}_{h}_{cc}_{kt}")
                    pends.append((psl, kts))
                pvsl = cc % 2
                while pends:
                    emit_pv_q(pends.pop(0))
                sch.group_end(r_pspv[pvsl])
                def emit_norm(h=h, cc=cc, pvsl=pvsl):
                    sch.step("act", lambda e, pvsl=pvsl: e.activation(
                        recv_sb[0:1, :], ps_pv[64:65, pvsl * 512:pvsl * 512 + 512], AF.Ln),
                        reads=[r_pspv[pvsl]], writes=[r_recv], tag=f"rl{var}_{h}_{cc}")
                    sch.step("act", lambda e: e.activation(
                        recv_sb[0:1, :], recv_sb[0:1, :], AF.Exp, scale=-1.0),
                        reads=[r_recv], writes=[r_recv], tag=f"rc{var}_{h}_{cc}")
                    sch.step("pe", MM(ps_decT[0:64, 512:1024], ones1[0:1, 0:64],
                                      recv_sb[0:1, :]),
                             reads=[r_recv, r_const], writes=[r_psd[1]],
                             tag=f"nb{var}_{h}_{cc}")
                    sch.step("dve", lambda e: e.tensor_copy(nrm_sb[:, :],
                                                            ps_decT[0:64, 512:1024]),
                        reads=[r_psd[1]], writes=[r_nrm], tag=f"nbe{var}_{h}_{cc}")
                    if h % 2 == 0:
                        sch.step("dve", lambda e, h=h, cc=cc, pvsl=pvsl: e.tensor_mul(
                            attn_all[0:64,
                                     (h // 2) * TOWN + cc * 512:(h // 2) * TOWN + cc * 512 + 512],
                            ps_pv[0:64, pvsl * 512:pvsl * 512 + 512], nrm_sb[:, :]),
                            reads=[r_pspv[pvsl], r_nrm], writes=[r_attnall[h]],
                            tag=f"an{var}_{h}_{cc}")
                    else:
                        sch.step("dve", lambda e, pvsl=pvsl: e.tensor_mul(
                            aschat[:, :], ps_pv[0:64, pvsl * 512:pvsl * 512 + 512],
                            nrm_sb[:, :]),
                            reads=[r_pspv[pvsl], r_nrm], writes=[r_aschat],
                            tag=f"an{var}_{h}_{cc}")
                        sch.step("sp", lambda e, h=h, cc=cc: e.dma_start(
                            out=attn_all[64:128,
                                         (h // 2) * TOWN + cc * 512:(h // 2) * TOWN + cc * 512 + 512],
                            in_=aschat[:, :]),
                            reads=[r_aschat], writes=[r_attnall[h]], dma=True,
                            tag=f"as{var}_{h}_{cc}")
                pending_norm.append(emit_norm)
            while pending_norm:
                pending_norm.pop(0)()
        sch.flush()
        end_counts.append(sch.snapshot())
    assert end_counts[0] == end_counts[1], (
        "switch bodies must produce identical semaphore counts",
        {k: (end_counts[0][k], end_counts[1][k]) for k in end_counts[0]
         if end_counts[0][k] != end_counts[1][k]})
    sch.set_counts(end_counts[0])

    t_attn = tap("attn_pre", [128, 8 * TOWN], BF16)
    if t_attn is not None:
        sch.step("sp", lambda e: e.dma_start(out=t_attn[:, :], in_=attn_all[:, :]),
                 deps=[r for h in range(NH) for r in ([r_attnall[h].w] if r_attnall[h].w else [])],
                 dma=True, tag="tap_attn")
    sch.flush()
    es3.close()
    if stop_after < 4:
        es_mid.close()
        esL.close()
        es.close()
        return nc, tap_t

    es4 = ExitStack()
    S4 = es4.enter_context
    xo_sb = [S4(nc.sbuf_tensor(f"xo{i}", [128, 8 * 512], F32)) for i in range(2)]
    po_sb = S4(nc.sbuf_tensor("po_sb", [128, 512], F32))
    ps_wo = S4(nc.psum_tensor("ps_wo", [128, 1024], F32))
    ps_wo2 = S4(nc.psum_tensor("ps_wo2", [128, 1024], F32))
    r_xo = [Reg("xo0"), Reg("xo1")]
    r_po = Reg("po")
    r_pswo = [Reg("pswo0"), Reg("pswo1")]
    r_pswo2 = [Reg("pswo20"), Reg("pswo21")]

    for cc in range(4):
        b = cc % 2
        sch.step("sp", lambda e, cc=cc, b=b: e.dma_start(
            out=bass.AP(xo_sb[b], 0, [[8 * 512, 128], [512, 8], [1, 512]]),
            in_=bass.AP(xTo.tensor, cc * 512, [[TOWN, 128], [128 * TOWN, 8], [1, 512]])),
            writes=[r_xo[b]], dma=True, tag=f"ld_xo{cc}")
        for dchunk in range(8):
            sl = dchunk % 2
            rp, rp2 = r_pswo[sl], r_pswo2[sl]
            for j, h in enumerate(range(0, NH, 2)):
                lhs = wo_sb[0:64,
                            (h // 2) * D + dchunk * 128:(h // 2) * D + dchunk * 128 + 128]
                rhs = attn_all[0:64,
                               (h // 2) * TOWN + cc * 512:(h // 2) * TOWN + cc * 512 + 512]
                sch.step("pe", MM(ps_wo[:, sl * 512:sl * 512 + 512], lhs, rhs,
                                  start=(j == 0), stop=(j == 7)),
                         reads=[r_wo, r_attnall[h]],
                         writes=([rp] if j == 0 else []), tag=f"wo{cc}_{dchunk}_{h}")
            sch.group_end(rp)
            for j, h in enumerate(range(1, NH, 2)):
                lhs = wo_sb[64:128,
                            (h // 2) * D + dchunk * 128:(h // 2) * D + dchunk * 128 + 128]
                rhs = attn_all[64:128,
                               (h // 2) * TOWN + cc * 512:(h // 2) * TOWN + cc * 512 + 512]
                sch.step("pe", MM(ps_wo2[:, sl * 512:sl * 512 + 512], lhs, rhs,
                                  start=(j == 0), stop=(j == 7)),
                         reads=[r_wo, r_attnall[h]],
                         writes=([rp2] if j == 0 else []), tag=f"wo2{cc}_{dchunk}_{h}")
            sch.group_end(rp2)
            sch.step("dve", lambda e, dchunk=dchunk, sl=sl, b=b: e.tensor_add(
                po_sb[:, :], ps_wo[:, sl * 512:sl * 512 + 512],
                xo_sb[b][:, dchunk * 512:dchunk * 512 + 512]),
                reads=[rp, r_xo[b]], writes=[r_po], tag=f"poa{cc}_{dchunk}")
            sch.step("dve", lambda e, sl=sl: e.scalar_tensor_tensor(
                po_sb[:, :], ps_wo2[:, sl * 512:sl * 512 + 512], 1.0, po_sb[:, :],
                ALU.mult, ALU.add),
                reads=[rp2, r_po], writes=[r_po], tag=f"pob{cc}_{dchunk}")
            sch.step("sp", lambda e, dchunk=dchunk, cc=cc: e.dma_start(
                out=outT[dchunk * 128:(dchunk + 1) * 128, cc * 512:cc * 512 + 512],
                in_=po_sb[:, :]),
                reads=[r_po], dma=True, tag=f"st_p{cc}_{dchunk}")
    sch.flush()
    es4.close()
    es_mid.close()
    esL.close()
    if stop_after < 5:
        es.close()
        return nc, tap_t


    es5 = ExitStack()
    S5 = es5.enter_context
    xn_sb = S5(nc.sbuf_tensor("xn_sb", [128, 8 * TOWN], BF16))
    h_sb = S5(nc.sbuf_tensor("h_sb", [128, 24 * 1024], BF16))
    silu_sb = S5(nc.sbuf_tensor("silu_sb", [128, 1024], BF16))
    wg_sb = [S5(nc.sbuf_tensor(f"wg{i}", [128, 8 * 384], BF16)) for i in range(2)]
    wu_sb = [S5(nc.sbuf_tensor(f"wu{i}", [128, 8 * 384], BF16)) for i in range(2)]
    wd_sb = [S5(nc.sbuf_tensor(f"wd{i}", [128, 24 * 128], BF16)) for i in range(2)]
    xo5_sb = [S5(nc.sbuf_tensor(f"xo5{i}", [128, 8 * 512], BF16)) for i in range(2)]
    xf5_sb = [S5(nc.sbuf_tensor(f"xf5{i}", [128, 4 * 512], F32)) for i in range(2)]
    xsq5 = S5(nc.sbuf_tensor("xsq5", [128, 2 * 512], BF16))
    pr_sb = [S5(nc.sbuf_tensor(f"pr{i}", [128, 512], F32)) for i in range(2)]
    ob_sb = S5(nc.sbuf_tensor("ob_sb", [128, 512], F32))
    vec5 = S5(nc.sbuf_tensor("vec5", [1, 3 * 512], F32))
    ps5_bc = S5(nc.psum_tensor("ps5_bc", [128, 1024], F32))
    es5a = ExitStack()
    ps5_ln = es5a.enter_context(nc.psum_tensor("ps5_ln", [1, 1024], F32))
    r_xn, r_h, r_silu = Reg("xn"), Reg("h"), Reg("silu")
    r_wg = [Reg("wg0"), Reg("wg1")]
    r_wu = [Reg("wu0"), Reg("wu1")]
    r_wd = [Reg("wd0"), Reg("wd1")]
    r_xo5 = [Reg("xo50"), Reg("xo51")]
    r_xf5 = [Reg("xf50"), Reg("xf51")]
    r_xsq5 = [Reg("xsq50"), Reg("xsq51")]
    r_pr = [Reg("pr0"), Reg("pr1")]
    r_ob, r_vec5 = Reg("ob"), Reg("vec5")
    r_ps5ln, r_ps5bc0, r_ps5bc1 = Reg("ps5ln"), Reg("ps5bc0"), Reg("ps5bc1")
    r_ps5g, r_ps5u = Reg("ps5g"), Reg("ps5u")
    r_ps5d = [Reg("ps5d0"), Reg("ps5d1")]

    def V5(i):
        return vec5[0:1, i * 512:(i + 1) * 512]

    for cc in range(4):
        b = cc % 2
        for hh2 in range(2):
            xfb = (2 * cc + hh2) % 2
            sch.step("sp", lambda e, cc=cc, hh2=hh2, xfb=xfb: e.dma_start(
                out=bass.AP(xf5_sb[xfb], 0, [[4 * 512, 128], [512, 4], [1, 512]]),
                in_=bass.AP(xTo.tensor, cc * 512 + hh2 * 4 * 128 * TOWN,
                            [[TOWN, 128], [128 * TOWN, 4], [1, 512]])),
                writes=[r_xf5[xfb]], dma=True, tag=f"l5x{cc}_{hh2}")
            sch.step("act", lambda e, b=b, hh2=hh2, xfb=xfb: e.activation(
                xo5_sb[b][:, hh2 * 2048:(hh2 + 1) * 2048], xf5_sb[xfb][:, :], AF.Copy),
                reads=[r_xf5[xfb]], writes=[r_xo5[b]], tag=f"cx5{cc}_{hh2}")
        for c in range(8):
            sch.step("pe", MM(ps5_ln[0:1, 0:512], onesv[:, 0:1],
                              xo5_sb[b][:, c * 512:(c + 1) * 512],
                              start=(c == 0), stop=(c == 7)),
                     reads=[r_xo5[b], r_const], writes=([r_ps5ln] if c == 0 else []),
                     tag=f"l5m{cc}_{c}")
        sch.group_end(r_ps5ln)
        sch.step("dve", lambda e: e.tensor_scalar_mul(V5(0), ps5_ln[0:1, 0:512], 1.0 / D),
                 reads=[r_ps5ln], writes=[r_vec5], tag=f"l5mm{cc}")
        sch.step("pe", MM(ps5_bc[:, 0:512], ones1[0:1, :], V5(0)),
                 reads=[r_vec5, r_const], writes=[r_ps5bc0], tag=f"l5bc{cc}")
        for c in range(8):
            sq = c % 2
            sch.step("act", lambda e, b=b, c=c, sq=sq: e.activation(
                xsq5[:, sq * 512:sq * 512 + 512], xo5_sb[b][:, c * 512:(c + 1) * 512],
                AF.Square),
                reads=[r_xo5[b]], writes=[r_xsq5[sq]], tag=f"l5s{cc}_{c}")
            sch.step("pe", MM(ps5_ln[0:1, 512:1024], onesv[:, 0:1],
                              xsq5[:, sq * 512:sq * 512 + 512],
                              start=(c == 0), stop=(c == 7)),
                     reads=[r_xsq5[sq], r_const], writes=([r_ps5ln] if c == 0 else []),
                     tag=f"l5q{cc}_{c}")
        sch.group_end(r_ps5ln)
        sch.step("dve", lambda e: e.tensor_scalar_mul(V5(1), ps5_ln[0:1, 512:1024], 1.0 / D),
                 reads=[r_ps5ln], writes=[r_vec5], tag=f"l5v{cc}")
        sch.step("dve", lambda e: e.tensor_mul(V5(2), V5(0), V5(0)),
                 reads=[r_vec5], writes=[r_vec5], tag=f"l5m2{cc}")
        sch.step("dve", lambda e: e.tensor_sub(V5(1), V5(1), V5(2)),
                 reads=[r_vec5], writes=[r_vec5], tag=f"l5var{cc}")
        sch.step("act", lambda e: e.activation(V5(1), V5(1), AF.Ln, bias=1e-5),
                 reads=[r_vec5], writes=[r_vec5], tag=f"l5sd{cc}")
        sch.step("act", lambda e: e.activation(V5(2), V5(1), AF.Exp, scale=-0.5),
                 reads=[r_vec5], writes=[r_vec5], tag=f"l5r{cc}")
        sch.step("pe", MM(ps5_bc[:, 512:1024], ones1[0:1, :], V5(2)),
                 reads=[r_vec5, r_const], writes=[r_ps5bc1], tag=f"l5rb{cc}")
        sch.step("dve", lambda e, b=b, cc=cc: e.tensor_sub(
            bass.AP(xn_sb, cc * 512, [[8 * TOWN, 128], [TOWN, 8], [1, 512]]),
            bass.AP(xo5_sb[b], 0, [[8 * 512, 128], [512, 8], [1, 512]]),
            bass.AP(ps5_bc, 0, [[1024, 128], [0, 8], [1, 512]])),
            reads=[r_xo5[b], r_ps5bc0], writes=[r_xn], tag=f"l5c{cc}")
        sch.step("dve", lambda e, cc=cc: e.tensor_mul(
            bass.AP(xn_sb, cc * 512, [[8 * TOWN, 128], [TOWN, 8], [1, 512]]),
            bass.AP(xn_sb, cc * 512, [[8 * TOWN, 128], [TOWN, 8], [1, 512]]),
            bass.AP(ps5_bc, 512, [[1024, 128], [0, 8], [1, 512]])),
            reads=[r_xn, r_ps5bc1], writes=[r_xn], tag=f"l5n{cc}")
    sch.flush()

    es5a.close()
    ps5_g = S5(nc.psum_tensor("ps5_g", [128, 1024], F32))
    ps5_u = S5(nc.psum_tensor("ps5_u", [128, 1024], F32))
    ps5_d = S5(nc.psum_tensor("ps5_d", [128, 1024], F32))
    r_outT = Reg("outT")
    last_store = None
    for tw in range(2):
        for hb in range(8):
            wbuf = hb % 2
            sch.step("pl", lambda e, hb=hb, wbuf=wbuf: e.dma_start(
                out=bass.AP(wg_sb[wbuf], 0, [[8 * 384, 128], [384, 8], [1, 384]]),
                in_=bass.AP(wgT.tensor, hb * 384, [[HID, 128], [128 * HID, 8], [1, 384]])),
                writes=[r_wg[wbuf]], dma=True, tag=f"f_wg{tw}_{hb}")
            sch.step("pl", lambda e, hb=hb, wbuf=wbuf: e.dma_start(
                out=bass.AP(wu_sb[wbuf], 0, [[8 * 384, 128], [384, 8], [1, 384]]),
                in_=bass.AP(wuT.tensor, hb * 384, [[HID, 128], [128 * HID, 8], [1, 384]])),
                writes=[r_wu[wbuf]], dma=True, tag=f"f_wu{tw}_{hb}")
            for j in range(3):
                hh = 3 * hb + j
                for nn in range(2):
                    for k in range(8):
                        sch.step("pe", MM(ps5_g[:, nn * 512:nn * 512 + 512],
                                          wg_sb[wbuf][:, k * 384 + j * 128:k * 384 + j * 128 + 128],
                                          xn_sb[:, k * TOWN + tw * 1024 + nn * 512:
                                                k * TOWN + tw * 1024 + nn * 512 + 512],
                                          start=(k == 0), stop=(k == 7)),
                                 reads=[r_wg[wbuf], r_xn],
                                 writes=([r_ps5g] if (nn == 0 and k == 0) else []),
                                 tag=f"fg{tw}_{hh}_{nn}_{k}")
                sch.group_end(r_ps5g)
                sch.step("act", lambda e: e.activation(
                    silu_sb[:, :], ps5_g[:, :], AF.Silu, scale=FFS),
                    reads=[r_ps5g], writes=[r_silu], tag=f"fs{tw}_{hh}")
                for nn in range(2):
                    for k in range(8):
                        sch.step("pe", MM(ps5_u[:, nn * 512:nn * 512 + 512],
                                          wu_sb[wbuf][:, k * 384 + j * 128:k * 384 + j * 128 + 128],
                                          xn_sb[:, k * TOWN + tw * 1024 + nn * 512:
                                                k * TOWN + tw * 1024 + nn * 512 + 512],
                                          start=(k == 0), stop=(k == 7)),
                                 reads=[r_wu[wbuf], r_xn],
                                 writes=([r_ps5u] if (nn == 0 and k == 0) else []),
                                 tag=f"fu{tw}_{hh}_{nn}_{k}")
                sch.group_end(r_ps5u)
                sch.step("dve", lambda e, hh=hh: e.scalar_tensor_tensor(
                    h_sb[:, hh * 1024:(hh + 1) * 1024], ps5_u[:, :], FFS, silu_sb[:, :],
                    ALU.mult, ALU.mult),
                    reads=[r_ps5u, r_silu], writes=[r_h], tag=f"fh{tw}_{hh}")
        def wd_load(dchunk):
            dbuf = dchunk % 2
            sch.step("pl", lambda e, dchunk=dchunk, dbuf=dbuf: e.dma_start(
                out=bass.AP(wd_sb[dbuf], 0, [[24 * 128, 128], [128, 24], [1, 128]]),
                in_=bass.AP(wdT.tensor, dchunk * 128, [[D, 128], [128 * D, 24], [1, 128]])),
                writes=[r_wd[dbuf]], dma=True, tag=f"f_wd{tw}_{dchunk}")
        wd_load(0)
        for dchunk in range(8):
            dbuf = dchunk % 2
            if dchunk + 1 < 8:
                wd_load(dchunk + 1)
            for nn in range(2):
                sl = nn
                for k in range(24):
                    sch.step("pe", MM(ps5_d[:, sl * 512:sl * 512 + 512],
                                      wd_sb[dbuf][:, k * 128:(k + 1) * 128],
                                      h_sb[:, k * 1024 + nn * 512:k * 1024 + nn * 512 + 512],
                                      start=(k == 0), stop=(k == 23)),
                             reads=[r_wd[dbuf], r_h],
                             writes=([r_ps5d[sl]] if k == 0 else []),
                             tag=f"fd{tw}_{dchunk}_{nn}_{k}")
                sch.group_end(r_ps5d[sl])
                pbuf = nn
                sch.step("sp", lambda e, dchunk=dchunk, tw=tw, nn=nn, pbuf=pbuf: e.dma_start(
                    out=pr_sb[pbuf][:, :],
                    in_=outT[dchunk * 128:(dchunk + 1) * 128,
                             tw * 1024 + nn * 512:tw * 1024 + nn * 512 + 512]),
                    reads=[r_outT], writes=[r_pr[pbuf]], dma=True, tag=f"fp{tw}_{dchunk}_{nn}")
                sch.step("dve", lambda e, sl=sl, pbuf=pbuf: e.scalar_tensor_tensor(
                    ob_sb[:, :], ps5_d[:, sl * 512:sl * 512 + 512], FFS, pr_sb[pbuf][:, :],
                    ALU.mult, ALU.add),
                    reads=[r_ps5d[sl], r_pr[pbuf]], writes=[r_ob], tag=f"fo{tw}_{dchunk}_{nn}")
                last_store = sch.step("sp", lambda e, dchunk=dchunk, tw=tw, nn=nn: e.dma_start(
                    out=outT[dchunk * 128:(dchunk + 1) * 128,
                             tw * 1024 + nn * 512:tw * 1024 + nn * 512 + 512],
                    in_=ob_sb[:, :]),
                    reads=[r_ob], writes=[r_outT], dma=True, tag=f"fst{tw}_{dchunk}_{nn}")
    sch.step("sp", lambda e: e.nop(), deps=[last_store], tag="final_wait")
    sch.flush()
    es5.close()
    es.close()
    return nc, tap_t


def make_core_inputs(ins, const, b, core_type):
    x = ins["L0_attn_x"][b]
    own = own_positions(core_type)
    cosq = np.ascontiguousarray(
        (np.tile(const["cos_t"], (4, 1))[:, own] * np.float32(SCALE)).astype(np.float32))
    sinq = np.ascontiguousarray(
        (np.tile(const["sin_t"], (4, 1))[:, own] * np.float32(SCALE)).astype(np.float32))
    return {
        "xT": np.ascontiguousarray(x.T),
        "xTo": np.ascontiguousarray(x.T[:, own]),
        "w_kv": ins["L0_attn_W_kv"], "w_q": ins["L0_attn_W_q"],
        "w_kdec": ins["L0_attn_W_kdec"], "w_vdec": ins["L0_attn_W_vdec"],
        "w_qdec": ins["L0_attn_W_qdec"], "w_krope": ins["L0_attn_W_krope"],
        "w_qrope": ins["L0_attn_W_qrope"], "w_o": ins["L0_attn_W_o"],
        "g_kv": ins["L0_attn_g_kv"],
        "wgT": np.ascontiguousarray(ins["L0_ffn_w_gate"].T),
        "wuT": np.ascontiguousarray(ins["L0_ffn_w_up"].T),
        "wdT": np.ascontiguousarray(ins["L0_ffn_w_down"].T),
        "cosk": const["cosk"], "sink": const["sink"],
        "cosqo": cosq, "sinqo": sinq,
        "maskc": const["maskc"], "r4t": const["r4t"], "psel": const["psel"],
    }


_NC_CACHE = {}


def kernel(**inputs):
    from concourse.bass_utils import run_bass_kernel_spmd
    if "nc" not in _NC_CACHE:
        _NC_CACHE["nc"] = build_nc()[0]
    nc = _NC_CACHE["nc"]
    const = host_constants()
    in_maps = [make_core_inputs(inputs, const, c // 2, c % 2) for c in range(8)]
    res = run_bass_kernel_spmd(nc, in_maps, core_ids=list(range(8)))
    out = np.empty((4, T, D), np.float32)
    for c in range(8):
        own = own_positions(c % 2)
        out[c // 2, own, :] = np.asarray(res.results[c]["outT"]).T
    return out


def host_constants():
    inv_freq = 1.0 / 10000.0 ** (np.arange(0, RD, 2, dtype=np.float32) / RD)
    t = np.arange(T, dtype=np.float32)
    freqs = np.outer(t, inv_freq)
    emb = np.concatenate([freqs, freqs], -1)
    cos_t = np.cos(emb).astype(np.float32).T
    sin_t = np.sin(emb).astype(np.float32).T
    cosk = np.ascontiguousarray(np.tile(cos_t, (4, 1)).astype(np.float32))
    sink = np.ascontiguousarray(np.tile(sin_t, (4, 1)).astype(np.float32))
    kk = np.arange(128)[:, None]
    qq = np.arange(512)[None, :]
    masks = np.ascontiguousarray(np.concatenate(
        [(qq - kk >= 128 * j).astype(np.float32) for j in range(4)],
        axis=1).astype(bfdt))
    R = np.zeros((32, 32), np.float32)
    for i in range(16):
        R[i, i + 16] = -1.0
        R[i + 16, i] = 1.0
    R4 = np.zeros((128, 128), np.float32)
    for g in range(4):
        R4[g * 32:(g + 1) * 32, g * 32:(g + 1) * 32] = R
    r4t = np.ascontiguousarray(R4.T.astype(bfdt))
    ps = np.zeros((128, 96), np.float32)
    for ci, g in enumerate((0, 1, 3)):
        for i in range(32):
            ps[g * 32 + i, ci * 32 + i] = 1.0
    psel = np.ascontiguousarray(ps.astype(bfdt))
    return dict(cosk=cosk, sink=sink, maskc=masks, r4t=r4t, psel=psel,
                cos_t=cos_t, sin_t=sin_t)


def own_positions(core_type):
    if core_type == 0:
        return np.concatenate([np.arange(0, 1024), np.arange(3072, 4096)])
    return np.arange(1024, 3072)


# version 14 (final)
# speedup vs baseline: 1.3033x; 1.0028x over previous
import numpy as np
import ml_dtypes
import concourse.bass as bass
import concourse.mybir as mybir
from contextlib import ExitStack

F32 = mybir.dt.float32
BF16 = mybir.dt.bfloat16
I32 = mybir.dt.int32
AF = mybir.ActivationFunctionType
ALU = mybir.AluOpType
bfdt = ml_dtypes.bfloat16

T, TOWN, D, KVR, QR, NH, HD, RD, HID = 4096, 2048, 1024, 256, 512, 16, 64, 32, 3072
NTG = T // 512
SCALE = float(1.0 / np.sqrt(96.0))
EXP_BIAS = -8.0
FFS = float(1.0 / 127.0)
CHUNKS = {0: [(0, 4), (512, 8), (3072, 28), (3584, 32)],
          1: [(1024, 12), (1536, 16), (2048, 20), (2560, 24)]}
ENGINES = ("pe", "act", "dve", "pl", "sp")


class Step:
    __slots__ = ("eng", "emit", "deps", "idx", "seq", "inc_amt", "needed", "tag",
                 "gen", "self_wait")

    def __init__(self, eng, emit, deps, idx, tag="", gen=0):
        self.eng, self.emit, self.deps, self.idx, self.tag = eng, emit, deps, idx, tag
        self.seq = None
        self.inc_amt = 0
        self.needed = False
        self.gen = gen
        self.self_wait = None


class Reg:
    __slots__ = ("w", "readers", "name")

    def __init__(self, name=""):
        self.w, self.readers, self.name = None, [], name


class Sched:
    def __init__(self, nc, es):
        self.nc = nc
        self.steps = []
        self.gen = 0
        self.sems = {}
        self.counts = {}
        for e in ENGINES:
            self.sems[e] = es.enter_context(nc.semaphore(f"sem_{e}"))
            self.sems[e + "_dma"] = es.enter_context(nc.semaphore(f"sem_{e}dma"))
            self.counts[e] = 0
            self.counts[e + "_dma"] = 0

    def step(self, eng, emit, reads=(), writes=(), deps=(), dma=False, tag=""):
        alldeps = list(deps)
        for r in reads:
            if r.w is not None:
                alldeps.append(r.w)
        for wreg in writes:
            if wreg.w is not None:
                alldeps.append(wreg.w)
            alldeps.extend(wreg.readers)
        key = eng + ("_dma" if dma else "")
        pruned, seen = [], set()
        for d in alldeps:
            if d is None or id(d) in seen:
                continue
            seen.add(id(d))
            if d.gen < self.gen:
                continue
            if d.eng == key and key in ("pe", "sp_dma", "pl_dma", "act_dma"):
                continue
            pruned.append(d)
        s = Step(key, emit, pruned, len(self.steps), tag, gen=self.gen)
        self.steps.append(s)
        for d in pruned:
            d.needed = True
        for r in reads:
            r.readers.append(s)
        for wreg in writes:
            wreg.w = s
            wreg.readers = []
        return s

    def group_end(self, reg):
        reg.w = self.steps[-1]
        reg.readers = []
        return reg.w

    def snapshot(self):
        return dict(self.counts)

    def set_counts(self, c):
        self.counts = dict(c)

    def flush(self, in_block=True):
        nc = self.nc
        per_eng = {e: [] for e in ENGINES}
        for s in self.steps:
            e = s.eng[:-4] if s.eng.endswith("_dma") else s.eng
            per_eng[e].append(s)
        for e in ENGINES:
            run = []

            def close_run(run):
                if run:
                    for s in run:
                        self.counts[s.eng] += 16
                    for s in run:
                        s.seq = self.counts[run[0].eng]
            nd = 0
            for s in per_eng[e]:
                if s.eng.endswith("_dma"):
                    if s.deps:
                        close_run(run)
                        s.self_wait = self.counts[s.eng]
                        run = [s]
                    else:
                        run.append(s)
                else:
                    close_run(run)
                    run = []
            close_run(run)
        for s in self.steps:
            if not s.eng.endswith("_dma") and s.needed:
                s.inc_amt = 1
                self.counts[s.eng] += 1
                s.seq = self.counts[s.eng]
        for s in self.steps:
            if s.eng.endswith("_dma"):
                s.inc_amt = 16
        sched = self

        def make(engname):
            def body(eng):
                cur = {}
                for s in per_eng[engname]:
                    isdma = s.eng.endswith("_dma")
                    if s.self_wait:
                        eng.wait_ge(sched.sems[s.eng], s.self_wait)
                    for d in s.deps:
                        assert d.seq is not None, f"dep of {s.tag} on {d.tag} lacks seq"
                        if not isdma and cur.get(d.eng, 0) >= d.seq:
                            continue
                        cur[d.eng] = max(cur.get(d.eng, 0), d.seq)
                        eng.wait_ge(sched.sems[d.eng], d.seq)
                    inst = s.emit(eng)
                    if s.inc_amt:
                        inst.then_inc(sched.sems[s.eng], s.inc_amt)
                dkey = engname + "_dma"
                if any(s.eng == dkey for s in per_eng[engname]):
                    eng.wait_ge(sched.sems[dkey], sched.counts[dkey])
            return body

        with nc.Block() as blk:
            for engname, attr in (("sp", "sync"), ("pl", "gpsimd"), ("pe", "tensor"),
                                  ("act", "scalar"), ("dve", "vector")):
                if per_eng[engname]:
                    getattr(blk, attr)(make(engname))
        self.steps = []
        self.gen += 1


def MM(out, lhsT, rhs, start=True, stop=True, tp=None):
    if tp is None:
        return lambda e: e.matmul(out, lhsT, rhs, start=start, stop=stop)
    return lambda e: e.matmul(out, lhsT, rhs, start=start, stop=stop, tile_position=tp)


def build_nc(taps=(), stop_after=99, ntg=NTG, parts=('ln','kv','qc','kr','rope')):
    taps = set(taps)
    nc = bass.Bass("TRN2", target_bir_lowering=False)

    xT = nc.dram_tensor("xT", [D, T], F32, kind="ExternalInput").ap()
    xTo = nc.dram_tensor("xTo", [D, TOWN], F32, kind="ExternalInput").ap()
    w_kv = nc.dram_tensor("w_kv", [D, KVR], F32, kind="ExternalInput").ap()
    w_q = nc.dram_tensor("w_q", [D, QR], F32, kind="ExternalInput").ap()
    w_kdec = nc.dram_tensor("w_kdec", [KVR, D], F32, kind="ExternalInput").ap()
    w_vdec = nc.dram_tensor("w_vdec", [KVR, D], F32, kind="ExternalInput").ap()
    w_qdec = nc.dram_tensor("w_qdec", [QR, D], F32, kind="ExternalInput").ap()
    w_krope = nc.dram_tensor("w_krope", [D, QR], F32, kind="ExternalInput").ap()
    w_qrope = nc.dram_tensor("w_qrope", [QR, QR], F32, kind="ExternalInput").ap()
    w_o = nc.dram_tensor("w_o", [D, D], F32, kind="ExternalInput").ap()
    g_kv = nc.dram_tensor("g_kv", [KVR], F32, kind="ExternalInput").ap()
    wgT = nc.dram_tensor("wgT", [D, HID], I32, kind="ExternalInput").ap()
    wuT = nc.dram_tensor("wuT", [D, HID], I32, kind="ExternalInput").ap()
    wdT = nc.dram_tensor("wdT", [HID, D], I32, kind="ExternalInput").ap()
    cosk = nc.dram_tensor("cosk", [128, T], F32, kind="ExternalInput").ap()
    sink = nc.dram_tensor("sink", [128, T], F32, kind="ExternalInput").ap()
    cosqo = nc.dram_tensor("cosqo", [128, TOWN], F32, kind="ExternalInput").ap()
    sinqo = nc.dram_tensor("sinqo", [128, TOWN], F32, kind="ExternalInput").ap()
    maskc = nc.dram_tensor("maskc", [128, 4 * 512], BF16, kind="ExternalInput").ap()
    r4t = nc.dram_tensor("r4t", [128, 128], BF16, kind="ExternalInput").ap()
    psel = nc.dram_tensor("psel", [128, 3 * 32], BF16, kind="ExternalInput").ap()
    outT = nc.dram_tensor("outT", [D, TOWN], F32, kind="ExternalOutput").ap()

    tap_t = {}

    def tap(name, shape, dtype=F32):
        if name in taps:
            tap_t[name] = nc.dram_tensor("tap_" + name, shape, dtype,
                                         kind="ExternalOutput").ap()
        return tap_t.get(name)

    es = ExitStack()
    S = es.enter_context
    sch = Sched(nc, es)

    mask_sb = S(nc.sbuf_tensor("mask_sb", [128, 4 * 512], BF16))
    r4t_sb = S(nc.sbuf_tensor("r4t_sb", [128, 128], BF16))
    psel_sb = S(nc.sbuf_tensor("psel_sb", [128, 3 * 32], BF16))
    onesv = S(nc.sbuf_tensor("onesv", [128, 1], BF16))
    ones1 = S(nc.sbuf_tensor("ones1", [1, 128], F32))
    gkv_sb = S(nc.sbuf_tensor("gkv_sb", [128, 2], F32))
    CVALS = [1e-5, 1e-6, EXP_BIAS, FFS, -0.5, -1.0]
    cst_sb = S(nc.sbuf_tensor("cst_sb", [128, len(CVALS)], F32))
    wst = {}
    r_const, r_gkv = Reg("const"), Reg("gkv")

    r_attnall = [Reg(f"at{h}") for h in range(NH)]
    r_wo = Reg("wo")

    esL = ExitStack()
    SL = esL.enter_context
    kv_sb = SL(nc.sbuf_tensor("kv_sb", [128, 2 * T], BF16))
    kr_sb = SL(nc.sbuf_tensor("kr_sb", [128, 4 * T], BF16))
    qc_dram = nc.dram_tensor("qc_scr", [4 * 128, T], BF16, kind="Internal").ap()
    r_kv, r_qc, r_kr, r_vec = Reg("kv"), Reg("qc"), Reg("kr"), Reg("vec")

    sch.step("sp", lambda e: e.dma_start(out=mask_sb[:, :], in_=maskc[:, :]),
             writes=[r_const], dma=True, tag="ld_mask")
    sch.step("sp", lambda e: e.dma_start(out=r4t_sb[:, :], in_=r4t[:, :]),
             writes=[r_const], dma=True, tag="ld_r4t")
    sch.step("sp", lambda e: e.dma_start(out=psel_sb[:, :], in_=psel[:, :]),
             writes=[r_const], dma=True, tag="ld_psel")
    sch.step("sp", lambda e: e.dma_start(
        out=bass.AP(gkv_sb, 0, [[2, 128], [1, 2]]),
        in_=bass.AP(g_kv.tensor, 0, [[1, 128], [128, 2]]),
        allow_slow_non_contiguous=True),
        writes=[r_gkv], dma=True, tag="ld_gkv")
    sch.step("pl", lambda e: e.memset(onesv[:, :], 1.0), writes=[r_const], tag="ones_v")
    sch.step("pl", lambda e: e.memset(ones1[:, :], 1.0), writes=[r_const], tag="ones_1")
    for i, v in enumerate(CVALS):
        sch.step("pl", lambda e, i=i, v=v: e.memset(cst_sb[:, i:i + 1], float(v)),
                 writes=[r_const], tag=f"cst{i}")
        nc.const_aps.register(F32, float(v), cst_sb[:, i:i + 1])

    es0 = ExitStack()
    S0 = es0.enter_context
    vec_sb = S0(nc.sbuf_tensor("vec_sb", [1, 6 * 512], F32))
    qc_sb = S0(nc.sbuf_tensor("qc_sb", [128, 4 * T], BF16))
    wkv_sb = S0(nc.sbuf_tensor("wkv_sb", [128, 8 * KVR], BF16))
    wq_sb = S0(nc.sbuf_tensor("wq_sb", [128, 8 * QR], BF16))
    wkr_sb = S0(nc.sbuf_tensor("wkr_sb", [128, 8 * QR], BF16))
    xg_sb = [S0(nc.sbuf_tensor(f"xg{i}", [128, 8 * 512], BF16)) for i in range(2)]
    xf_sb = [S0(nc.sbuf_tensor(f"xf{i}", [128, 4 * 512], F32)) for i in range(2)]
    xsq_sb = S0(nc.sbuf_tensor("xsq_sb", [128, 2 * 512], BF16))
    xc_sb = S0(nc.sbuf_tensor("xc_sb", [128, 8 * 512], BF16))
    rb_sb = [S0(nc.sbuf_tensor(f"rb{i}", [128, 512], BF16)) for i in range(2)]
    kvsq_sb = S0(nc.sbuf_tensor("kvsq_sb", [128, 2 * 512], BF16))
    ktab_sb = [S0(nc.sbuf_tensor(f"ktab{i}", [128, 512], F32)) for i in range(2)]
    stab_sb = [S0(nc.sbuf_tensor(f"stab{i}", [128, 512], F32)) for i in range(2)]
    rot_scr = S0(nc.sbuf_tensor("rot_scr", [128, 512], BF16))
    rot_scr2 = S0(nc.sbuf_tensor("rot_scr2", [128, 512], BF16))
    ps_bc = S0(nc.psum_tensor("ps_bc", [128, 1024], F32))
    ps_kv = S0(nc.psum_tensor("ps_kv", [128, 1024], F32))
    ps_mm2 = S0(nc.psum_tensor("ps_mm2", [128, 1024], F32))
    ps_misc = S0(nc.psum_tensor("ps_misc", [128, 512], F32))
    ps_rot = S0(nc.psum_tensor("ps_rot", [128, 512], F32))
    ps_ln = ps_misc
    ps_kr = ps_mm2
    ps_qc = ps_mm2
    ps_ss = ps_misc

    r_wkv, r_wq, r_wkr = Reg("wkv"), Reg("wq"), Reg("wkr")
    r_xg = [Reg("xg0"), Reg("xg1")]
    r_xf = [Reg("xf0"), Reg("xf1")]
    r_xsq = [Reg("xsq0"), Reg("xsq1")]
    r_xc = Reg("xc")
    r_xcc = [Reg(f"xcc{i}") for i in range(8)]
    r_rb = [Reg("rb0"), Reg("rb1")]
    r_kvsq = Reg("kvsq")
    r_ktab = [Reg("kt0"), Reg("kt1")]
    r_rsc, r_rsc2 = Reg("rsc"), Reg("rsc2")
    r_psln, r_psbc0, r_psbc1 = Reg("psmisc"), Reg("psbc0"), Reg("psbc1")
    r_pskv, r_psrot = Reg("pskv"), Reg("psrot")
    r_psss = r_psln
    r_mm2 = [Reg("mm2_0"), Reg("mm2_1")]
    r_pskr = r_mm2
    r_psqc = r_mm2

    def wload(eng, dst, src, cols, kchunks, reg, tag):
        wstage, r_wstage = wst["t"], wst["r"]
        per = max(1, 2048 // cols)
        for k0 in range(0, kchunks, per):
            kn = min(per, kchunks - k0)
            sch.step("sp", lambda e, k0=k0, kn=kn: e.dma_start(
                out=bass.AP(wstage, 0, [[2048, 128], [cols, kn], [1, cols]]),
                in_=bass.AP(src.tensor, k0 * 128 * cols,
                            [[cols, 128], [128 * cols, kn], [1, cols]])),
                writes=[r_wstage], dma=True, tag=f"{tag}_{k0}")
            sch.step("dve", lambda e, k0=k0, kn=kn: e.tensor_copy(
                dst[:, k0 * cols:(k0 + kn) * cols], wstage[:, 0:kn * cols]),
                reads=[r_wstage], writes=[reg], tag=f"{tag}c_{k0}")

    wst["t"] = S0(nc.sbuf_tensor("wstage0", [128, 2048], F32))
    wst["r"] = Reg("wstage0")
    wload("pl", wkv_sb, w_kv, KVR, 8, r_wkv, "ld_wkv")
    wload("pl", wq_sb, w_q, QR, 8, r_wq, "ld_wq")
    wload("pl", wkr_sb, w_krope, QR, 8, r_wkr, "ld_wkr")

    VM, VSD, VR, VMSQ, VA, VRHO = range(6)

    def V(i):
        return vec_sb[0:1, i * 512:(i + 1) * 512]

    for tg in range(ntg):
        b = tg % 2
        c0 = tg * 512
        xg, rxg = xg_sb[b], r_xg[b]
        for hh2 in range(2):
            xfb = (2 * tg + hh2) % 2
            sch.step("sp", lambda e, c0=c0, hh2=hh2, xfb=xfb: e.dma_start(
                out=bass.AP(xf_sb[xfb], 0, [[4 * 512, 128], [512, 4], [1, 512]]),
                in_=bass.AP(xT.tensor, c0 + hh2 * 4 * 128 * T,
                            [[T, 128], [128 * T, 4], [1, 512]])),
                writes=[r_xf[xfb]], dma=True, tag=f"ld_x{tg}_{hh2}")
            sch.step("act", lambda e, xg=xg, hh2=hh2, xfb=xfb: e.activation(
                xg[:, hh2 * 2048:(hh2 + 1) * 2048], xf_sb[xfb][:, :], AF.Copy),
                reads=[r_xf[xfb]], writes=[rxg], tag=f"cx{tg}_{hh2}")
        sch.step("sp", lambda e, b=b, c0=c0: e.dma_start(
            out=ktab_sb[b][:, :], in_=cosk[:, c0:c0 + 512]),
            writes=[r_ktab[b]], dma=True, tag=f"ld_ck{tg}")
        sch.step("sp", lambda e, b=b, c0=c0: e.dma_start(
            out=stab_sb[b][:, :], in_=sink[:, c0:c0 + 512]),
            writes=[r_ktab[b]], dma=True, tag=f"ld_sk{tg}")
        if 'ln' not in parts:
            continue
        for c in range(8):
            sch.step("pe", MM(ps_ln[0:1, 0:512], onesv[:, 0:1],
                              xg[:, c * 512:(c + 1) * 512], start=(c == 0), stop=(c == 7)),
                     reads=[rxg, r_const], writes=([r_psln] if c == 0 else []),
                     tag=f"mmm{tg}_{c}")
        sch.group_end(r_psln)
        sch.step("dve", lambda e: e.tensor_scalar_mul(V(VM), ps_ln[0:1, 0:512], 1.0 / D),
                 reads=[r_psln], writes=[r_vec], tag=f"m{tg}")
        sch.step("pe", MM(ps_bc[:, 0:512], ones1[0:1, :], V(VM)),
                 reads=[r_vec, r_const], writes=[r_psbc0], tag=f"mbc{tg}")
        for c in range(8):
            sq = c % 2
            sch.step("act", lambda e, xg=xg, c=c, sq=sq: e.activation(
                xsq_sb[:, sq * 512:sq * 512 + 512], xg[:, c * 512:(c + 1) * 512], AF.Square),
                reads=[rxg], writes=[r_xsq[sq]], tag=f"sq{tg}_{c}")
            sch.step("pe", MM(ps_ln[0:1, 0:512], onesv[:, 0:1],
                              xsq_sb[:, sq * 512:sq * 512 + 512],
                              start=(c == 0), stop=(c == 7)),
                     reads=[r_xsq[sq], r_const], writes=([r_psln] if c == 0 else []),
                     tag=f"mms{tg}_{c}")
        sch.group_end(r_psln)
        sch.step("dve", lambda e: e.tensor_scalar_mul(V(VSD), ps_ln[0:1, 0:512], 1.0 / D),
                 reads=[r_psln], writes=[r_vec], tag=f"v1_{tg}")
        sch.step("dve", lambda e: e.tensor_mul(V(VA), V(VM), V(VM)),
                 reads=[r_vec], writes=[r_vec], tag=f"m2_{tg}")
        sch.step("dve", lambda e: e.tensor_sub(V(VSD), V(VSD), V(VA)),
                 reads=[r_vec], writes=[r_vec], tag=f"var{tg}")
        sch.step("act", lambda e: e.activation(V(VSD), V(VSD), AF.Ln, bias=1e-5),
                 reads=[r_vec], writes=[r_vec], tag=f"sd{tg}")
        sch.step("act", lambda e: e.activation(V(VR), V(VSD), AF.Exp, scale=-0.5),
                 reads=[r_vec], writes=[r_vec], tag=f"r{tg}")
        for c in range(8):
            sch.step("dve", lambda e, xg=xg, c=c: e.tensor_sub(
                xc_sb[:, c * 512:(c + 1) * 512], xg[:, c * 512:(c + 1) * 512],
                ps_bc[:, 0:512]),
                reads=[rxg, r_psbc0], writes=[r_xcc[c]], tag=f"cent{tg}_{c}")
        for mt in range(2 if 'kv' in parts else 0):
            for k in range(8):
                sch.step("pe", MM(ps_kv[:, mt * 512:mt * 512 + 512],
                                  wkv_sb[:, k * KVR + mt * 128:k * KVR + mt * 128 + 128],
                                  xc_sb[:, k * 512:(k + 1) * 512],
                                  start=(k == 0), stop=(k == 7)),
                         reads=[r_xcc[k], r_wkv],
                         writes=([r_pskv] if (mt == 0 and k == 0) else []),
                         tag=f"kv{tg}_{mt}_{k}")
        sch.group_end(r_pskv)
        for mt in range(2 if 'kv' in parts else 0):
            sch.step("dve", lambda e, mt=mt, c0=c0: e.tensor_copy(
                kv_sb[:, mt * T + c0:mt * T + c0 + 512], ps_kv[:, mt * 512:mt * 512 + 512]),
                reads=[r_pskv], writes=[r_kv], tag=f"kve{tg}_{mt}")
            sch.step("act", lambda e, mt=mt, c0=c0: e.activation(
                kvsq_sb[:, mt * 512:mt * 512 + 512],
                kv_sb[:, mt * T + c0:mt * T + c0 + 512], AF.Square),
                reads=[r_kv], writes=[r_kvsq], tag=f"kvq{tg}_{mt}")
        sch.step("pe", MM(ps_bc[:, 512:1024], ones1[0:1, :], V(VR)),
                 reads=[r_vec, r_const], writes=[r_psbc1], tag=f"rbc{tg}")
        sch.step("dve", lambda e, b=b: e.tensor_copy(rb_sb[b][:, :], ps_bc[:, 512:1024]),
                 reads=[r_psbc1], writes=[r_rb[b]], tag=f"rbe{tg}")
        for k in range(2 if 'kv' in parts else 0):
            sch.step("pe", MM(ps_ss[0:1, 0:512], onesv[:, 0:1],
                              kvsq_sb[:, k * 512:(k + 1) * 512], start=(k == 0), stop=(k == 1)),
                     reads=[r_kvsq, r_const], writes=([r_psss] if k == 0 else []),
                     tag=f"ss{tg}_{k}")
        sch.group_end(r_psss)
        if 'kv' not in parts:
            continue
        sch.step("dve", lambda e: e.tensor_scalar_mul(V(VMSQ), ps_ss[0:1, 0:512], 1.0 / KVR),
                 reads=[r_psss], writes=[r_vec], tag=f"msq{tg}")
        sch.step("dve", lambda e: e.tensor_mul(V(VA), V(VR), V(VR)),
                 reads=[r_vec], writes=[r_vec], tag=f"r2{tg}")
        sch.step("dve", lambda e: e.tensor_mul(V(VA), V(VA), V(VMSQ)),
                 reads=[r_vec], writes=[r_vec], tag=f"r2m{tg}")
        sch.step("act", lambda e: e.activation(V(VA), V(VA), AF.Ln, bias=1e-6),
                 reads=[r_vec], writes=[r_vec], tag=f"sqr{tg}")
        sch.step("act", lambda e: e.activation(V(VRHO), V(VA), AF.Exp, scale=-0.5),
                 reads=[r_vec], writes=[r_vec], tag=f"rec{tg}")
        sch.step("dve", lambda e: e.tensor_mul(V(VRHO), V(VRHO), V(VR)),
                 reads=[r_vec], writes=[r_vec], tag=f"rho{tg}")
        for mt in range(4 if 'qc' in parts else 0):
            sl = mt % 2
            rp = r_psqc[sl]
            for k in range(8):
                sch.step("pe", MM(ps_qc[:, sl * 512:sl * 512 + 512],
                                  wq_sb[:, k * QR + mt * 128:k * QR + mt * 128 + 128],
                                  xc_sb[:, k * 512:(k + 1) * 512],
                                  start=(k == 0), stop=(k == 7)),
                         reads=[r_xcc[k], r_wq], writes=([rp] if k == 0 else []),
                         tag=f"qc{tg}_{mt}_{k}")
            sch.group_end(rp)
            sch.step("dve", lambda e, mt=mt, sl=sl, b=b, c0=c0: e.tensor_mul(
                qc_sb[:, mt * T + c0:mt * T + c0 + 512],
                ps_qc[:, sl * 512:sl * 512 + 512], rb_sb[b][:, :]),
                reads=[rp, r_rb[b]], writes=[r_qc], tag=f"qce{tg}_{mt}")
        sch.step("pe", MM(ps_bc[:, 512:1024], ones1[0:1, :], V(VRHO)),
                 reads=[r_vec, r_const], writes=[r_psbc1], tag=f"rhobc{tg}")
        for mt in range(2):
            sch.step("dve", lambda e, mt=mt, c0=c0: e.tensor_mul(
                kv_sb[:, mt * T + c0:mt * T + c0 + 512],
                kv_sb[:, mt * T + c0:mt * T + c0 + 512], ps_bc[:, 512:1024]),
                reads=[r_kv, r_psbc1], writes=[r_kv], tag=f"kvn{tg}_{mt}")
        for g in range(4 if 'kr' in parts else 0):
            sl = g % 2
            rp = r_pskr[sl]
            for k in range(8):
                sch.step("pe", MM(ps_kr[:, sl * 512:sl * 512 + 512],
                                  wkr_sb[:, k * QR + g * 128:k * QR + g * 128 + 128],
                                  xc_sb[:, k * 512:(k + 1) * 512],
                                  start=(k == 0), stop=(k == 7)),
                         reads=[r_xcc[k], r_wkr], writes=([rp] if k == 0 else []),
                         tag=f"kr{tg}_{g}_{k}")
            sch.group_end(rp)
            sch.step("dve", lambda e, g=g, sl=sl, b=b, c0=c0: e.tensor_mul(
                kr_sb[:, g * T + c0:g * T + c0 + 512],
                ps_kr[:, sl * 512:sl * 512 + 512], rb_sb[b][:, :]),
                reads=[rp, r_rb[b]], writes=[r_kr], tag=f"kre{tg}_{g}")
            if 'rope' not in parts:
                continue
            sch.step("pe", MM(ps_rot[:, 0:512], r4t_sb[:, :],
                              kr_sb[:, g * T + c0:g * T + c0 + 512]),
                     reads=[r_kr, r_const], writes=[r_psrot], tag=f"rot{tg}_{g}")
            sch.step("dve", lambda e, g=g, b=b, c0=c0: e.tensor_mul(
                rot_scr[:, :], kr_sb[:, g * T + c0:g * T + c0 + 512], ktab_sb[b][:, :]),
                reads=[r_kr, r_ktab[b]], writes=[r_rsc], tag=f"t1_{tg}_{g}")
            sch.step("dve", lambda e, b=b: e.tensor_mul(
                rot_scr2[:, :], ps_rot[:, 0:512], stab_sb[b][:, :]),
                reads=[r_psrot, r_ktab[b]], writes=[r_rsc2], tag=f"u_{tg}_{g}")
            sch.step("dve", lambda e, g=g, c0=c0: e.tensor_add(
                kr_sb[:, g * T + c0:g * T + c0 + 512], rot_scr[:, :], rot_scr2[:, :]),
                reads=[r_rsc, r_rsc2], writes=[r_kr], tag=f"kh_{tg}_{g}")
    sch.flush()

    for mt in range(4):
        sch.step("sp", lambda e, mt=mt: e.dma_start(
            out=qc_dram[mt * 128:(mt + 1) * 128, :], in_=qc_sb[:, mt * T:(mt + 1) * T]),
            reads=[r_qc], dma=True, tag=f"qcspill{mt}")
    r_qcd = Reg("qcd")
    r_qcd.w = sch.steps[-1]

    t_kvn = tap("kv_norm", [2 * 128, T], BF16)
    t_qc = tap("q_c", [4 * 128, T], BF16)
    t_khat = tap("khat", [4 * 128, T], BF16)
    for name, tt, src_sb, reg, n in (("kv_norm", t_kvn, kv_sb, r_kv, 2),
                                     ("q_c", t_qc, qc_sb, r_qc, 4),
                                     ("khat", t_khat, kr_sb, r_kr, 4)):
        if tt is not None:
            for mt in range(n):
                sch.step("sp", lambda e, mt=mt, tt=tt, src_sb=src_sb: e.dma_start(
                    out=tt[mt * 128:(mt + 1) * 128, :], in_=src_sb[:, mt * T:(mt + 1) * T]),
                    reads=[reg], dma=True, tag=f"tap_{name}{mt}")
    sch.flush()
    es0.close()
    es_mid = ExitStack()
    attn_all = es_mid.enter_context(nc.sbuf_tensor("attn_all", [128, 8 * TOWN], BF16))
    wo_sb = es_mid.enter_context(nc.sbuf_tensor("wo_sb", [128, 8 * D], BF16))
    if stop_after < 3:
        es_mid.close()
        esL.close()
        es.close()
        return nc, tap_t

    es3 = ExitStack()
    S3 = es3.enter_context
    wkd_sb = S3(nc.sbuf_tensor("wkd_sb", [128, 2 * D], BF16))
    wvd_sb = S3(nc.sbuf_tensor("wvd_sb", [128, 2 * D], BF16))
    wqd_sb = S3(nc.sbuf_tensor("wqd_sb", [128, 4 * D], BF16))
    wqr_sb = S3(nc.sbuf_tensor("wqr_sb", [128, 4 * QR], BF16))
    r_wkd, r_wvd, r_wqd, r_wqr = Reg("wkd"), Reg("wvd"), Reg("wqd"), Reg("wqr")

    es3w = ExitStack()
    wst["t"] = es3w.enter_context(nc.sbuf_tensor("wstage3", [128, 2048], F32))
    wst["r"] = Reg("wstage3")
    wload("pl", wkd_sb, w_kdec, D, 2, r_wkd, "ld_wkd")
    wload("pl", wvd_sb, w_vdec, D, 2, r_wvd, "ld_wvd")
    wload("pl", wqd_sb, w_qdec, D, 4, r_wqd, "ld_wqd")
    wload("pl", wqr_sb, w_qrope, QR, 4, r_wqr, "ld_wqr")
    wload("pl", wo_sb, w_o, D, 8, r_wo, "ld_wo")
    for w_sbuf, reg in ((wkd_sb, r_wkd), (wvd_sb, r_wvd)):
        for k in range(2):
            sch.step("dve", lambda e, w_sbuf=w_sbuf, k=k: e.tensor_scalar_mul(
                w_sbuf[:, k * D:(k + 1) * D], w_sbuf[:, k * D:(k + 1) * D],
                gkv_sb[:, k:k + 1]),
                reads=[reg, r_gkv], writes=[reg], tag=f"gfold{k}")
    for k in range(4):
        sch.step("dve", lambda e, k=k: e.tensor_scalar_mul(
            wqd_sb[:, k * D:(k + 1) * D], wqd_sb[:, k * D:(k + 1) * D], SCALE),
            reads=[r_wqd], writes=[r_wqd], tag=f"sfold{k}")
    sch.flush()
    es3w.close()
    qhat_sb = S3(nc.sbuf_tensor("qhat_sb", [128, 4 * TOWN], BF16))
    qco_sb = S3(nc.sbuf_tensor("qco_sb", [128, 4 * TOWN], BF16))
    kbuf_sb = [S3(nc.sbuf_tensor(f"kbuf{i}", [96, T], BF16)) for i in range(2)]
    qbuf_sb = [S3(nc.sbuf_tensor(f"qbuf{i}", [96, TOWN], BF16)) for i in range(2)]
    v_sb = S3(nc.sbuf_tensor("v_sb", [128, 32 * 130], BF16))
    p_sb = [S3(nc.sbuf_tensor(f"p{i}", [128, 1024], BF16)) for i in range(2)]
    nrm_sb = S3(nc.sbuf_tensor("nrm_sb", [64, 512], F32))
    aschat = S3(nc.sbuf_tensor("aschat", [64, 512], BF16))
    qtab_sb = [S3(nc.sbuf_tensor(f"qtab{i}", [128, 512], F32)) for i in range(2)]
    rsc3 = S3(nc.sbuf_tensor("rsc3", [128, 512], BF16))
    rsc4 = S3(nc.sbuf_tensor("rsc4", [128, 512], BF16))
    recv_sb = S3(nc.sbuf_tensor("recv_sb", [1, 512], F32))
    ps_sc = [S3(nc.psum_tensor(f"ps_sc{i}", [128, 1024], F32)) for i in range(2)]
    ps_pv = S3(nc.psum_tensor("ps_pv", [65, 1024], F32))
    ps_decT = S3(nc.psum_tensor("ps_decT", [128, 1024], F32))
    r_qhat, r_qtab = Reg("qhat"), Reg("qtab")
    r_kbuf = [Reg("kb0"), Reg("kb1")]
    r_qbuf = [Reg("qb0"), Reg("qb1")]
    r_v = Reg("v")
    r_p = [Reg("p0"), Reg("p1")]
    r_nrm, r_aschat, r_recv = Reg("nrm"), Reg("aschat"), Reg("recv")
    r_rsc3, r_rsc4 = Reg("rsc3"), Reg("rsc4")
    r_pssc = [Reg("pssc0"), Reg("pssc1")]
    r_pspv = [Reg("pspv0"), Reg("pspv1")]
    r_psd = [Reg("psd0"), Reg("psd1")]
    r_qtabs = [Reg("qt0"), Reg("qt1")]

    snap0 = sch.snapshot()
    end_counts = []
    pid = nc.partition_id()
    engs = [nc.tensor, nc.vector, nc.scalar, nc.gpsimd, nc.sync]
    idx = nc.snap(pid % 2)
    for var in nc.Switch(engines=engs, index=[idx] * 5, n=2):
        sch.set_counts(snap0)
        chunks = CHUNKS[var]
        own0 = [q0 for q0, _ in chunks]
        r_qco = Reg("qco")

        def dslot(sl):
            return ps_decT[:, sl * 512:sl * 512 + 512]
        for mt in range(4):
            for cc in range(4):
                sch.step("sp", lambda e, mt=mt, cc=cc, q0=own0[cc]: e.dma_start(
                    out=qco_sb[:, mt * TOWN + cc * 512:mt * TOWN + cc * 512 + 512],
                    in_=qc_dram[mt * 128:(mt + 1) * 128, q0:q0 + 512]),
                    reads=[r_qcd], writes=[r_qco], dma=True, tag=f"qcl{var}_{mt}_{cc}")

        for cc in range(4):
            tb = cc % 2
            q0 = own0[cc]
            sch.step("sp", lambda e, cc=cc: e.dma_start(
                out=qtab_sb[0][:, :], in_=cosqo[:, cc * 512:cc * 512 + 512]),
                writes=[r_qtabs[0]], dma=True, tag=f"qtc{var}_{cc}")
            sch.step("sp", lambda e, cc=cc: e.dma_start(
                out=qtab_sb[1][:, :], in_=sinqo[:, cc * 512:cc * 512 + 512]),
                writes=[r_qtabs[1]], dma=True, tag=f"qts{var}_{cc}")
            for g in range(4):
                sl = g % 2
                rp = r_psd[sl]
                for k in range(4):
                    sch.step("pe", MM(dslot(sl),
                                      wqr_sb[:, k * QR + g * 128:k * QR + g * 128 + 128],
                                      qco_sb[:, k * TOWN + cc * 512:k * TOWN + cc * 512 + 512],
                                      start=(k == 0), stop=(k == 3)),
                             reads=[r_qco, r_wqr], writes=([rp] if k == 0 else []),
                             tag=f"qr{var}_{g}_{cc}")
                sch.group_end(rp)
                sch.step("dve", lambda e, g=g, cc=cc, sl=sl: e.tensor_copy(
                    qhat_sb[:, g * TOWN + cc * 512:g * TOWN + cc * 512 + 512], dslot(sl)),
                    reads=[rp], writes=[r_qhat], tag=f"qre{var}_{g}_{cc}")
                sch.step("pe", MM(dslot(sl), r4t_sb[:, :],
                                  qhat_sb[:, g * TOWN + cc * 512:g * TOWN + cc * 512 + 512]),
                         reads=[r_qhat, r_const], writes=[rp], tag=f"qrot{var}_{g}_{cc}")
                sch.step("dve", lambda e, g=g, cc=cc, tb=tb: e.tensor_mul(
                    rsc3[:, :],
                    qhat_sb[:, g * TOWN + cc * 512:g * TOWN + cc * 512 + 512],
                    qtab_sb[0][:, :]),
                    reads=[r_qhat, r_qtabs[0]], writes=[r_rsc3], tag=f"qt1{var}_{g}_{cc}")
                sch.step("dve", lambda e, sl=sl: e.tensor_mul(
                    rsc4[:, :], dslot(sl), qtab_sb[1][:, :]),
                    reads=[rp, r_qtabs[1]], writes=[r_rsc4], tag=f"qu{var}_{g}_{cc}")
                sch.step("dve", lambda e, g=g, cc=cc: e.tensor_add(
                    qhat_sb[:, g * TOWN + cc * 512:g * TOWN + cc * 512 + 512],
                    rsc3[:, :], rsc4[:, :]),
                    reads=[r_rsc3, r_rsc4], writes=[r_qhat], tag=f"qh{var}_{g}_{cc}")

        def decode_head_gen(h, slot):
            kb_, qb_ = kbuf_sb[slot], qbuf_sb[slot]
            rkb, rqb = r_kbuf[slot], r_qbuf[slot]
            for nc8 in range(8):
                sl = nc8 % 2
                for k in range(2):
                    sch.step("pe", MM(dslot(sl)[0:64, :],
                                      wkd_sb[:, k * D + h * 64:k * D + h * 64 + 64],
                                      kv_sb[:, k * T + nc8 * 512:k * T + nc8 * 512 + 512],
                                      start=(k == 0), stop=(k == 1)),
                             reads=[r_kv, r_wkd],
                             writes=([r_psd[sl]] if k == 0 else []), tag=f"kd{var}_{h}_{nc8}")
                sch.group_end(r_psd[sl])
                sch.step("dve", lambda e, nc8=nc8, kb_=kb_, sl=sl: e.tensor_copy(
                    kb_[0:64, nc8 * 512:nc8 * 512 + 512], dslot(sl)[0:64, :]),
                    reads=[r_psd[sl]], writes=[rkb], tag=f"kde{var}_{h}_{nc8}")
                yield
            G, gsl = h // 4, h % 4
            if gsl == 2:
                sch.step("dve", lambda e, G=G: e.tensor_copy(
                    kb_[64:96, :], kr_sb[64:96, G * T:(G + 1) * T]),
                    reads=[r_kr], writes=[rkb], tag=f"kcp{var}_{h}")
                yield
            else:
                ci = {0: 0, 1: 1, 3: 2}[gsl]
                for nc8 in range(8):
                    sl = nc8 % 2
                    sch.step("pe", MM(dslot(sl)[64:96, :],
                                      psel_sb[:, ci * 32:(ci + 1) * 32],
                                      kr_sb[:, G * T + nc8 * 512:G * T + nc8 * 512 + 512],
                                      tp=(0, 64)),
                             reads=[r_kr, r_const], writes=[r_psd[sl]], tag=f"kpm{var}_{h}_{nc8}")
                    sch.step("dve", lambda e, nc8=nc8, kb_=kb_, sl=sl: e.tensor_copy(
                        kb_[64:96, nc8 * 512:nc8 * 512 + 512], dslot(sl)[64:96, :]),
                        reads=[r_psd[sl]], writes=[rkb], tag=f"kpe{var}_{h}_{nc8}")
                    yield
            for cc in range(4):
                sl = cc % 2
                for k in range(4):
                    sch.step("pe", MM(dslot(sl)[0:64, :],
                                      wqd_sb[:, k * D + h * 64:k * D + h * 64 + 64],
                                      qco_sb[:, k * TOWN + cc * 512:k * TOWN + cc * 512 + 512],
                                      start=(k == 0), stop=(k == 3)),
                             reads=[r_qco, r_wqd],
                             writes=([r_psd[sl]] if k == 0 else []), tag=f"qd{var}_{h}_{cc}")
                sch.group_end(r_psd[sl])
                sch.step("dve", lambda e, cc=cc, qb_=qb_, sl=sl: e.tensor_copy(
                    qb_[0:64, cc * 512:cc * 512 + 512], dslot(sl)[0:64, :]),
                    reads=[r_psd[sl]], writes=[rqb], tag=f"qde{var}_{h}_{cc}")
                yield
            if gsl == 2:
                sch.step("dve", lambda e, G=G, qb_=qb_: e.tensor_copy(
                    qb_[64:96, :], qhat_sb[64:96, G * TOWN:(G + 1) * TOWN]),
                    reads=[r_qhat], writes=[rqb], tag=f"qcp{var}_{h}")
                yield
            else:
                ci = {0: 0, 1: 1, 3: 2}[gsl]
                for cc in range(4):
                    sl = cc % 2
                    sch.step("pe", MM(dslot(sl)[64:96, :],
                                      psel_sb[:, ci * 32:(ci + 1) * 32],
                                      qhat_sb[:, G * TOWN + cc * 512:G * TOWN + cc * 512 + 512],
                                      tp=(0, 64)),
                             reads=[r_qhat, r_const], writes=[r_psd[sl]], tag=f"qpm{var}_{h}_{cc}")
                    sch.step("dve", lambda e, cc=cc, qb_=qb_, sl=sl: e.tensor_copy(
                        qb_[64:96, cc * 512:cc * 512 + 512], dslot(sl)[64:96, :]),
                        reads=[r_psd[sl]], writes=[rqb], tag=f"qpe{var}_{h}_{cc}")
                    yield

        def decode_v_gen(hg):
            for tt in range(32):
                sl = tt % 2
                for k in range(2):
                    sch.step("pe", MM(dslot(sl)[:, 0:128],
                                      kv_sb[:, k * T + tt * 128:k * T + tt * 128 + 128],
                                      wvd_sb[:, k * D + hg * 128:k * D + hg * 128 + 128],
                                      start=(k == 0), stop=(k == 1)),
                             reads=[r_kv, r_wvd],
                             writes=([r_psd[sl]] if k == 0 else []), tag=f"v{var}_{hg}_{tt}")
                sch.group_end(r_psd[sl])
                sch.step("dve", lambda e, tt=tt, sl=sl: e.tensor_copy(
                    bass.AP(v_sb, tt * 130, [[32 * 130, 128], [65, 2], [1, 64]]),
                    bass.AP(ps_decT, sl * 512, [[1024, 128], [64, 2], [1, 64]])),
                    reads=[r_psd[sl]], writes=[r_v], tag=f"ve{var}_{hg}_{tt}")
                if tt % 2 == 1:
                    yield
            sch.step("pl", lambda e: e.memset(
                bass.AP(v_sb, 64, [[32 * 130, 128], [130, 32], [65, 2], [1, 1]]), 1.0),
                writes=[r_v], tag=f"vones{var}_{hg}")

        def pump(g, k=1):
            if g is None:
                return None
            try:
                for _ in range(k):
                    next(g)
                return g
            except StopIteration:
                return None

        def drain(g):
            if g is not None:
                for _ in g:
                    pass

        for _ in decode_head_gen(0, 0):
            pass
        filler = None
        for h in range(NH):
            slot = h % 2
            kb_, qb_ = kbuf_sb[slot], qbuf_sb[slot]
            rkb, rqb = r_kbuf[slot], r_qbuf[slot]
            drain(filler)
            if h % 2 == 0:
                drain(decode_v_gen(h // 2))
            filler = decode_head_gen(h + 1, (h + 1) % 2) if h + 1 < NH else None
            hl = h % 2
            pending_norm = []
            for cc in range(4):
                while len(pending_norm) > 1:
                    pending_norm.pop(0)()
                q0c, nkt = chunks[cc]
                nb4 = (nkt + 1) // 2
                pends = []
                for bb in range(nb4):
                    if len(pends) >= 2:
                        emit_pv_q(pends.pop(0))
                    psl = bb % 2
                    psc, rsc_ = ps_sc[psl], r_pssc[psl]
                    pb, rpb = p_sb[psl], r_p[psl]
                    kts = [kt for kt in range(2 * bb, 2 * bb + 2) if kt < nkt]
                    for j, kt in enumerate(kts):
                        sch.step("pe", MM(psc[:, j * 512:j * 512 + 512],
                                          kb_[:, kt * 128:kt * 128 + 128],
                                          qb_[:, cc * 512:cc * 512 + 512]),
                                 reads=[rkb, rqb],
                                 writes=([rsc_] if j == 0 else []),
                                 tag=f"sc{var}_{h}_{cc}_{kt}")
                    sch.group_end(rsc_)
                    wid = 512 * len(kts)
                    sch.step("act", lambda e, psc=psc, pb=pb, wid=wid: e.activation(
                        pb[:, 0:wid], psc[:, 0:wid], AF.Exp, bias=EXP_BIAS),
                        reads=[rsc_], writes=[rpb], tag=f"ex{var}_{h}_{cc}_{bb}")
                    filler = pump(filler)
                    if bb == nb4 - 1 and pending_norm:
                        pending_norm.pop(0)()
                    for j, kt in enumerate(kts):
                        mj = kt - (nkt - 4)
                        if 0 <= mj < 4:
                            sch.step("pl", lambda e, pb=pb, j=j, mj=mj: e.tensor_mul(
                                pb[:, j * 512:j * 512 + 512], pb[:, j * 512:j * 512 + 512],
                                mask_sb[:, mj * 512:mj * 512 + 512]),
                                reads=[rpb, r_const], writes=[rpb],
                                tag=f"mk{var}_{h}_{cc}_{kt}")
                    pvsl = cc % 2

                    def emit_pv_q(pend, pvsl=pvsl, nkt=nkt):
                        ppsl, pkts = pend
                        for j, kt in enumerate(pkts):
                            sch.step("pe", MM(ps_pv[:, pvsl * 512:pvsl * 512 + 512],
                                              bass.AP(v_sb, kt * 130 + hl * 65,
                                                      [[32 * 130, 128], [1, 65]]),
                                              p_sb[ppsl][:, j * 512:j * 512 + 512],
                                              start=(kt == 0), stop=(kt == nkt - 1)),
                                     reads=[r_p[ppsl], r_v],
                                     writes=([r_pspv[pvsl]] if kt == 0 else []),
                                     tag=f"pv{var}_{h}_{cc}_{kt}")
                    pends.append((psl, kts))
                pvsl = cc % 2
                while pends:
                    emit_pv_q(pends.pop(0))
                sch.group_end(r_pspv[pvsl])
                def emit_norm(h=h, cc=cc, pvsl=pvsl):
                    sch.step("act", lambda e, pvsl=pvsl: e.activation(
                        recv_sb[0:1, :], ps_pv[64:65, pvsl * 512:pvsl * 512 + 512], AF.Ln),
                        reads=[r_pspv[pvsl]], writes=[r_recv], tag=f"rl{var}_{h}_{cc}")
                    sch.step("act", lambda e: e.activation(
                        recv_sb[0:1, :], recv_sb[0:1, :], AF.Exp, scale=-1.0),
                        reads=[r_recv], writes=[r_recv], tag=f"rc{var}_{h}_{cc}")
                    sch.step("pe", MM(ps_decT[0:64, 512:1024], ones1[0:1, 0:64],
                                      recv_sb[0:1, :]),
                             reads=[r_recv, r_const], writes=[r_psd[1]],
                             tag=f"nb{var}_{h}_{cc}")
                    sch.step("dve", lambda e: e.tensor_copy(nrm_sb[:, :],
                                                            ps_decT[0:64, 512:1024]),
                        reads=[r_psd[1]], writes=[r_nrm], tag=f"nbe{var}_{h}_{cc}")
                    if h % 2 == 0:
                        sch.step("dve", lambda e, h=h, cc=cc, pvsl=pvsl: e.tensor_mul(
                            attn_all[0:64,
                                     (h // 2) * TOWN + cc * 512:(h // 2) * TOWN + cc * 512 + 512],
                            ps_pv[0:64, pvsl * 512:pvsl * 512 + 512], nrm_sb[:, :]),
                            reads=[r_pspv[pvsl], r_nrm], writes=[r_attnall[h]],
                            tag=f"an{var}_{h}_{cc}")
                    else:
                        sch.step("dve", lambda e, pvsl=pvsl: e.tensor_mul(
                            aschat[:, :], ps_pv[0:64, pvsl * 512:pvsl * 512 + 512],
                            nrm_sb[:, :]),
                            reads=[r_pspv[pvsl], r_nrm], writes=[r_aschat],
                            tag=f"an{var}_{h}_{cc}")
                        sch.step("sp", lambda e, h=h, cc=cc: e.dma_start(
                            out=attn_all[64:128,
                                         (h // 2) * TOWN + cc * 512:(h // 2) * TOWN + cc * 512 + 512],
                            in_=aschat[:, :]),
                            reads=[r_aschat], writes=[r_attnall[h]], dma=True,
                            tag=f"as{var}_{h}_{cc}")
                pending_norm.append(emit_norm)
            while pending_norm:
                pending_norm.pop(0)()
        sch.flush()
        end_counts.append(sch.snapshot())
    assert end_counts[0] == end_counts[1], (
        "switch bodies must produce identical semaphore counts",
        {k: (end_counts[0][k], end_counts[1][k]) for k in end_counts[0]
         if end_counts[0][k] != end_counts[1][k]})
    sch.set_counts(end_counts[0])

    t_attn = tap("attn_pre", [128, 8 * TOWN], BF16)
    if t_attn is not None:
        sch.step("sp", lambda e: e.dma_start(out=t_attn[:, :], in_=attn_all[:, :]),
                 deps=[r for h in range(NH) for r in ([r_attnall[h].w] if r_attnall[h].w else [])],
                 dma=True, tag="tap_attn")
    sch.flush()
    es3.close()
    if stop_after < 4:
        es_mid.close()
        esL.close()
        es.close()
        return nc, tap_t

    es4 = ExitStack()
    S4 = es4.enter_context
    xo_sb = [S4(nc.sbuf_tensor(f"xo{i}", [128, 8 * 512], F32)) for i in range(2)]
    po_sb = S4(nc.sbuf_tensor("po_sb", [128, 512], F32))
    ps_wo = S4(nc.psum_tensor("ps_wo", [128, 1024], F32))
    ps_wo2 = S4(nc.psum_tensor("ps_wo2", [128, 1024], F32))
    r_xo = [Reg("xo0"), Reg("xo1")]
    r_po = Reg("po")
    r_pswo = [Reg("pswo0"), Reg("pswo1")]
    r_pswo2 = [Reg("pswo20"), Reg("pswo21")]

    for cc in range(4):
        b = cc % 2
        sch.step("sp", lambda e, cc=cc, b=b: e.dma_start(
            out=bass.AP(xo_sb[b], 0, [[8 * 512, 128], [512, 8], [1, 512]]),
            in_=bass.AP(xTo.tensor, cc * 512, [[TOWN, 128], [128 * TOWN, 8], [1, 512]])),
            writes=[r_xo[b]], dma=True, tag=f"ld_xo{cc}")
        for dchunk in range(8):
            sl = dchunk % 2
            rp, rp2 = r_pswo[sl], r_pswo2[sl]
            for j, h in enumerate(range(0, NH, 2)):
                lhs = wo_sb[0:64,
                            (h // 2) * D + dchunk * 128:(h // 2) * D + dchunk * 128 + 128]
                rhs = attn_all[0:64,
                               (h // 2) * TOWN + cc * 512:(h // 2) * TOWN + cc * 512 + 512]
                sch.step("pe", MM(ps_wo[:, sl * 512:sl * 512 + 512], lhs, rhs,
                                  start=(j == 0), stop=(j == 7)),
                         reads=[r_wo, r_attnall[h]],
                         writes=([rp] if j == 0 else []), tag=f"wo{cc}_{dchunk}_{h}")
            sch.group_end(rp)
            for j, h in enumerate(range(1, NH, 2)):
                lhs = wo_sb[64:128,
                            (h // 2) * D + dchunk * 128:(h // 2) * D + dchunk * 128 + 128]
                rhs = attn_all[64:128,
                               (h // 2) * TOWN + cc * 512:(h // 2) * TOWN + cc * 512 + 512]
                sch.step("pe", MM(ps_wo2[:, sl * 512:sl * 512 + 512], lhs, rhs,
                                  start=(j == 0), stop=(j == 7)),
                         reads=[r_wo, r_attnall[h]],
                         writes=([rp2] if j == 0 else []), tag=f"wo2{cc}_{dchunk}_{h}")
            sch.group_end(rp2)
            sch.step("dve", lambda e, dchunk=dchunk, sl=sl, b=b: e.tensor_add(
                po_sb[:, :], ps_wo[:, sl * 512:sl * 512 + 512],
                xo_sb[b][:, dchunk * 512:dchunk * 512 + 512]),
                reads=[rp, r_xo[b]], writes=[r_po], tag=f"poa{cc}_{dchunk}")
            sch.step("dve", lambda e, sl=sl: e.scalar_tensor_tensor(
                po_sb[:, :], ps_wo2[:, sl * 512:sl * 512 + 512], 1.0, po_sb[:, :],
                ALU.mult, ALU.add),
                reads=[rp2, r_po], writes=[r_po], tag=f"pob{cc}_{dchunk}")
            sch.step("sp", lambda e, dchunk=dchunk, cc=cc: e.dma_start(
                out=outT[dchunk * 128:(dchunk + 1) * 128, cc * 512:cc * 512 + 512],
                in_=po_sb[:, :]),
                reads=[r_po], dma=True, tag=f"st_p{cc}_{dchunk}")
    sch.flush()
    es4.close()
    es_mid.close()
    esL.close()
    if stop_after < 5:
        es.close()
        return nc, tap_t


    es5 = ExitStack()
    S5 = es5.enter_context
    xn_sb = S5(nc.sbuf_tensor("xn_sb", [128, 8 * TOWN], BF16))
    h_sb = S5(nc.sbuf_tensor("h_sb", [128, 24 * 1024], BF16))
    silu_sb = S5(nc.sbuf_tensor("silu_sb", [128, 1024], BF16))
    wg_sb = [S5(nc.sbuf_tensor(f"wg{i}", [128, 8 * 384], BF16)) for i in range(2)]
    wu_sb = [S5(nc.sbuf_tensor(f"wu{i}", [128, 8 * 384], BF16)) for i in range(2)]
    wd_sb = [S5(nc.sbuf_tensor(f"wd{i}", [128, 24 * 128], BF16)) for i in range(2)]
    xo5_sb = [S5(nc.sbuf_tensor(f"xo5{i}", [128, 8 * 512], BF16)) for i in range(2)]
    xf5_sb = [S5(nc.sbuf_tensor(f"xf5{i}", [128, 4 * 512], F32)) for i in range(2)]
    xsq5 = S5(nc.sbuf_tensor("xsq5", [128, 2 * 512], BF16))
    pr_sb = [S5(nc.sbuf_tensor(f"pr{i}", [128, 512], F32)) for i in range(2)]
    ob_sb = S5(nc.sbuf_tensor("ob_sb", [128, 512], F32))
    vec5 = S5(nc.sbuf_tensor("vec5", [1, 3 * 512], F32))
    ps5_bc = S5(nc.psum_tensor("ps5_bc", [128, 1024], F32))
    es5a = ExitStack()
    ps5_ln = es5a.enter_context(nc.psum_tensor("ps5_ln", [1, 1024], F32))
    r_xn, r_h, r_silu = Reg("xn"), Reg("h"), Reg("silu")
    r_wg = [Reg("wg0"), Reg("wg1")]
    r_wu = [Reg("wu0"), Reg("wu1")]
    r_wd = [Reg("wd0"), Reg("wd1")]
    r_xo5 = [Reg("xo50"), Reg("xo51")]
    r_xf5 = [Reg("xf50"), Reg("xf51")]
    r_xsq5 = [Reg("xsq50"), Reg("xsq51")]
    r_pr = [Reg("pr0"), Reg("pr1")]
    r_ob, r_vec5 = Reg("ob"), Reg("vec5")
    r_ps5ln, r_ps5bc0, r_ps5bc1 = Reg("ps5ln"), Reg("ps5bc0"), Reg("ps5bc1")
    r_ps5g, r_ps5u = Reg("ps5g"), Reg("ps5u")
    r_ps5d = [Reg("ps5d0"), Reg("ps5d1")]

    def V5(i):
        return vec5[0:1, i * 512:(i + 1) * 512]

    for cc in range(4):
        b = cc % 2
        for hh2 in range(2):
            xfb = (2 * cc + hh2) % 2
            sch.step("sp", lambda e, cc=cc, hh2=hh2, xfb=xfb: e.dma_start(
                out=bass.AP(xf5_sb[xfb], 0, [[4 * 512, 128], [512, 4], [1, 512]]),
                in_=bass.AP(xTo.tensor, cc * 512 + hh2 * 4 * 128 * TOWN,
                            [[TOWN, 128], [128 * TOWN, 4], [1, 512]])),
                writes=[r_xf5[xfb]], dma=True, tag=f"l5x{cc}_{hh2}")
            sch.step("act", lambda e, b=b, hh2=hh2, xfb=xfb: e.activation(
                xo5_sb[b][:, hh2 * 2048:(hh2 + 1) * 2048], xf5_sb[xfb][:, :], AF.Copy),
                reads=[r_xf5[xfb]], writes=[r_xo5[b]], tag=f"cx5{cc}_{hh2}")
        for c in range(8):
            sch.step("pe", MM(ps5_ln[0:1, 0:512], onesv[:, 0:1],
                              xo5_sb[b][:, c * 512:(c + 1) * 512],
                              start=(c == 0), stop=(c == 7)),
                     reads=[r_xo5[b], r_const], writes=([r_ps5ln] if c == 0 else []),
                     tag=f"l5m{cc}_{c}")
        sch.group_end(r_ps5ln)
        sch.step("dve", lambda e: e.tensor_scalar_mul(V5(0), ps5_ln[0:1, 0:512], 1.0 / D),
                 reads=[r_ps5ln], writes=[r_vec5], tag=f"l5mm{cc}")
        sch.step("pe", MM(ps5_bc[:, 0:512], ones1[0:1, :], V5(0)),
                 reads=[r_vec5, r_const], writes=[r_ps5bc0], tag=f"l5bc{cc}")
        for c in range(8):
            sq = c % 2
            sch.step("act", lambda e, b=b, c=c, sq=sq: e.activation(
                xsq5[:, sq * 512:sq * 512 + 512], xo5_sb[b][:, c * 512:(c + 1) * 512],
                AF.Square),
                reads=[r_xo5[b]], writes=[r_xsq5[sq]], tag=f"l5s{cc}_{c}")
            sch.step("pe", MM(ps5_ln[0:1, 512:1024], onesv[:, 0:1],
                              xsq5[:, sq * 512:sq * 512 + 512],
                              start=(c == 0), stop=(c == 7)),
                     reads=[r_xsq5[sq], r_const], writes=([r_ps5ln] if c == 0 else []),
                     tag=f"l5q{cc}_{c}")
        sch.group_end(r_ps5ln)
        sch.step("dve", lambda e: e.tensor_scalar_mul(V5(1), ps5_ln[0:1, 512:1024], 1.0 / D),
                 reads=[r_ps5ln], writes=[r_vec5], tag=f"l5v{cc}")
        sch.step("dve", lambda e: e.tensor_mul(V5(2), V5(0), V5(0)),
                 reads=[r_vec5], writes=[r_vec5], tag=f"l5m2{cc}")
        sch.step("dve", lambda e: e.tensor_sub(V5(1), V5(1), V5(2)),
                 reads=[r_vec5], writes=[r_vec5], tag=f"l5var{cc}")
        sch.step("act", lambda e: e.activation(V5(1), V5(1), AF.Ln, bias=1e-5),
                 reads=[r_vec5], writes=[r_vec5], tag=f"l5sd{cc}")
        sch.step("act", lambda e: e.activation(V5(2), V5(1), AF.Exp, scale=-0.5),
                 reads=[r_vec5], writes=[r_vec5], tag=f"l5r{cc}")
        sch.step("pe", MM(ps5_bc[:, 512:1024], ones1[0:1, :], V5(2)),
                 reads=[r_vec5, r_const], writes=[r_ps5bc1], tag=f"l5rb{cc}")
        sch.step("dve", lambda e, b=b, cc=cc: e.tensor_sub(
            bass.AP(xn_sb, cc * 512, [[8 * TOWN, 128], [TOWN, 8], [1, 512]]),
            bass.AP(xo5_sb[b], 0, [[8 * 512, 128], [512, 8], [1, 512]]),
            bass.AP(ps5_bc, 0, [[1024, 128], [0, 8], [1, 512]])),
            reads=[r_xo5[b], r_ps5bc0], writes=[r_xn], tag=f"l5c{cc}")
        sch.step("dve", lambda e, cc=cc: e.tensor_mul(
            bass.AP(xn_sb, cc * 512, [[8 * TOWN, 128], [TOWN, 8], [1, 512]]),
            bass.AP(xn_sb, cc * 512, [[8 * TOWN, 128], [TOWN, 8], [1, 512]]),
            bass.AP(ps5_bc, 512, [[1024, 128], [0, 8], [1, 512]])),
            reads=[r_xn, r_ps5bc1], writes=[r_xn], tag=f"l5n{cc}")
    sch.flush()

    es5a.close()
    ps5_g = S5(nc.psum_tensor("ps5_g", [128, 1024], F32))
    ps5_u = S5(nc.psum_tensor("ps5_u", [128, 1024], F32))
    ps5_d = S5(nc.psum_tensor("ps5_d", [128, 1024], F32))
    r_outT = Reg("outT")
    last_store = None
    for tw in range(2):
        for hb in (range(8) if tw == 0 else range(7, -1, -1)):
            wbuf = hb % 2
            if not (tw == 1 and hb >= 6):
                sch.step("pl", lambda e, hb=hb, wbuf=wbuf: e.dma_start(
                    out=bass.AP(wg_sb[wbuf], 0, [[8 * 384, 128], [384, 8], [1, 384]]),
                    in_=bass.AP(wgT.tensor, hb * 384, [[HID, 128], [128 * HID, 8], [1, 384]])),
                    writes=[r_wg[wbuf]], dma=True, tag=f"f_wg{tw}_{hb}")
                sch.step("pl", lambda e, hb=hb, wbuf=wbuf: e.dma_start(
                    out=bass.AP(wu_sb[wbuf], 0, [[8 * 384, 128], [384, 8], [1, 384]]),
                    in_=bass.AP(wuT.tensor, hb * 384, [[HID, 128], [128 * HID, 8], [1, 384]])),
                    writes=[r_wu[wbuf]], dma=True, tag=f"f_wu{tw}_{hb}")
            for j in range(3):
                hh = 3 * hb + j
                for nn in range(2):
                    for k in range(8):
                        sch.step("pe", MM(ps5_g[:, nn * 512:nn * 512 + 512],
                                          wg_sb[wbuf][:, k * 384 + j * 128:k * 384 + j * 128 + 128],
                                          xn_sb[:, k * TOWN + tw * 1024 + nn * 512:
                                                k * TOWN + tw * 1024 + nn * 512 + 512],
                                          start=(k == 0), stop=(k == 7)),
                                 reads=[r_wg[wbuf], r_xn],
                                 writes=([r_ps5g] if (nn == 0 and k == 0) else []),
                                 tag=f"fg{tw}_{hh}_{nn}_{k}")
                sch.group_end(r_ps5g)
                sch.step("act", lambda e: e.activation(
                    silu_sb[:, :], ps5_g[:, :], AF.Silu, scale=FFS),
                    reads=[r_ps5g], writes=[r_silu], tag=f"fs{tw}_{hh}")
                for nn in range(2):
                    for k in range(8):
                        sch.step("pe", MM(ps5_u[:, nn * 512:nn * 512 + 512],
                                          wu_sb[wbuf][:, k * 384 + j * 128:k * 384 + j * 128 + 128],
                                          xn_sb[:, k * TOWN + tw * 1024 + nn * 512:
                                                k * TOWN + tw * 1024 + nn * 512 + 512],
                                          start=(k == 0), stop=(k == 7)),
                                 reads=[r_wu[wbuf], r_xn],
                                 writes=([r_ps5u] if (nn == 0 and k == 0) else []),
                                 tag=f"fu{tw}_{hh}_{nn}_{k}")
                sch.group_end(r_ps5u)
                sch.step("dve", lambda e, hh=hh: e.scalar_tensor_tensor(
                    h_sb[:, hh * 1024:(hh + 1) * 1024], ps5_u[:, :], FFS, silu_sb[:, :],
                    ALU.mult, ALU.mult),
                    reads=[r_ps5u, r_silu], writes=[r_h], tag=f"fh{tw}_{hh}")
        def wd_load(dchunk):
            dbuf = dchunk % 2
            sch.step("pl", lambda e, dchunk=dchunk, dbuf=dbuf: e.dma_start(
                out=bass.AP(wd_sb[dbuf], 0, [[24 * 128, 128], [128, 24], [1, 128]]),
                in_=bass.AP(wdT.tensor, dchunk * 128, [[D, 128], [128 * D, 24], [1, 128]])),
                writes=[r_wd[dbuf]], dma=True, tag=f"f_wd{tw}_{dchunk}")
        dcs = list(range(8)) if tw == 0 else list(range(7, -1, -1))
        if tw == 0:
            wd_load(dcs[0])
        for di, dchunk in enumerate(dcs):
            dbuf = dchunk % 2
            if di + 1 < 8 and not (tw == 1 and dcs[di + 1] >= 6):
                wd_load(dcs[di + 1])
            for nn in range(2):
                sl = nn
                for k in range(24):
                    sch.step("pe", MM(ps5_d[:, sl * 512:sl * 512 + 512],
                                      wd_sb[dbuf][:, k * 128:(k + 1) * 128],
                                      h_sb[:, k * 1024 + nn * 512:k * 1024 + nn * 512 + 512],
                                      start=(k == 0), stop=(k == 23)),
                             reads=[r_wd[dbuf], r_h],
                             writes=([r_ps5d[sl]] if k == 0 else []),
                             tag=f"fd{tw}_{dchunk}_{nn}_{k}")
                sch.group_end(r_ps5d[sl])
                pbuf = nn
                sch.step("sp", lambda e, dchunk=dchunk, tw=tw, nn=nn, pbuf=pbuf: e.dma_start(
                    out=pr_sb[pbuf][:, :],
                    in_=outT[dchunk * 128:(dchunk + 1) * 128,
                             tw * 1024 + nn * 512:tw * 1024 + nn * 512 + 512]),
                    reads=[r_outT], writes=[r_pr[pbuf]], dma=True, tag=f"fp{tw}_{dchunk}_{nn}")
                sch.step("dve", lambda e, sl=sl, pbuf=pbuf: e.scalar_tensor_tensor(
                    ob_sb[:, :], ps5_d[:, sl * 512:sl * 512 + 512], FFS, pr_sb[pbuf][:, :],
                    ALU.mult, ALU.add),
                    reads=[r_ps5d[sl], r_pr[pbuf]], writes=[r_ob], tag=f"fo{tw}_{dchunk}_{nn}")
                last_store = sch.step("sp", lambda e, dchunk=dchunk, tw=tw, nn=nn: e.dma_start(
                    out=outT[dchunk * 128:(dchunk + 1) * 128,
                             tw * 1024 + nn * 512:tw * 1024 + nn * 512 + 512],
                    in_=ob_sb[:, :]),
                    reads=[r_ob], writes=[r_outT], dma=True, tag=f"fst{tw}_{dchunk}_{nn}")
    sch.step("sp", lambda e: e.nop(), deps=[last_store], tag="final_wait")
    sch.flush()
    es5.close()
    es.close()
    return nc, tap_t


def make_core_inputs(ins, const, b, core_type):
    x = ins["L0_attn_x"][b]
    own = own_positions(core_type)
    cosq = np.ascontiguousarray(
        (np.tile(const["cos_t"], (4, 1))[:, own] * np.float32(SCALE)).astype(np.float32))
    sinq = np.ascontiguousarray(
        (np.tile(const["sin_t"], (4, 1))[:, own] * np.float32(SCALE)).astype(np.float32))
    return {
        "xT": np.ascontiguousarray(x.T),
        "xTo": np.ascontiguousarray(x.T[:, own]),
        "w_kv": ins["L0_attn_W_kv"], "w_q": ins["L0_attn_W_q"],
        "w_kdec": ins["L0_attn_W_kdec"], "w_vdec": ins["L0_attn_W_vdec"],
        "w_qdec": ins["L0_attn_W_qdec"], "w_krope": ins["L0_attn_W_krope"],
        "w_qrope": ins["L0_attn_W_qrope"], "w_o": ins["L0_attn_W_o"],
        "g_kv": ins["L0_attn_g_kv"],
        "wgT": np.ascontiguousarray(ins["L0_ffn_w_gate"].T),
        "wuT": np.ascontiguousarray(ins["L0_ffn_w_up"].T),
        "wdT": np.ascontiguousarray(ins["L0_ffn_w_down"].T),
        "cosk": const["cosk"], "sink": const["sink"],
        "cosqo": cosq, "sinqo": sinq,
        "maskc": const["maskc"], "r4t": const["r4t"], "psel": const["psel"],
    }


_NC_CACHE = {}


def kernel(**inputs):
    from concourse.bass_utils import run_bass_kernel_spmd
    if "nc" not in _NC_CACHE:
        _NC_CACHE["nc"] = build_nc()[0]
    nc = _NC_CACHE["nc"]
    const = host_constants()
    in_maps = [make_core_inputs(inputs, const, c // 2, c % 2) for c in range(8)]
    res = run_bass_kernel_spmd(nc, in_maps, core_ids=list(range(8)))
    out = np.empty((4, T, D), np.float32)
    for c in range(8):
        own = own_positions(c % 2)
        out[c // 2, own, :] = np.asarray(res.results[c]["outT"]).T
    return out


def host_constants():
    inv_freq = 1.0 / 10000.0 ** (np.arange(0, RD, 2, dtype=np.float32) / RD)
    t = np.arange(T, dtype=np.float32)
    freqs = np.outer(t, inv_freq)
    emb = np.concatenate([freqs, freqs], -1)
    cos_t = np.cos(emb).astype(np.float32).T
    sin_t = np.sin(emb).astype(np.float32).T
    cosk = np.ascontiguousarray(np.tile(cos_t, (4, 1)).astype(np.float32))
    sink = np.ascontiguousarray(np.tile(sin_t, (4, 1)).astype(np.float32))
    kk = np.arange(128)[:, None]
    qq = np.arange(512)[None, :]
    masks = np.ascontiguousarray(np.concatenate(
        [(qq - kk >= 128 * j).astype(np.float32) for j in range(4)],
        axis=1).astype(bfdt))
    R = np.zeros((32, 32), np.float32)
    for i in range(16):
        R[i, i + 16] = -1.0
        R[i + 16, i] = 1.0
    R4 = np.zeros((128, 128), np.float32)
    for g in range(4):
        R4[g * 32:(g + 1) * 32, g * 32:(g + 1) * 32] = R
    r4t = np.ascontiguousarray(R4.T.astype(bfdt))
    ps = np.zeros((128, 96), np.float32)
    for ci, g in enumerate((0, 1, 3)):
        for i in range(32):
            ps[g * 32 + i, ci * 32 + i] = 1.0
    psel = np.ascontiguousarray(ps.astype(bfdt))
    return dict(cosk=cosk, sink=sink, maskc=masks, r4t=r4t, psel=psel,
                cos_t=cos_t, sin_t=sin_t)


def own_positions(core_type):
    if core_type == 0:
        return np.concatenate([np.arange(0, 1024), np.arange(3072, 4096)])
    return np.arange(1024, 3072)


# version 15
# speedup vs baseline: 1.3045x; 1.0009x over previous
import numpy as np
import ml_dtypes
import concourse.bass as bass
import concourse.mybir as mybir
from contextlib import ExitStack

F32 = mybir.dt.float32
BF16 = mybir.dt.bfloat16
I32 = mybir.dt.int32
AF = mybir.ActivationFunctionType
ALU = mybir.AluOpType
bfdt = ml_dtypes.bfloat16

T, TOWN, D, KVR, QR, NH, HD, RD, HID = 4096, 2048, 1024, 256, 512, 16, 64, 32, 3072
NTG = T // 512
SCALE = float(1.0 / np.sqrt(96.0))
EXP_BIAS = -8.0
FFS = float(1.0 / 127.0)
CHUNKS = {0: [(0, 4), (512, 8), (3072, 28), (3584, 32)],
          1: [(1024, 12), (1536, 16), (2048, 20), (2560, 24)]}
ENGINES = ("pe", "act", "dve", "pl", "sp")


class Step:
    __slots__ = ("eng", "emit", "deps", "idx", "seq", "inc_amt", "needed", "tag",
                 "gen", "self_wait")

    def __init__(self, eng, emit, deps, idx, tag="", gen=0):
        self.eng, self.emit, self.deps, self.idx, self.tag = eng, emit, deps, idx, tag
        self.seq = None
        self.inc_amt = 0
        self.needed = False
        self.gen = gen
        self.self_wait = None


class Reg:
    __slots__ = ("w", "readers", "name")

    def __init__(self, name=""):
        self.w, self.readers, self.name = None, [], name


class Sched:
    def __init__(self, nc, es):
        self.nc = nc
        self.steps = []
        self.gen = 0
        self.sems = {}
        self.counts = {}
        for e in ENGINES:
            self.sems[e] = es.enter_context(nc.semaphore(f"sem_{e}"))
            self.sems[e + "_dma"] = es.enter_context(nc.semaphore(f"sem_{e}dma"))
            self.counts[e] = 0
            self.counts[e + "_dma"] = 0

    def step(self, eng, emit, reads=(), writes=(), deps=(), dma=False, tag=""):
        alldeps = list(deps)
        for r in reads:
            if r.w is not None:
                alldeps.append(r.w)
        for wreg in writes:
            if wreg.w is not None:
                alldeps.append(wreg.w)
            alldeps.extend(wreg.readers)
        key = eng + ("_dma" if dma else "")
        pruned, seen = [], set()
        for d in alldeps:
            if d is None or id(d) in seen:
                continue
            seen.add(id(d))
            if d.gen < self.gen:
                continue
            if d.eng == key and key in ("pe", "sp_dma", "pl_dma", "act_dma"):
                continue
            pruned.append(d)
        s = Step(key, emit, pruned, len(self.steps), tag, gen=self.gen)
        self.steps.append(s)
        for d in pruned:
            d.needed = True
        for r in reads:
            r.readers.append(s)
        for wreg in writes:
            wreg.w = s
            wreg.readers = []
        return s

    def group_end(self, reg):
        reg.w = self.steps[-1]
        reg.readers = []
        return reg.w

    def snapshot(self):
        return dict(self.counts)

    def set_counts(self, c):
        self.counts = dict(c)

    def flush(self, in_block=True):
        nc = self.nc
        per_eng = {e: [] for e in ENGINES}
        for s in self.steps:
            e = s.eng[:-4] if s.eng.endswith("_dma") else s.eng
            per_eng[e].append(s)
        for e in ENGINES:
            run = []

            def close_run(run):
                if run:
                    for s in run:
                        self.counts[s.eng] += 16
                    for s in run:
                        s.seq = self.counts[run[0].eng]
            nd = 0
            for s in per_eng[e]:
                if s.eng.endswith("_dma"):
                    if s.deps:
                        close_run(run)
                        s.self_wait = self.counts[s.eng]
                        run = [s]
                    else:
                        run.append(s)
                else:
                    close_run(run)
                    run = []
            close_run(run)
        for s in self.steps:
            if not s.eng.endswith("_dma") and s.needed:
                s.inc_amt = 1
                self.counts[s.eng] += 1
                s.seq = self.counts[s.eng]
        for s in self.steps:
            if s.eng.endswith("_dma"):
                s.inc_amt = 16
        sched = self

        def make(engname):
            def body(eng):
                cur = {}
                for s in per_eng[engname]:
                    isdma = s.eng.endswith("_dma")
                    if s.self_wait:
                        eng.wait_ge(sched.sems[s.eng], s.self_wait)
                    for d in s.deps:
                        assert d.seq is not None, f"dep of {s.tag} on {d.tag} lacks seq"
                        if not isdma and cur.get(d.eng, 0) >= d.seq:
                            continue
                        cur[d.eng] = max(cur.get(d.eng, 0), d.seq)
                        eng.wait_ge(sched.sems[d.eng], d.seq)
                    inst = s.emit(eng)
                    if s.inc_amt:
                        inst.then_inc(sched.sems[s.eng], s.inc_amt)
                dkey = engname + "_dma"
                if any(s.eng == dkey for s in per_eng[engname]):
                    eng.wait_ge(sched.sems[dkey], sched.counts[dkey])
            return body

        with nc.Block() as blk:
            for engname, attr in (("sp", "sync"), ("pl", "gpsimd"), ("pe", "tensor"),
                                  ("act", "scalar"), ("dve", "vector")):
                if per_eng[engname]:
                    getattr(blk, attr)(make(engname))
        self.steps = []
        self.gen += 1


def MM(out, lhsT, rhs, start=True, stop=True, tp=None):
    if tp is None:
        return lambda e: e.matmul(out, lhsT, rhs, start=start, stop=stop)
    return lambda e: e.matmul(out, lhsT, rhs, start=start, stop=stop, tile_position=tp)


def build_nc(taps=(), stop_after=99, ntg=NTG, parts=('ln','kv','qc','kr','rope')):
    taps = set(taps)
    nc = bass.Bass("TRN2", target_bir_lowering=False)

    xT = nc.dram_tensor("xT", [D, T], F32, kind="ExternalInput").ap()
    xTo = nc.dram_tensor("xTo", [D, TOWN], F32, kind="ExternalInput").ap()
    w_kv = nc.dram_tensor("w_kv", [D, KVR], F32, kind="ExternalInput").ap()
    w_q = nc.dram_tensor("w_q", [D, QR], F32, kind="ExternalInput").ap()
    w_kdec = nc.dram_tensor("w_kdec", [KVR, D], F32, kind="ExternalInput").ap()
    w_vdec = nc.dram_tensor("w_vdec", [KVR, D], F32, kind="ExternalInput").ap()
    w_qdec = nc.dram_tensor("w_qdec", [QR, D], F32, kind="ExternalInput").ap()
    w_krope = nc.dram_tensor("w_krope", [D, QR], F32, kind="ExternalInput").ap()
    w_qrope = nc.dram_tensor("w_qrope", [QR, QR], F32, kind="ExternalInput").ap()
    w_o = nc.dram_tensor("w_o", [D, D], F32, kind="ExternalInput").ap()
    g_kv = nc.dram_tensor("g_kv", [KVR], F32, kind="ExternalInput").ap()
    wgT = nc.dram_tensor("wgT", [D, HID], I32, kind="ExternalInput").ap()
    wuT = nc.dram_tensor("wuT", [D, HID], I32, kind="ExternalInput").ap()
    wdT = nc.dram_tensor("wdT", [HID, D], I32, kind="ExternalInput").ap()
    cosk = nc.dram_tensor("cosk", [128, T], F32, kind="ExternalInput").ap()
    sink = nc.dram_tensor("sink", [128, T], F32, kind="ExternalInput").ap()
    cosqo = nc.dram_tensor("cosqo", [128, TOWN], F32, kind="ExternalInput").ap()
    sinqo = nc.dram_tensor("sinqo", [128, TOWN], F32, kind="ExternalInput").ap()
    maskc = nc.dram_tensor("maskc", [128, 4 * 512], BF16, kind="ExternalInput").ap()
    r4t = nc.dram_tensor("r4t", [128, 128], BF16, kind="ExternalInput").ap()
    psel = nc.dram_tensor("psel", [128, 3 * 32], BF16, kind="ExternalInput").ap()
    outT = nc.dram_tensor("outT", [D, TOWN], F32, kind="ExternalOutput").ap()

    tap_t = {}

    def tap(name, shape, dtype=F32):
        if name in taps:
            tap_t[name] = nc.dram_tensor("tap_" + name, shape, dtype,
                                         kind="ExternalOutput").ap()
        return tap_t.get(name)

    es = ExitStack()
    S = es.enter_context
    sch = Sched(nc, es)

    mask_sb = S(nc.sbuf_tensor("mask_sb", [128, 4 * 512], BF16))
    r4t_sb = S(nc.sbuf_tensor("r4t_sb", [128, 128], BF16))
    psel_sb = S(nc.sbuf_tensor("psel_sb", [128, 3 * 32], BF16))
    onesv = S(nc.sbuf_tensor("onesv", [128, 1], BF16))
    ones1 = S(nc.sbuf_tensor("ones1", [1, 128], F32))
    gkv_sb = S(nc.sbuf_tensor("gkv_sb", [128, 2], F32))
    CVALS = [1e-5, 1e-6, EXP_BIAS, FFS, -0.5, -1.0]
    cst_sb = S(nc.sbuf_tensor("cst_sb", [128, len(CVALS)], F32))
    wst = {}
    r_const, r_gkv = Reg("const"), Reg("gkv")

    r_attnall = [Reg(f"at{h}") for h in range(NH)]
    r_wo = Reg("wo")

    esL = ExitStack()
    SL = esL.enter_context
    kv_sb = SL(nc.sbuf_tensor("kv_sb", [128, 2 * T], BF16))
    kr_sb = SL(nc.sbuf_tensor("kr_sb", [128, 4 * T], BF16))
    qc_dram = nc.dram_tensor("qc_scr", [4 * 128, T], BF16, kind="Internal").ap()
    r_kv, r_qc, r_kr, r_vec = Reg("kv"), Reg("qc"), Reg("kr"), Reg("vec")

    sch.step("sp", lambda e: e.dma_start(out=mask_sb[:, :], in_=maskc[:, :]),
             writes=[r_const], dma=True, tag="ld_mask")
    sch.step("sp", lambda e: e.dma_start(out=r4t_sb[:, :], in_=r4t[:, :]),
             writes=[r_const], dma=True, tag="ld_r4t")
    sch.step("sp", lambda e: e.dma_start(out=psel_sb[:, :], in_=psel[:, :]),
             writes=[r_const], dma=True, tag="ld_psel")
    sch.step("sp", lambda e: e.dma_start(
        out=bass.AP(gkv_sb, 0, [[2, 128], [1, 2]]),
        in_=bass.AP(g_kv.tensor, 0, [[1, 128], [128, 2]]),
        allow_slow_non_contiguous=True),
        writes=[r_gkv], dma=True, tag="ld_gkv")
    sch.step("pl", lambda e: e.memset(onesv[:, :], 1.0), writes=[r_const], tag="ones_v")
    sch.step("pl", lambda e: e.memset(ones1[:, :], 1.0), writes=[r_const], tag="ones_1")
    for i, v in enumerate(CVALS):
        sch.step("pl", lambda e, i=i, v=v: e.memset(cst_sb[:, i:i + 1], float(v)),
                 writes=[r_const], tag=f"cst{i}")
        nc.const_aps.register(F32, float(v), cst_sb[:, i:i + 1])

    es0 = ExitStack()
    S0 = es0.enter_context
    vec_sb = S0(nc.sbuf_tensor("vec_sb", [1, 6 * 512], F32))
    qc_sb = S0(nc.sbuf_tensor("qc_sb", [128, 4 * T], BF16))
    wkv_sb = S0(nc.sbuf_tensor("wkv_sb", [128, 8 * KVR], BF16))
    wq_sb = S0(nc.sbuf_tensor("wq_sb", [128, 8 * QR], BF16))
    wkr_sb = S0(nc.sbuf_tensor("wkr_sb", [128, 8 * QR], BF16))
    xg_sb = [S0(nc.sbuf_tensor(f"xg{i}", [128, 8 * 512], BF16)) for i in range(2)]
    xf_sb = [S0(nc.sbuf_tensor(f"xf{i}", [128, 4 * 512], F32)) for i in range(2)]
    xsq_sb = S0(nc.sbuf_tensor("xsq_sb", [128, 2 * 512], BF16))
    xc_sb = S0(nc.sbuf_tensor("xc_sb", [128, 8 * 512], BF16))
    rb_sb = [S0(nc.sbuf_tensor(f"rb{i}", [128, 512], BF16)) for i in range(2)]
    kvsq_sb = S0(nc.sbuf_tensor("kvsq_sb", [128, 2 * 512], BF16))
    ktab_sb = [S0(nc.sbuf_tensor(f"ktab{i}", [128, 512], F32)) for i in range(2)]
    stab_sb = [S0(nc.sbuf_tensor(f"stab{i}", [128, 512], F32)) for i in range(2)]
    rot_scr = S0(nc.sbuf_tensor("rot_scr", [128, 512], BF16))
    rot_scr2 = S0(nc.sbuf_tensor("rot_scr2", [128, 512], BF16))
    ps_bc = S0(nc.psum_tensor("ps_bc", [128, 1024], F32))
    ps_kv = S0(nc.psum_tensor("ps_kv", [128, 1024], F32))
    ps_mm2 = S0(nc.psum_tensor("ps_mm2", [128, 1024], F32))
    ps_misc = S0(nc.psum_tensor("ps_misc", [128, 512], F32))
    ps_rot = S0(nc.psum_tensor("ps_rot", [128, 512], F32))
    ps_ln = ps_misc
    ps_kr = ps_mm2
    ps_qc = ps_mm2
    ps_ss = ps_misc

    r_wkv, r_wq, r_wkr = Reg("wkv"), Reg("wq"), Reg("wkr")
    r_xg = [Reg("xg0"), Reg("xg1")]
    r_xf = [Reg("xf0"), Reg("xf1")]
    r_xsq = [Reg("xsq0"), Reg("xsq1")]
    r_xc = Reg("xc")
    r_xcc = [Reg(f"xcc{i}") for i in range(8)]
    r_rb = [Reg("rb0"), Reg("rb1")]
    r_kvsq = Reg("kvsq")
    r_ktab = [Reg("kt0"), Reg("kt1")]
    r_rsc, r_rsc2 = Reg("rsc"), Reg("rsc2")
    r_psln, r_psbc0, r_psbc1 = Reg("psmisc"), Reg("psbc0"), Reg("psbc1")
    r_pskv, r_psrot = Reg("pskv"), Reg("psrot")
    r_psss = r_psln
    r_mm2 = [Reg("mm2_0"), Reg("mm2_1")]
    r_pskr = r_mm2
    r_psqc = r_mm2

    def wload(eng, dst, src, cols, kchunks, reg, tag):
        wstage, r_wstage = wst["t"], wst["r"]
        per = max(1, 2048 // cols)
        for k0 in range(0, kchunks, per):
            kn = min(per, kchunks - k0)
            sch.step("sp", lambda e, k0=k0, kn=kn: e.dma_start(
                out=bass.AP(wstage, 0, [[2048, 128], [cols, kn], [1, cols]]),
                in_=bass.AP(src.tensor, k0 * 128 * cols,
                            [[cols, 128], [128 * cols, kn], [1, cols]])),
                writes=[r_wstage], dma=True, tag=f"{tag}_{k0}")
            sch.step("dve", lambda e, k0=k0, kn=kn: e.tensor_copy(
                dst[:, k0 * cols:(k0 + kn) * cols], wstage[:, 0:kn * cols]),
                reads=[r_wstage], writes=[reg], tag=f"{tag}c_{k0}")

    wst["t"] = S0(nc.sbuf_tensor("wstage0", [128, 2048], F32))
    wst["r"] = Reg("wstage0")
    wload("pl", wkv_sb, w_kv, KVR, 8, r_wkv, "ld_wkv")
    wload("pl", wq_sb, w_q, QR, 8, r_wq, "ld_wq")
    wload("pl", wkr_sb, w_krope, QR, 8, r_wkr, "ld_wkr")

    VM, VSD, VR, VMSQ, VA, VRHO = range(6)

    def V(i):
        return vec_sb[0:1, i * 512:(i + 1) * 512]

    for tg in range(ntg):
        b = tg % 2
        c0 = tg * 512
        xg, rxg = xg_sb[b], r_xg[b]
        for hh2 in range(2):
            xfb = (2 * tg + hh2) % 2
            sch.step("sp", lambda e, c0=c0, hh2=hh2, xfb=xfb: e.dma_start(
                out=bass.AP(xf_sb[xfb], 0, [[4 * 512, 128], [512, 4], [1, 512]]),
                in_=bass.AP(xT.tensor, c0 + hh2 * 4 * 128 * T,
                            [[T, 128], [128 * T, 4], [1, 512]])),
                writes=[r_xf[xfb]], dma=True, tag=f"ld_x{tg}_{hh2}")
            sch.step("act", lambda e, xg=xg, hh2=hh2, xfb=xfb: e.activation(
                xg[:, hh2 * 2048:(hh2 + 1) * 2048], xf_sb[xfb][:, :], AF.Copy),
                reads=[r_xf[xfb]], writes=[rxg], tag=f"cx{tg}_{hh2}")
        sch.step("sp", lambda e, b=b, c0=c0: e.dma_start(
            out=ktab_sb[b][:, :], in_=cosk[:, c0:c0 + 512]),
            writes=[r_ktab[b]], dma=True, tag=f"ld_ck{tg}")
        sch.step("sp", lambda e, b=b, c0=c0: e.dma_start(
            out=stab_sb[b][:, :], in_=sink[:, c0:c0 + 512]),
            writes=[r_ktab[b]], dma=True, tag=f"ld_sk{tg}")
        if 'ln' not in parts:
            continue
        for c in range(8):
            sch.step("pe", MM(ps_ln[0:1, 0:512], onesv[:, 0:1],
                              xg[:, c * 512:(c + 1) * 512], start=(c == 0), stop=(c == 7)),
                     reads=[rxg, r_const], writes=([r_psln] if c == 0 else []),
                     tag=f"mmm{tg}_{c}")
        sch.group_end(r_psln)
        sch.step("dve", lambda e: e.tensor_scalar_mul(V(VM), ps_ln[0:1, 0:512], 1.0 / D),
                 reads=[r_psln], writes=[r_vec], tag=f"m{tg}")
        sch.step("pe", MM(ps_bc[:, 0:512], ones1[0:1, :], V(VM)),
                 reads=[r_vec, r_const], writes=[r_psbc0], tag=f"mbc{tg}")
        for c in range(8):
            sq = c % 2
            sch.step("act", lambda e, xg=xg, c=c, sq=sq: e.activation(
                xsq_sb[:, sq * 512:sq * 512 + 512], xg[:, c * 512:(c + 1) * 512], AF.Square),
                reads=[rxg], writes=[r_xsq[sq]], tag=f"sq{tg}_{c}")
            sch.step("pe", MM(ps_ln[0:1, 0:512], onesv[:, 0:1],
                              xsq_sb[:, sq * 512:sq * 512 + 512],
                              start=(c == 0), stop=(c == 7)),
                     reads=[r_xsq[sq], r_const], writes=([r_psln] if c == 0 else []),
                     tag=f"mms{tg}_{c}")
        sch.group_end(r_psln)
        sch.step("dve", lambda e: e.tensor_scalar_mul(V(VSD), ps_ln[0:1, 0:512], 1.0 / D),
                 reads=[r_psln], writes=[r_vec], tag=f"v1_{tg}")
        sch.step("dve", lambda e: e.tensor_mul(V(VA), V(VM), V(VM)),
                 reads=[r_vec], writes=[r_vec], tag=f"m2_{tg}")
        sch.step("dve", lambda e: e.tensor_sub(V(VSD), V(VSD), V(VA)),
                 reads=[r_vec], writes=[r_vec], tag=f"var{tg}")
        sch.step("act", lambda e: e.activation(V(VSD), V(VSD), AF.Ln, bias=1e-5),
                 reads=[r_vec], writes=[r_vec], tag=f"sd{tg}")
        sch.step("act", lambda e: e.activation(V(VR), V(VSD), AF.Exp, scale=-0.5),
                 reads=[r_vec], writes=[r_vec], tag=f"r{tg}")
        for c in range(8):
            sch.step("dve", lambda e, xg=xg, c=c: e.tensor_sub(
                xc_sb[:, c * 512:(c + 1) * 512], xg[:, c * 512:(c + 1) * 512],
                ps_bc[:, 0:512]),
                reads=[rxg, r_psbc0], writes=[r_xcc[c]], tag=f"cent{tg}_{c}")
        for mt in range(2 if 'kv' in parts else 0):
            for k in range(8):
                sch.step("pe", MM(ps_kv[:, mt * 512:mt * 512 + 512],
                                  wkv_sb[:, k * KVR + mt * 128:k * KVR + mt * 128 + 128],
                                  xc_sb[:, k * 512:(k + 1) * 512],
                                  start=(k == 0), stop=(k == 7)),
                         reads=[r_xcc[k], r_wkv],
                         writes=([r_pskv] if (mt == 0 and k == 0) else []),
                         tag=f"kv{tg}_{mt}_{k}")
        sch.group_end(r_pskv)
        for mt in range(2 if 'kv' in parts else 0):
            sch.step("dve", lambda e, mt=mt, c0=c0: e.tensor_copy(
                kv_sb[:, mt * T + c0:mt * T + c0 + 512], ps_kv[:, mt * 512:mt * 512 + 512]),
                reads=[r_pskv], writes=[r_kv], tag=f"kve{tg}_{mt}")
            sch.step("act", lambda e, mt=mt, c0=c0: e.activation(
                kvsq_sb[:, mt * 512:mt * 512 + 512],
                kv_sb[:, mt * T + c0:mt * T + c0 + 512], AF.Square),
                reads=[r_kv], writes=[r_kvsq], tag=f"kvq{tg}_{mt}")
        sch.step("pe", MM(ps_bc[:, 512:1024], ones1[0:1, :], V(VR)),
                 reads=[r_vec, r_const], writes=[r_psbc1], tag=f"rbc{tg}")
        sch.step("dve", lambda e, b=b: e.tensor_copy(rb_sb[b][:, :], ps_bc[:, 512:1024]),
                 reads=[r_psbc1], writes=[r_rb[b]], tag=f"rbe{tg}")
        for k in range(2 if 'kv' in parts else 0):
            sch.step("pe", MM(ps_ss[0:1, 0:512], onesv[:, 0:1],
                              kvsq_sb[:, k * 512:(k + 1) * 512], start=(k == 0), stop=(k == 1)),
                     reads=[r_kvsq, r_const], writes=([r_psss] if k == 0 else []),
                     tag=f"ss{tg}_{k}")
        sch.group_end(r_psss)
        if 'kv' not in parts:
            continue
        sch.step("dve", lambda e: e.tensor_scalar_mul(V(VMSQ), ps_ss[0:1, 0:512], 1.0 / KVR),
                 reads=[r_psss], writes=[r_vec], tag=f"msq{tg}")
        sch.step("dve", lambda e: e.tensor_mul(V(VA), V(VR), V(VR)),
                 reads=[r_vec], writes=[r_vec], tag=f"r2{tg}")
        sch.step("dve", lambda e: e.tensor_mul(V(VA), V(VA), V(VMSQ)),
                 reads=[r_vec], writes=[r_vec], tag=f"r2m{tg}")
        sch.step("act", lambda e: e.activation(V(VA), V(VA), AF.Ln, bias=1e-6),
                 reads=[r_vec], writes=[r_vec], tag=f"sqr{tg}")
        sch.step("act", lambda e: e.activation(V(VRHO), V(VA), AF.Exp, scale=-0.5),
                 reads=[r_vec], writes=[r_vec], tag=f"rec{tg}")
        sch.step("dve", lambda e: e.tensor_mul(V(VRHO), V(VRHO), V(VR)),
                 reads=[r_vec], writes=[r_vec], tag=f"rho{tg}")
        for mt in range(4 if 'qc' in parts else 0):
            sl = mt % 2
            rp = r_psqc[sl]
            for k in range(8):
                sch.step("pe", MM(ps_qc[:, sl * 512:sl * 512 + 512],
                                  wq_sb[:, k * QR + mt * 128:k * QR + mt * 128 + 128],
                                  xc_sb[:, k * 512:(k + 1) * 512],
                                  start=(k == 0), stop=(k == 7)),
                         reads=[r_xcc[k], r_wq], writes=([rp] if k == 0 else []),
                         tag=f"qc{tg}_{mt}_{k}")
            sch.group_end(rp)
            sch.step("dve", lambda e, mt=mt, sl=sl, b=b, c0=c0: e.tensor_mul(
                qc_sb[:, mt * T + c0:mt * T + c0 + 512],
                ps_qc[:, sl * 512:sl * 512 + 512], rb_sb[b][:, :]),
                reads=[rp, r_rb[b]], writes=[r_qc], tag=f"qce{tg}_{mt}")
        sch.step("pe", MM(ps_bc[:, 512:1024], ones1[0:1, :], V(VRHO)),
                 reads=[r_vec, r_const], writes=[r_psbc1], tag=f"rhobc{tg}")
        for mt in range(2):
            sch.step("dve", lambda e, mt=mt, c0=c0: e.tensor_mul(
                kv_sb[:, mt * T + c0:mt * T + c0 + 512],
                kv_sb[:, mt * T + c0:mt * T + c0 + 512], ps_bc[:, 512:1024]),
                reads=[r_kv, r_psbc1], writes=[r_kv], tag=f"kvn{tg}_{mt}")
        for g in range(4 if 'kr' in parts else 0):
            sl = g % 2
            rp = r_pskr[sl]
            for k in range(8):
                sch.step("pe", MM(ps_kr[:, sl * 512:sl * 512 + 512],
                                  wkr_sb[:, k * QR + g * 128:k * QR + g * 128 + 128],
                                  xc_sb[:, k * 512:(k + 1) * 512],
                                  start=(k == 0), stop=(k == 7)),
                         reads=[r_xcc[k], r_wkr], writes=([rp] if k == 0 else []),
                         tag=f"kr{tg}_{g}_{k}")
            sch.group_end(rp)
            sch.step("dve", lambda e, g=g, sl=sl, b=b, c0=c0: e.tensor_mul(
                kr_sb[:, g * T + c0:g * T + c0 + 512],
                ps_kr[:, sl * 512:sl * 512 + 512], rb_sb[b][:, :]),
                reads=[rp, r_rb[b]], writes=[r_kr], tag=f"kre{tg}_{g}")
            if 'rope' not in parts:
                continue
            sch.step("pe", MM(ps_rot[:, 0:512], r4t_sb[:, :],
                              kr_sb[:, g * T + c0:g * T + c0 + 512]),
                     reads=[r_kr, r_const], writes=[r_psrot], tag=f"rot{tg}_{g}")
            sch.step("dve", lambda e, g=g, b=b, c0=c0: e.tensor_mul(
                rot_scr[:, :], kr_sb[:, g * T + c0:g * T + c0 + 512], ktab_sb[b][:, :]),
                reads=[r_kr, r_ktab[b]], writes=[r_rsc], tag=f"t1_{tg}_{g}")
            sch.step("dve", lambda e, b=b: e.tensor_mul(
                rot_scr2[:, :], ps_rot[:, 0:512], stab_sb[b][:, :]),
                reads=[r_psrot, r_ktab[b]], writes=[r_rsc2], tag=f"u_{tg}_{g}")
            sch.step("dve", lambda e, g=g, c0=c0: e.tensor_add(
                kr_sb[:, g * T + c0:g * T + c0 + 512], rot_scr[:, :], rot_scr2[:, :]),
                reads=[r_rsc, r_rsc2], writes=[r_kr], tag=f"kh_{tg}_{g}")
    sch.flush()

    for mt in range(4):
        sch.step("sp", lambda e, mt=mt: e.dma_start(
            out=qc_dram[mt * 128:(mt + 1) * 128, :], in_=qc_sb[:, mt * T:(mt + 1) * T]),
            reads=[r_qc], dma=True, tag=f"qcspill{mt}")
    r_qcd = Reg("qcd")
    r_qcd.w = sch.steps[-1]

    t_kvn = tap("kv_norm", [2 * 128, T], BF16)
    t_qc = tap("q_c", [4 * 128, T], BF16)
    t_khat = tap("khat", [4 * 128, T], BF16)
    for name, tt, src_sb, reg, n in (("kv_norm", t_kvn, kv_sb, r_kv, 2),
                                     ("q_c", t_qc, qc_sb, r_qc, 4),
                                     ("khat", t_khat, kr_sb, r_kr, 4)):
        if tt is not None:
            for mt in range(n):
                sch.step("sp", lambda e, mt=mt, tt=tt, src_sb=src_sb: e.dma_start(
                    out=tt[mt * 128:(mt + 1) * 128, :], in_=src_sb[:, mt * T:(mt + 1) * T]),
                    reads=[reg], dma=True, tag=f"tap_{name}{mt}")
    sch.flush()
    es0.close()
    es_mid = ExitStack()
    attn_all = es_mid.enter_context(nc.sbuf_tensor("attn_all", [128, 8 * TOWN], BF16))
    wo_sb = es_mid.enter_context(nc.sbuf_tensor("wo_sb", [128, 8 * D], BF16))
    if stop_after < 3:
        es_mid.close()
        esL.close()
        es.close()
        return nc, tap_t

    es3 = ExitStack()
    S3 = es3.enter_context
    wkd_sb = S3(nc.sbuf_tensor("wkd_sb", [128, 2 * D], BF16))
    wvd_sb = S3(nc.sbuf_tensor("wvd_sb", [128, 2 * D], BF16))
    wqd_sb = S3(nc.sbuf_tensor("wqd_sb", [128, 4 * D], BF16))
    wqr_sb = S3(nc.sbuf_tensor("wqr_sb", [128, 4 * QR], BF16))
    r_wkd, r_wvd, r_wqd, r_wqr = Reg("wkd"), Reg("wvd"), Reg("wqd"), Reg("wqr")

    es3w = ExitStack()
    wst["t"] = es3w.enter_context(nc.sbuf_tensor("wstage3", [128, 2048], F32))
    wst["r"] = Reg("wstage3")
    wload("pl", wkd_sb, w_kdec, D, 2, r_wkd, "ld_wkd")
    wload("pl", wvd_sb, w_vdec, D, 2, r_wvd, "ld_wvd")
    wload("pl", wqd_sb, w_qdec, D, 4, r_wqd, "ld_wqd")
    wload("pl", wqr_sb, w_qrope, QR, 4, r_wqr, "ld_wqr")
    wload("pl", wo_sb, w_o, D, 8, r_wo, "ld_wo")
    for w_sbuf, reg in ((wkd_sb, r_wkd), (wvd_sb, r_wvd)):
        for k in range(2):
            sch.step("dve", lambda e, w_sbuf=w_sbuf, k=k: e.tensor_scalar_mul(
                w_sbuf[:, k * D:(k + 1) * D], w_sbuf[:, k * D:(k + 1) * D],
                gkv_sb[:, k:k + 1]),
                reads=[reg, r_gkv], writes=[reg], tag=f"gfold{k}")
    for k in range(4):
        sch.step("dve", lambda e, k=k: e.tensor_scalar_mul(
            wqd_sb[:, k * D:(k + 1) * D], wqd_sb[:, k * D:(k + 1) * D], SCALE),
            reads=[r_wqd], writes=[r_wqd], tag=f"sfold{k}")
    sch.flush()
    es3w.close()
    qhat_sb = S3(nc.sbuf_tensor("qhat_sb", [128, 4 * TOWN], BF16))
    qco_sb = S3(nc.sbuf_tensor("qco_sb", [128, 4 * TOWN], BF16))
    kbuf_sb = [S3(nc.sbuf_tensor(f"kbuf{i}", [96, T], BF16)) for i in range(2)]
    qbuf_sb = [S3(nc.sbuf_tensor(f"qbuf{i}", [96, TOWN], BF16)) for i in range(2)]
    v_sb = S3(nc.sbuf_tensor("v_sb", [128, 32 * 130], BF16))
    p_sb = [S3(nc.sbuf_tensor(f"p{i}", [128, 1024], BF16)) for i in range(2)]
    nrm_sb = S3(nc.sbuf_tensor("nrm_sb", [64, 512], F32))
    aschat = S3(nc.sbuf_tensor("aschat", [64, 512], BF16))
    qtab_sb = [S3(nc.sbuf_tensor(f"qtab{i}", [128, 512], F32)) for i in range(2)]
    rsc3 = S3(nc.sbuf_tensor("rsc3", [128, 512], BF16))
    rsc4 = S3(nc.sbuf_tensor("rsc4", [128, 512], BF16))
    recv_sb = S3(nc.sbuf_tensor("recv_sb", [1, 512], F32))
    ps_sc = [S3(nc.psum_tensor(f"ps_sc{i}", [128, 1024], F32)) for i in range(2)]
    ps_pv = S3(nc.psum_tensor("ps_pv", [65, 1024], F32))
    ps_decT = S3(nc.psum_tensor("ps_decT", [128, 1024], F32))
    r_qhat, r_qtab = Reg("qhat"), Reg("qtab")
    r_kbuf = [Reg("kb0"), Reg("kb1")]
    r_qbuf = [Reg("qb0"), Reg("qb1")]
    r_v = Reg("v")
    r_p = [Reg("p0"), Reg("p1")]
    r_nrm, r_aschat, r_recv = Reg("nrm"), Reg("aschat"), Reg("recv")
    r_rsc3, r_rsc4 = Reg("rsc3"), Reg("rsc4")
    r_pssc = [Reg("pssc0"), Reg("pssc1")]
    r_pspv = [Reg("pspv0"), Reg("pspv1")]
    r_psd = [Reg("psd0"), Reg("psd1")]
    r_qtabs = [Reg("qt0"), Reg("qt1")]

    snap0 = sch.snapshot()
    end_counts = []
    pid = nc.partition_id()
    engs = [nc.tensor, nc.vector, nc.scalar, nc.gpsimd, nc.sync]
    idx = nc.snap(pid % 2)
    for var in nc.Switch(engines=engs, index=[idx] * 5, n=2):
        sch.set_counts(snap0)
        chunks = CHUNKS[var]
        own0 = [q0 for q0, _ in chunks]
        r_qco = Reg("qco")

        def dslot(sl):
            return ps_decT[:, sl * 512:sl * 512 + 512]
        for mt in range(4):
            for cc in range(4):
                sch.step("sp", lambda e, mt=mt, cc=cc, q0=own0[cc]: e.dma_start(
                    out=qco_sb[:, mt * TOWN + cc * 512:mt * TOWN + cc * 512 + 512],
                    in_=qc_dram[mt * 128:(mt + 1) * 128, q0:q0 + 512]),
                    reads=[r_qcd], writes=[r_qco], dma=True, tag=f"qcl{var}_{mt}_{cc}")

        for cc in range(4):
            tb = cc % 2
            q0 = own0[cc]
            sch.step("sp", lambda e, cc=cc: e.dma_start(
                out=qtab_sb[0][:, :], in_=cosqo[:, cc * 512:cc * 512 + 512]),
                writes=[r_qtabs[0]], dma=True, tag=f"qtc{var}_{cc}")
            sch.step("sp", lambda e, cc=cc: e.dma_start(
                out=qtab_sb[1][:, :], in_=sinqo[:, cc * 512:cc * 512 + 512]),
                writes=[r_qtabs[1]], dma=True, tag=f"qts{var}_{cc}")
            for g in range(4):
                sl = g % 2
                rp = r_psd[sl]
                for k in range(4):
                    sch.step("pe", MM(dslot(sl),
                                      wqr_sb[:, k * QR + g * 128:k * QR + g * 128 + 128],
                                      qco_sb[:, k * TOWN + cc * 512:k * TOWN + cc * 512 + 512],
                                      start=(k == 0), stop=(k == 3)),
                             reads=[r_qco, r_wqr], writes=([rp] if k == 0 else []),
                             tag=f"qr{var}_{g}_{cc}")
                sch.group_end(rp)
                sch.step("dve", lambda e, g=g, cc=cc, sl=sl: e.tensor_copy(
                    qhat_sb[:, g * TOWN + cc * 512:g * TOWN + cc * 512 + 512], dslot(sl)),
                    reads=[rp], writes=[r_qhat], tag=f"qre{var}_{g}_{cc}")
                sch.step("pe", MM(dslot(sl), r4t_sb[:, :],
                                  qhat_sb[:, g * TOWN + cc * 512:g * TOWN + cc * 512 + 512]),
                         reads=[r_qhat, r_const], writes=[rp], tag=f"qrot{var}_{g}_{cc}")
                sch.step("dve", lambda e, g=g, cc=cc, tb=tb: e.tensor_mul(
                    rsc3[:, :],
                    qhat_sb[:, g * TOWN + cc * 512:g * TOWN + cc * 512 + 512],
                    qtab_sb[0][:, :]),
                    reads=[r_qhat, r_qtabs[0]], writes=[r_rsc3], tag=f"qt1{var}_{g}_{cc}")
                sch.step("dve", lambda e, sl=sl: e.tensor_mul(
                    rsc4[:, :], dslot(sl), qtab_sb[1][:, :]),
                    reads=[rp, r_qtabs[1]], writes=[r_rsc4], tag=f"qu{var}_{g}_{cc}")
                sch.step("dve", lambda e, g=g, cc=cc: e.tensor_add(
                    qhat_sb[:, g * TOWN + cc * 512:g * TOWN + cc * 512 + 512],
                    rsc3[:, :], rsc4[:, :]),
                    reads=[r_rsc3, r_rsc4], writes=[r_qhat], tag=f"qh{var}_{g}_{cc}")

        def decode_head_gen(h, slot):
            kb_, qb_ = kbuf_sb[slot], qbuf_sb[slot]
            rkb, rqb = r_kbuf[slot], r_qbuf[slot]
            for nc8 in range(8):
                sl = nc8 % 2
                for k in range(2):
                    sch.step("pe", MM(dslot(sl)[0:64, :],
                                      wkd_sb[:, k * D + h * 64:k * D + h * 64 + 64],
                                      kv_sb[:, k * T + nc8 * 512:k * T + nc8 * 512 + 512],
                                      start=(k == 0), stop=(k == 1)),
                             reads=[r_kv, r_wkd],
                             writes=([r_psd[sl]] if k == 0 else []), tag=f"kd{var}_{h}_{nc8}")
                sch.group_end(r_psd[sl])
                sch.step("dve", lambda e, nc8=nc8, kb_=kb_, sl=sl: e.tensor_copy(
                    kb_[0:64, nc8 * 512:nc8 * 512 + 512], dslot(sl)[0:64, :]),
                    reads=[r_psd[sl]], writes=[rkb], tag=f"kde{var}_{h}_{nc8}")
                yield
            G, gsl = h // 4, h % 4
            if gsl == 2:
                sch.step("dve", lambda e, G=G: e.tensor_copy(
                    kb_[64:96, :], kr_sb[64:96, G * T:(G + 1) * T]),
                    reads=[r_kr], writes=[rkb], tag=f"kcp{var}_{h}")
                yield
            else:
                ci = {0: 0, 1: 1, 3: 2}[gsl]
                for nc8 in range(8):
                    sl = nc8 % 2
                    sch.step("pe", MM(dslot(sl)[64:96, :],
                                      psel_sb[:, ci * 32:(ci + 1) * 32],
                                      kr_sb[:, G * T + nc8 * 512:G * T + nc8 * 512 + 512],
                                      tp=(0, 64)),
                             reads=[r_kr, r_const], writes=[r_psd[sl]], tag=f"kpm{var}_{h}_{nc8}")
                    sch.step("dve", lambda e, nc8=nc8, kb_=kb_, sl=sl: e.tensor_copy(
                        kb_[64:96, nc8 * 512:nc8 * 512 + 512], dslot(sl)[64:96, :]),
                        reads=[r_psd[sl]], writes=[rkb], tag=f"kpe{var}_{h}_{nc8}")
                    yield
            for cc in range(4):
                sl = cc % 2
                for k in range(4):
                    sch.step("pe", MM(dslot(sl)[0:64, :],
                                      wqd_sb[:, k * D + h * 64:k * D + h * 64 + 64],
                                      qco_sb[:, k * TOWN + cc * 512:k * TOWN + cc * 512 + 512],
                                      start=(k == 0), stop=(k == 3)),
                             reads=[r_qco, r_wqd],
                             writes=([r_psd[sl]] if k == 0 else []), tag=f"qd{var}_{h}_{cc}")
                sch.group_end(r_psd[sl])
                sch.step("dve", lambda e, cc=cc, qb_=qb_, sl=sl: e.tensor_copy(
                    qb_[0:64, cc * 512:cc * 512 + 512], dslot(sl)[0:64, :]),
                    reads=[r_psd[sl]], writes=[rqb], tag=f"qde{var}_{h}_{cc}")
                yield
            if gsl == 2:
                sch.step("dve", lambda e, G=G, qb_=qb_: e.tensor_copy(
                    qb_[64:96, :], qhat_sb[64:96, G * TOWN:(G + 1) * TOWN]),
                    reads=[r_qhat], writes=[rqb], tag=f"qcp{var}_{h}")
                yield
            else:
                ci = {0: 0, 1: 1, 3: 2}[gsl]
                for cc in range(4):
                    sl = cc % 2
                    sch.step("pe", MM(dslot(sl)[64:96, :],
                                      psel_sb[:, ci * 32:(ci + 1) * 32],
                                      qhat_sb[:, G * TOWN + cc * 512:G * TOWN + cc * 512 + 512],
                                      tp=(0, 64)),
                             reads=[r_qhat, r_const], writes=[r_psd[sl]], tag=f"qpm{var}_{h}_{cc}")
                    sch.step("dve", lambda e, cc=cc, qb_=qb_, sl=sl: e.tensor_copy(
                        qb_[64:96, cc * 512:cc * 512 + 512], dslot(sl)[64:96, :]),
                        reads=[r_psd[sl]], writes=[rqb], tag=f"qpe{var}_{h}_{cc}")
                    yield

        def decode_v_gen(hg):
            for tt in range(32):
                sl = tt % 2
                for k in range(2):
                    sch.step("pe", MM(dslot(sl)[:, 0:128],
                                      kv_sb[:, k * T + tt * 128:k * T + tt * 128 + 128],
                                      wvd_sb[:, k * D + hg * 128:k * D + hg * 128 + 128],
                                      start=(k == 0), stop=(k == 1)),
                             reads=[r_kv, r_wvd],
                             writes=([r_psd[sl]] if k == 0 else []), tag=f"v{var}_{hg}_{tt}")
                sch.group_end(r_psd[sl])
                sch.step("dve", lambda e, tt=tt, sl=sl: e.tensor_copy(
                    bass.AP(v_sb, tt * 130, [[32 * 130, 128], [65, 2], [1, 64]]),
                    bass.AP(ps_decT, sl * 512, [[1024, 128], [64, 2], [1, 64]])),
                    reads=[r_psd[sl]], writes=[r_v], tag=f"ve{var}_{hg}_{tt}")
                if tt % 2 == 1:
                    yield
            sch.step("pl", lambda e: e.memset(
                bass.AP(v_sb, 64, [[32 * 130, 128], [130, 32], [65, 2], [1, 1]]), 1.0),
                writes=[r_v], tag=f"vones{var}_{hg}")

        def pump(g, k=1):
            if g is None:
                return None
            try:
                for _ in range(k):
                    next(g)
                return g
            except StopIteration:
                return None

        def drain(g):
            if g is not None:
                for _ in g:
                    pass

        for _ in decode_head_gen(0, 0):
            pass
        filler = None
        for h in range(NH):
            slot = h % 2
            kb_, qb_ = kbuf_sb[slot], qbuf_sb[slot]
            rkb, rqb = r_kbuf[slot], r_qbuf[slot]
            drain(filler)
            if h % 2 == 0:
                drain(decode_v_gen(h // 2))
            filler = decode_head_gen(h + 1, (h + 1) % 2) if h + 1 < NH else None
            hl = h % 2
            pending_norm = []
            for cc in range(4):
                while len(pending_norm) > 1:
                    pending_norm.pop(0)()
                q0c, nkt = chunks[cc]
                nb4 = (nkt + 1) // 2
                pends = []
                for bb in range(nb4):
                    if len(pends) >= 2:
                        emit_pv_q(pends.pop(0))
                    psl = bb % 2
                    psc, rsc_ = ps_sc[psl], r_pssc[psl]
                    pb, rpb = p_sb[psl], r_p[psl]
                    kts = [kt for kt in range(2 * bb, 2 * bb + 2) if kt < nkt]
                    for j, kt in enumerate(kts):
                        sch.step("pe", MM(psc[:, j * 512:j * 512 + 512],
                                          kb_[:, kt * 128:kt * 128 + 128],
                                          qb_[:, cc * 512:cc * 512 + 512]),
                                 reads=[rkb, rqb],
                                 writes=([rsc_] if j == 0 else []),
                                 tag=f"sc{var}_{h}_{cc}_{kt}")
                    sch.group_end(rsc_)
                    wid = 512 * len(kts)
                    sch.step("act", lambda e, psc=psc, pb=pb, wid=wid: e.activation(
                        pb[:, 0:wid], psc[:, 0:wid], AF.Exp, bias=EXP_BIAS),
                        reads=[rsc_], writes=[rpb], tag=f"ex{var}_{h}_{cc}_{bb}")
                    filler = pump(filler)
                    if bb == nb4 - 1 and pending_norm:
                        pending_norm.pop(0)()
                    for j, kt in enumerate(kts):
                        mj = kt - (nkt - 4)
                        if 0 <= mj < 4:
                            sch.step("pl", lambda e, pb=pb, j=j, mj=mj: e.tensor_mul(
                                pb[:, j * 512:j * 512 + 512], pb[:, j * 512:j * 512 + 512],
                                mask_sb[:, mj * 512:mj * 512 + 512]),
                                reads=[rpb, r_const], writes=[rpb],
                                tag=f"mk{var}_{h}_{cc}_{kt}")
                    pvsl = cc % 2

                    def emit_pv_q(pend, pvsl=pvsl, nkt=nkt):
                        ppsl, pkts = pend
                        for j, kt in enumerate(pkts):
                            sch.step("pe", MM(ps_pv[:, pvsl * 512:pvsl * 512 + 512],
                                              bass.AP(v_sb, kt * 130 + hl * 65,
                                                      [[32 * 130, 128], [1, 65]]),
                                              p_sb[ppsl][:, j * 512:j * 512 + 512],
                                              start=(kt == 0), stop=(kt == nkt - 1)),
                                     reads=[r_p[ppsl], r_v],
                                     writes=([r_pspv[pvsl]] if kt == 0 else []),
                                     tag=f"pv{var}_{h}_{cc}_{kt}")
                    pends.append((psl, kts))
                pvsl = cc % 2
                while pends:
                    emit_pv_q(pends.pop(0))
                sch.group_end(r_pspv[pvsl])
                def emit_norm(h=h, cc=cc, pvsl=pvsl):
                    sch.step("act", lambda e, pvsl=pvsl: e.activation(
                        recv_sb[0:1, :], ps_pv[64:65, pvsl * 512:pvsl * 512 + 512], AF.Ln),
                        reads=[r_pspv[pvsl]], writes=[r_recv], tag=f"rl{var}_{h}_{cc}")
                    sch.step("act", lambda e: e.activation(
                        recv_sb[0:1, :], recv_sb[0:1, :], AF.Exp, scale=-1.0),
                        reads=[r_recv], writes=[r_recv], tag=f"rc{var}_{h}_{cc}")
                    sch.step("pe", MM(ps_decT[0:64, 512:1024], ones1[0:1, 0:64],
                                      recv_sb[0:1, :]),
                             reads=[r_recv, r_const], writes=[r_psd[1]],
                             tag=f"nb{var}_{h}_{cc}")
                    sch.step("dve", lambda e: e.tensor_copy(nrm_sb[:, :],
                                                            ps_decT[0:64, 512:1024]),
                        reads=[r_psd[1]], writes=[r_nrm], tag=f"nbe{var}_{h}_{cc}")
                    if h % 2 == 0:
                        sch.step("dve", lambda e, h=h, cc=cc, pvsl=pvsl: e.tensor_mul(
                            attn_all[0:64,
                                     (h // 2) * TOWN + cc * 512:(h // 2) * TOWN + cc * 512 + 512],
                            ps_pv[0:64, pvsl * 512:pvsl * 512 + 512], nrm_sb[:, :]),
                            reads=[r_pspv[pvsl], r_nrm], writes=[r_attnall[h]],
                            tag=f"an{var}_{h}_{cc}")
                    else:
                        sch.step("dve", lambda e, pvsl=pvsl: e.tensor_mul(
                            aschat[:, :], ps_pv[0:64, pvsl * 512:pvsl * 512 + 512],
                            nrm_sb[:, :]),
                            reads=[r_pspv[pvsl], r_nrm], writes=[r_aschat],
                            tag=f"an{var}_{h}_{cc}")
                        sch.step("sp", lambda e, h=h, cc=cc: e.dma_start(
                            out=attn_all[64:128,
                                         (h // 2) * TOWN + cc * 512:(h // 2) * TOWN + cc * 512 + 512],
                            in_=aschat[:, :]),
                            reads=[r_aschat], writes=[r_attnall[h]], dma=True,
                            tag=f"as{var}_{h}_{cc}")
                pending_norm.append(emit_norm)
            while pending_norm:
                pending_norm.pop(0)()
        sch.flush()
        end_counts.append(sch.snapshot())
    assert end_counts[0] == end_counts[1], (
        "switch bodies must produce identical semaphore counts",
        {k: (end_counts[0][k], end_counts[1][k]) for k in end_counts[0]
         if end_counts[0][k] != end_counts[1][k]})
    sch.set_counts(end_counts[0])

    t_attn = tap("attn_pre", [128, 8 * TOWN], BF16)
    if t_attn is not None:
        sch.step("sp", lambda e: e.dma_start(out=t_attn[:, :], in_=attn_all[:, :]),
                 deps=[r for h in range(NH) for r in ([r_attnall[h].w] if r_attnall[h].w else [])],
                 dma=True, tag="tap_attn")
    sch.flush()
    es3.close()
    if stop_after < 4:
        es_mid.close()
        esL.close()
        es.close()
        return nc, tap_t

    es4 = ExitStack()
    S4 = es4.enter_context
    xo_sb = [S4(nc.sbuf_tensor(f"xo{i}", [128, 8 * 512], F32)) for i in range(2)]
    po_sb = S4(nc.sbuf_tensor("po_sb", [128, 512], F32))
    ps_wo = S4(nc.psum_tensor("ps_wo", [128, 1024], F32))
    ps_wo2 = S4(nc.psum_tensor("ps_wo2", [128, 1024], F32))
    r_xo = [Reg("xo0"), Reg("xo1")]
    r_po = Reg("po")
    r_pswo = [Reg("pswo0"), Reg("pswo1")]
    r_pswo2 = [Reg("pswo20"), Reg("pswo21")]

    for cc in range(4):
        b = cc % 2
        sch.step("sp", lambda e, cc=cc, b=b: e.dma_start(
            out=bass.AP(xo_sb[b], 0, [[8 * 512, 128], [512, 8], [1, 512]]),
            in_=bass.AP(xTo.tensor, cc * 512, [[TOWN, 128], [128 * TOWN, 8], [1, 512]])),
            writes=[r_xo[b]], dma=True, tag=f"ld_xo{cc}")
        for dchunk in range(8):
            sl = dchunk % 2
            rp, rp2 = r_pswo[sl], r_pswo2[sl]
            for j, h in enumerate(range(0, NH, 2)):
                lhs = wo_sb[0:64,
                            (h // 2) * D + dchunk * 128:(h // 2) * D + dchunk * 128 + 128]
                rhs = attn_all[0:64,
                               (h // 2) * TOWN + cc * 512:(h // 2) * TOWN + cc * 512 + 512]
                sch.step("pe", MM(ps_wo[:, sl * 512:sl * 512 + 512], lhs, rhs,
                                  start=(j == 0), stop=(j == 7)),
                         reads=[r_wo, r_attnall[h]],
                         writes=([rp] if j == 0 else []), tag=f"wo{cc}_{dchunk}_{h}")
            sch.group_end(rp)
            for j, h in enumerate(range(1, NH, 2)):
                lhs = wo_sb[64:128,
                            (h // 2) * D + dchunk * 128:(h // 2) * D + dchunk * 128 + 128]
                rhs = attn_all[64:128,
                               (h // 2) * TOWN + cc * 512:(h // 2) * TOWN + cc * 512 + 512]
                sch.step("pe", MM(ps_wo2[:, sl * 512:sl * 512 + 512], lhs, rhs,
                                  start=(j == 0), stop=(j == 7)),
                         reads=[r_wo, r_attnall[h]],
                         writes=([rp2] if j == 0 else []), tag=f"wo2{cc}_{dchunk}_{h}")
            sch.group_end(rp2)
            sch.step("dve", lambda e, dchunk=dchunk, sl=sl, b=b: e.tensor_add(
                po_sb[:, :], ps_wo[:, sl * 512:sl * 512 + 512],
                xo_sb[b][:, dchunk * 512:dchunk * 512 + 512]),
                reads=[rp, r_xo[b]], writes=[r_po], tag=f"poa{cc}_{dchunk}")
            sch.step("dve", lambda e, sl=sl: e.scalar_tensor_tensor(
                po_sb[:, :], ps_wo2[:, sl * 512:sl * 512 + 512], 1.0, po_sb[:, :],
                ALU.mult, ALU.add),
                reads=[rp2, r_po], writes=[r_po], tag=f"pob{cc}_{dchunk}")
            sch.step("sp", lambda e, dchunk=dchunk, cc=cc: e.dma_start(
                out=outT[dchunk * 128:(dchunk + 1) * 128, cc * 512:cc * 512 + 512],
                in_=po_sb[:, :]),
                reads=[r_po], dma=True, tag=f"st_p{cc}_{dchunk}")
    sch.flush()
    es4.close()
    es_mid.close()
    esL.close()
    if stop_after < 5:
        es.close()
        return nc, tap_t


    es5 = ExitStack()
    S5 = es5.enter_context
    xn_sb = S5(nc.sbuf_tensor("xn_sb", [128, 8 * TOWN], BF16))
    h_sb = S5(nc.sbuf_tensor("h_sb", [128, 24 * 1024], BF16))
    silu_sb = S5(nc.sbuf_tensor("silu_sb", [128, 1024], BF16))
    wg_sb = [S5(nc.sbuf_tensor(f"wg{i}", [128, 8 * 384], BF16)) for i in range(3)]
    wu_sb = [S5(nc.sbuf_tensor(f"wu{i}", [128, 8 * 384], BF16)) for i in range(3)]
    wd_sb = [S5(nc.sbuf_tensor(f"wd{i}", [128, 24 * 128], BF16)) for i in range(2)]
    xo5_sb = [S5(nc.sbuf_tensor(f"xo5{i}", [128, 8 * 512], BF16)) for i in range(2)]
    xf5_sb = [S5(nc.sbuf_tensor(f"xf5{i}", [128, 4 * 512], F32)) for i in range(2)]
    xsq5 = S5(nc.sbuf_tensor("xsq5", [128, 2 * 512], BF16))
    pr_sb = [S5(nc.sbuf_tensor(f"pr{i}", [128, 512], F32)) for i in range(2)]
    ob_sb = S5(nc.sbuf_tensor("ob_sb", [128, 512], F32))
    vec5 = S5(nc.sbuf_tensor("vec5", [1, 3 * 512], F32))
    ps5_bc = S5(nc.psum_tensor("ps5_bc", [128, 1024], F32))
    es5a = ExitStack()
    ps5_ln = es5a.enter_context(nc.psum_tensor("ps5_ln", [1, 1024], F32))
    r_xn, r_h, r_silu = Reg("xn"), Reg("h"), Reg("silu")
    r_wg = [Reg("wg0"), Reg("wg1"), Reg("wg2")]
    r_wu = [Reg("wu0"), Reg("wu1"), Reg("wu2")]
    r_wd = [Reg("wd0"), Reg("wd1")]
    r_xo5 = [Reg("xo50"), Reg("xo51")]
    r_xf5 = [Reg("xf50"), Reg("xf51")]
    r_xsq5 = [Reg("xsq50"), Reg("xsq51")]
    r_pr = [Reg("pr0"), Reg("pr1")]
    r_ob, r_vec5 = Reg("ob"), Reg("vec5")
    r_ps5ln, r_ps5bc0, r_ps5bc1 = Reg("ps5ln"), Reg("ps5bc0"), Reg("ps5bc1")
    r_ps5g, r_ps5u = Reg("ps5g"), Reg("ps5u")
    r_ps5d = [Reg("ps5d0"), Reg("ps5d1")]

    def V5(i):
        return vec5[0:1, i * 512:(i + 1) * 512]

    for cc in range(4):
        b = cc % 2
        for hh2 in range(2):
            xfb = (2 * cc + hh2) % 2
            sch.step("sp", lambda e, cc=cc, hh2=hh2, xfb=xfb: e.dma_start(
                out=bass.AP(xf5_sb[xfb], 0, [[4 * 512, 128], [512, 4], [1, 512]]),
                in_=bass.AP(xTo.tensor, cc * 512 + hh2 * 4 * 128 * TOWN,
                            [[TOWN, 128], [128 * TOWN, 4], [1, 512]])),
                writes=[r_xf5[xfb]], dma=True, tag=f"l5x{cc}_{hh2}")
            sch.step("act", lambda e, b=b, hh2=hh2, xfb=xfb: e.activation(
                xo5_sb[b][:, hh2 * 2048:(hh2 + 1) * 2048], xf5_sb[xfb][:, :], AF.Copy),
                reads=[r_xf5[xfb]], writes=[r_xo5[b]], tag=f"cx5{cc}_{hh2}")
        for c in range(8):
            sch.step("pe", MM(ps5_ln[0:1, 0:512], onesv[:, 0:1],
                              xo5_sb[b][:, c * 512:(c + 1) * 512],
                              start=(c == 0), stop=(c == 7)),
                     reads=[r_xo5[b], r_const], writes=([r_ps5ln] if c == 0 else []),
                     tag=f"l5m{cc}_{c}")
        sch.group_end(r_ps5ln)
        sch.step("dve", lambda e: e.tensor_scalar_mul(V5(0), ps5_ln[0:1, 0:512], 1.0 / D),
                 reads=[r_ps5ln], writes=[r_vec5], tag=f"l5mm{cc}")
        sch.step("pe", MM(ps5_bc[:, 0:512], ones1[0:1, :], V5(0)),
                 reads=[r_vec5, r_const], writes=[r_ps5bc0], tag=f"l5bc{cc}")
        for c in range(8):
            sq = c % 2
            sch.step("act", lambda e, b=b, c=c, sq=sq: e.activation(
                xsq5[:, sq * 512:sq * 512 + 512], xo5_sb[b][:, c * 512:(c + 1) * 512],
                AF.Square),
                reads=[r_xo5[b]], writes=[r_xsq5[sq]], tag=f"l5s{cc}_{c}")
            sch.step("pe", MM(ps5_ln[0:1, 512:1024], onesv[:, 0:1],
                              xsq5[:, sq * 512:sq * 512 + 512],
                              start=(c == 0), stop=(c == 7)),
                     reads=[r_xsq5[sq], r_const], writes=([r_ps5ln] if c == 0 else []),
                     tag=f"l5q{cc}_{c}")
        sch.group_end(r_ps5ln)
        sch.step("dve", lambda e: e.tensor_scalar_mul(V5(1), ps5_ln[0:1, 512:1024], 1.0 / D),
                 reads=[r_ps5ln], writes=[r_vec5], tag=f"l5v{cc}")
        sch.step("dve", lambda e: e.tensor_mul(V5(2), V5(0), V5(0)),
                 reads=[r_vec5], writes=[r_vec5], tag=f"l5m2{cc}")
        sch.step("dve", lambda e: e.tensor_sub(V5(1), V5(1), V5(2)),
                 reads=[r_vec5], writes=[r_vec5], tag=f"l5var{cc}")
        sch.step("act", lambda e: e.activation(V5(1), V5(1), AF.Ln, bias=1e-5),
                 reads=[r_vec5], writes=[r_vec5], tag=f"l5sd{cc}")
        sch.step("act", lambda e: e.activation(V5(2), V5(1), AF.Exp, scale=-0.5),
                 reads=[r_vec5], writes=[r_vec5], tag=f"l5r{cc}")
        sch.step("pe", MM(ps5_bc[:, 512:1024], ones1[0:1, :], V5(2)),
                 reads=[r_vec5, r_const], writes=[r_ps5bc1], tag=f"l5rb{cc}")
        sch.step("dve", lambda e, b=b, cc=cc: e.tensor_sub(
            bass.AP(xn_sb, cc * 512, [[8 * TOWN, 128], [TOWN, 8], [1, 512]]),
            bass.AP(xo5_sb[b], 0, [[8 * 512, 128], [512, 8], [1, 512]]),
            bass.AP(ps5_bc, 0, [[1024, 128], [0, 8], [1, 512]])),
            reads=[r_xo5[b], r_ps5bc0], writes=[r_xn], tag=f"l5c{cc}")
        sch.step("dve", lambda e, cc=cc: e.tensor_mul(
            bass.AP(xn_sb, cc * 512, [[8 * TOWN, 128], [TOWN, 8], [1, 512]]),
            bass.AP(xn_sb, cc * 512, [[8 * TOWN, 128], [TOWN, 8], [1, 512]]),
            bass.AP(ps5_bc, 512, [[1024, 128], [0, 8], [1, 512]])),
            reads=[r_xn, r_ps5bc1], writes=[r_xn], tag=f"l5n{cc}")
    sch.flush()

    es5a.close()
    ps5_g = S5(nc.psum_tensor("ps5_g", [128, 1024], F32))
    ps5_u = S5(nc.psum_tensor("ps5_u", [128, 1024], F32))
    ps5_d = S5(nc.psum_tensor("ps5_d", [128, 1024], F32))
    r_outT = Reg("outT")
    last_store = None
    for tw in range(2):
        for hb in (range(8) if tw == 0 else range(7, -1, -1)):
            wbuf = hb % 3
            if not (tw == 1 and hb >= 5):
                sch.step("pl", lambda e, hb=hb, wbuf=wbuf: e.dma_start(
                    out=bass.AP(wg_sb[wbuf], 0, [[8 * 384, 128], [384, 8], [1, 384]]),
                    in_=bass.AP(wgT.tensor, hb * 384, [[HID, 128], [128 * HID, 8], [1, 384]])),
                    writes=[r_wg[wbuf]], dma=True, tag=f"f_wg{tw}_{hb}")
                sch.step("pl", lambda e, hb=hb, wbuf=wbuf: e.dma_start(
                    out=bass.AP(wu_sb[wbuf], 0, [[8 * 384, 128], [384, 8], [1, 384]]),
                    in_=bass.AP(wuT.tensor, hb * 384, [[HID, 128], [128 * HID, 8], [1, 384]])),
                    writes=[r_wu[wbuf]], dma=True, tag=f"f_wu{tw}_{hb}")
            for j in range(3):
                hh = 3 * hb + j
                for nn in range(2):
                    for k in range(8):
                        sch.step("pe", MM(ps5_g[:, nn * 512:nn * 512 + 512],
                                          wg_sb[wbuf][:, k * 384 + j * 128:k * 384 + j * 128 + 128],
                                          xn_sb[:, k * TOWN + tw * 1024 + nn * 512:
                                                k * TOWN + tw * 1024 + nn * 512 + 512],
                                          start=(k == 0), stop=(k == 7)),
                                 reads=[r_wg[wbuf], r_xn],
                                 writes=([r_ps5g] if (nn == 0 and k == 0) else []),
                                 tag=f"fg{tw}_{hh}_{nn}_{k}")
                sch.group_end(r_ps5g)
                sch.step("act", lambda e: e.activation(
                    silu_sb[:, :], ps5_g[:, :], AF.Silu, scale=FFS),
                    reads=[r_ps5g], writes=[r_silu], tag=f"fs{tw}_{hh}")
                for nn in range(2):
                    for k in range(8):
                        sch.step("pe", MM(ps5_u[:, nn * 512:nn * 512 + 512],
                                          wu_sb[wbuf][:, k * 384 + j * 128:k * 384 + j * 128 + 128],
                                          xn_sb[:, k * TOWN + tw * 1024 + nn * 512:
                                                k * TOWN + tw * 1024 + nn * 512 + 512],
                                          start=(k == 0), stop=(k == 7)),
                                 reads=[r_wu[wbuf], r_xn],
                                 writes=([r_ps5u] if (nn == 0 and k == 0) else []),
                                 tag=f"fu{tw}_{hh}_{nn}_{k}")
                sch.group_end(r_ps5u)
                sch.step("dve", lambda e, hh=hh: e.scalar_tensor_tensor(
                    h_sb[:, hh * 1024:(hh + 1) * 1024], ps5_u[:, :], FFS, silu_sb[:, :],
                    ALU.mult, ALU.mult),
                    reads=[r_ps5u, r_silu], writes=[r_h], tag=f"fh{tw}_{hh}")
        def wd_load(dchunk):
            dbuf = dchunk % 2
            sch.step("pl", lambda e, dchunk=dchunk, dbuf=dbuf: e.dma_start(
                out=bass.AP(wd_sb[dbuf], 0, [[24 * 128, 128], [128, 24], [1, 128]]),
                in_=bass.AP(wdT.tensor, dchunk * 128, [[D, 128], [128 * D, 24], [1, 128]])),
                writes=[r_wd[dbuf]], dma=True, tag=f"f_wd{tw}_{dchunk}")
        dcs = list(range(8)) if tw == 0 else list(range(7, -1, -1))
        if tw == 0:
            wd_load(dcs[0])
        for di, dchunk in enumerate(dcs):
            dbuf = dchunk % 2
            if di + 1 < 8 and not (tw == 1 and dcs[di + 1] >= 6):
                wd_load(dcs[di + 1])
            for nn in range(2):
                sl = nn
                for k in range(24):
                    sch.step("pe", MM(ps5_d[:, sl * 512:sl * 512 + 512],
                                      wd_sb[dbuf][:, k * 128:(k + 1) * 128],
                                      h_sb[:, k * 1024 + nn * 512:k * 1024 + nn * 512 + 512],
                                      start=(k == 0), stop=(k == 23)),
                             reads=[r_wd[dbuf], r_h],
                             writes=([r_ps5d[sl]] if k == 0 else []),
                             tag=f"fd{tw}_{dchunk}_{nn}_{k}")
                sch.group_end(r_ps5d[sl])
                pbuf = nn
                sch.step("sp", lambda e, dchunk=dchunk, tw=tw, nn=nn, pbuf=pbuf: e.dma_start(
                    out=pr_sb[pbuf][:, :],
                    in_=outT[dchunk * 128:(dchunk + 1) * 128,
                             tw * 1024 + nn * 512:tw * 1024 + nn * 512 + 512]),
                    reads=[r_outT], writes=[r_pr[pbuf]], dma=True, tag=f"fp{tw}_{dchunk}_{nn}")
                sch.step("dve", lambda e, sl=sl, pbuf=pbuf: e.scalar_tensor_tensor(
                    ob_sb[:, :], ps5_d[:, sl * 512:sl * 512 + 512], FFS, pr_sb[pbuf][:, :],
                    ALU.mult, ALU.add),
                    reads=[r_ps5d[sl], r_pr[pbuf]], writes=[r_ob], tag=f"fo{tw}_{dchunk}_{nn}")
                last_store = sch.step("sp", lambda e, dchunk=dchunk, tw=tw, nn=nn: e.dma_start(
                    out=outT[dchunk * 128:(dchunk + 1) * 128,
                             tw * 1024 + nn * 512:tw * 1024 + nn * 512 + 512],
                    in_=ob_sb[:, :]),
                    reads=[r_ob], writes=[r_outT], dma=True, tag=f"fst{tw}_{dchunk}_{nn}")
    sch.step("sp", lambda e: e.nop(), deps=[last_store], tag="final_wait")
    sch.flush()
    es5.close()
    es.close()
    return nc, tap_t


def make_core_inputs(ins, const, b, core_type):
    x = ins["L0_attn_x"][b]
    own = own_positions(core_type)
    cosq = np.ascontiguousarray(
        (np.tile(const["cos_t"], (4, 1))[:, own] * np.float32(SCALE)).astype(np.float32))
    sinq = np.ascontiguousarray(
        (np.tile(const["sin_t"], (4, 1))[:, own] * np.float32(SCALE)).astype(np.float32))
    return {
        "xT": np.ascontiguousarray(x.T),
        "xTo": np.ascontiguousarray(x.T[:, own]),
        "w_kv": ins["L0_attn_W_kv"], "w_q": ins["L0_attn_W_q"],
        "w_kdec": ins["L0_attn_W_kdec"], "w_vdec": ins["L0_attn_W_vdec"],
        "w_qdec": ins["L0_attn_W_qdec"], "w_krope": ins["L0_attn_W_krope"],
        "w_qrope": ins["L0_attn_W_qrope"], "w_o": ins["L0_attn_W_o"],
        "g_kv": ins["L0_attn_g_kv"],
        "wgT": np.ascontiguousarray(ins["L0_ffn_w_gate"].T),
        "wuT": np.ascontiguousarray(ins["L0_ffn_w_up"].T),
        "wdT": np.ascontiguousarray(ins["L0_ffn_w_down"].T),
        "cosk": const["cosk"], "sink": const["sink"],
        "cosqo": cosq, "sinqo": sinq,
        "maskc": const["maskc"], "r4t": const["r4t"], "psel": const["psel"],
    }


_NC_CACHE = {}


def kernel(**inputs):
    from concourse.bass_utils import run_bass_kernel_spmd
    if "nc" not in _NC_CACHE:
        _NC_CACHE["nc"] = build_nc()[0]
    nc = _NC_CACHE["nc"]
    const = host_constants()
    in_maps = [make_core_inputs(inputs, const, c // 2, c % 2) for c in range(8)]
    res = run_bass_kernel_spmd(nc, in_maps, core_ids=list(range(8)))
    out = np.empty((4, T, D), np.float32)
    for c in range(8):
        own = own_positions(c % 2)
        out[c // 2, own, :] = np.asarray(res.results[c]["outT"]).T
    return out


def host_constants():
    inv_freq = 1.0 / 10000.0 ** (np.arange(0, RD, 2, dtype=np.float32) / RD)
    t = np.arange(T, dtype=np.float32)
    freqs = np.outer(t, inv_freq)
    emb = np.concatenate([freqs, freqs], -1)
    cos_t = np.cos(emb).astype(np.float32).T
    sin_t = np.sin(emb).astype(np.float32).T
    cosk = np.ascontiguousarray(np.tile(cos_t, (4, 1)).astype(np.float32))
    sink = np.ascontiguousarray(np.tile(sin_t, (4, 1)).astype(np.float32))
    kk = np.arange(128)[:, None]
    qq = np.arange(512)[None, :]
    masks = np.ascontiguousarray(np.concatenate(
        [(qq - kk >= 128 * j).astype(np.float32) for j in range(4)],
        axis=1).astype(bfdt))
    R = np.zeros((32, 32), np.float32)
    for i in range(16):
        R[i, i + 16] = -1.0
        R[i + 16, i] = 1.0
    R4 = np.zeros((128, 128), np.float32)
    for g in range(4):
        R4[g * 32:(g + 1) * 32, g * 32:(g + 1) * 32] = R
    r4t = np.ascontiguousarray(R4.T.astype(bfdt))
    ps = np.zeros((128, 96), np.float32)
    for ci, g in enumerate((0, 1, 3)):
        for i in range(32):
            ps[g * 32 + i, ci * 32 + i] = 1.0
    psel = np.ascontiguousarray(ps.astype(bfdt))
    return dict(cosk=cosk, sink=sink, maskc=masks, r4t=r4t, psel=psel,
                cos_t=cos_t, sin_t=sin_t)


def own_positions(core_type):
    if core_type == 0:
        return np.concatenate([np.arange(0, 1024), np.arange(3072, 4096)])
    return np.arange(1024, 3072)
